# version 03
# baseline (speedup 1.0000x reference)
import contextlib
import numpy as np
import ml_dtypes
import concourse.bass as bass
import concourse.mybir as mybir
from concourse.bass_utils import run_bass_kernel_spmd

F32 = mybir.dt.float32
BF16 = mybir.dt.bfloat16
AF = mybir.ActivationFunctionType
ALU = mybir.AluOpType
AX = mybir.AxisListType

D = 1024
DC = D // 128
NH = 16
NKV = 8
HD = 64
M = 3584
MC = M // 128
NE = 8
EPS_LN = 1e-5
EPS_RMS = 1e-6

PERM_Q = [0, 2, 1, 3, 4, 6, 5, 7, 8, 10, 9, 11, 12, 14, 13, 15]
POS_OF = {h: p for p, h in enumerate(PERM_Q)}

ENGINES = ("sync", "gpsimd", "tensor", "scalar", "vector")


class Phase:
    def __init__(self, name):
        self.name = name
        self.ops = {k: [] for k in ENGINES}

    def em(self, eng, fn):
        self.ops[eng].append(fn)

    def play(self, nc):
        with nc.Block() as b:
            for eng in ENGINES:
                fns = self.ops[eng]
                if fns:
                    getattr(b, eng)(lambda e, fns=fns: [f(e) for f in fns])


class Ctr:
    def __init__(self):
        self.v = {}

    def add(self, k, n=1):
        self.v[k] = self.v.get(k, 0) + n
        return self.v[k]

    def val(self, k):
        return self.v.get(k, 0)


def build(S=4096, debug=False):
    NCH = S // 128
    NSUP = S // 512
    NTQ = max(1, S // 1024)
    TQT = S // NTQ
    TQC = TQT // 128

    nc = bass.Bass("TRN2", target_bir_lowering=False)

    x_d = nc.dram_tensor("x", [S, D], F32, kind="ExternalInput").ap()
    wq_d = nc.dram_tensor("wq", [D, D], F32, kind="ExternalInput").ap()
    wk_d = nc.dram_tensor("wk", [D, D], F32, kind="ExternalInput").ap()
    wv_d = nc.dram_tensor("wv", [D, D], F32, kind="ExternalInput").ap()
    wo_d = nc.dram_tensor("wo", [D, D], F32, kind="ExternalInput").ap()
    ln1_d = nc.dram_tensor("ln1", [128, D], F32, kind="ExternalInput").ap()
    ln2_d = nc.dram_tensor("ln2", [128, D], F32, kind="ExternalInput").ap()
    fwq_d = nc.dram_tensor("fwq", [D, D], F32, kind="ExternalInput").ap()
    fwk_d = nc.dram_tensor("fwk", [D, 512], F32, kind="ExternalInput").ap()
    fwv_d = nc.dram_tensor("fwv", [D, 512], F32, kind="ExternalInput").ap()
    fwo_d = nc.dram_tensor("fwo", [D, D], F32, kind="ExternalInput").ap()
    rw_d = nc.dram_tensor("rw", [D, NE], F32, kind="ExternalInput").ap()
    wg_d = nc.dram_tensor("wgate", [NE, D, M], F32, kind="ExternalInput").ap()
    wu_d = nc.dram_tensor("wup", [NE, D, M], F32, kind="ExternalInput").ap()
    wd_d = nc.dram_tensor("wdown", [NE, M, D], F32, kind="ExternalInput").ap()
    identb_d = nc.dram_tensor("identb", [128, 128], BF16, kind="ExternalInput").ap()
    identf_d = nc.dram_tensor("identf", [128, 128], F32, kind="ExternalInput").ap()
    onesb_d = nc.dram_tensor("onesb", [128, 128], BF16, kind="ExternalInput").ap()
    cs_d = nc.dram_tensor("cossin", [2, S, NH, 32], F32, kind="ExternalInput").ap()
    mask_d = nc.dram_tensor("mask01", [4, 128, 512], BF16, kind="ExternalInput").ap()
    out_d = nc.dram_tensor("out", [S, D], F32, kind="ExternalOutput").ap()

    skind = "ExternalOutput" if debug else "Internal"
    x1_d = nc.dram_tensor("x1s", [S, D], F32, kind=skind).ap()
    y1_d = nc.dram_tensor("y1s", [S, D], F32, kind=skind).ap()
    aout_d = nc.dram_tensor("aouts", [NSUP, 128, NH // 2, 512], BF16,
                            kind="Internal").ap()
    h2t_d = nc.dram_tensor("h2ts", [DC, 128, S], BF16, kind=skind).ap()
    ncb_d = nc.dram_tensor("ncbs", [2, 16], F32, kind="Internal").ap()

    C = Ctr()
    es = contextlib.ExitStack()
    with es:
        sem = {}
        for name in ("sx0", "sx1", "sa0", "sa1", "shq", "sgu0", "sgu1",
                     "swd0", "swd1", "so0", "so1", "sout", "swg", "saux", "snc",
                     "tmm", "tmv", "tslc", "vev", "vmk", "ssc",
                     "tmoe", "tdown", "ssil", "vmoe", "vdev", "vz"):
            sem[name] = es.enter_context(nc.semaphore(name))

        identb = es.enter_context(nc.sbuf_tensor("identb_s", [128, 128], BF16))
        identf = es.enter_context(nc.sbuf_tensor("identf_s", [128, 128], F32))
        onesb = es.enter_context(nc.sbuf_tensor("onesb_s", [128, 128], BF16))
        maskb = es.enter_context(nc.sbuf_tensor("maskb", [128, 4, 512], BF16))
        wfull = es.enter_context(nc.sbuf_tensor("wfull", [128, NTQ * NE * TQC], F32))

        ph = Phase("P0")

        def _p0(e):
            e.dma_start(identb[:], identb_d).then_inc(sem["saux"], 16)
            e.dma_start(identf[:], identf_d).then_inc(sem["saux"], 16)
            e.dma_start(onesb[:], onesb_d).then_inc(sem["saux"], 16)
            e.dma_start(maskb[:], mask_d.rearrange("a p q -> p a q")
                        ).then_inc(sem["saux"], 16)
        ph.em("sync", _p0)
        C.add("saux", 64)

        def _p0w(e, w=C.val("saux")):
            e.wait_ge(sem["saux"], w)
        ph.em("vector", _p0w)
        ph.play(nc)

        def transposes(ph, pT, src_ap_of, dst_ap_of, n,
                       extra_tensor_wait=None, extra_vector_wait=None,
                       data_wait=None):
            if data_wait is None:
                data_wait = ("vev", C.val("vev"))
            for j in range(n):
                pt = pT[j % 2]

                def _tr(e, pt=pt, j=j, dw=data_wait,
                        wwar=(C.val("vev") - 1 if j >= 2 else 0),
                        src=src_ap_of(j),
                        xw=extra_tensor_wait if j == 0 else None):
                    if xw is not None:
                        xw(e)
                    if j == 0 and dw[1] > 0:
                        e.wait_ge(sem[dw[0]], dw[1])
                    if wwar > 0:
                        e.wait_ge(sem["vev"], wwar)
                    e.transpose(pt[:], src, identb[:]).then_inc(sem["tmm"], 1)
                ph.em("tensor", _tr)
                C.add("tmm")

                def _ev(e, pt=pt, wt=C.val("tmm"), dst=dst_ap_of(j),
                        xw=extra_vector_wait if j == 0 else None):
                    if xw is not None:
                        xw(e)
                    e.wait_ge(sem["tmm"], wt)
                    e.tensor_copy(dst, pt[:]).then_inc(sem["vev"], 1)
                ph.em("vector", _ev)
                C.add("vev")

        def emit_attention(name, src_x_d, nqh, kv_of, head0, weights_mode,
                           ln_rms=False, ln1r=None):
            ph = Phase(name)
            st = contextlib.ExitStack()
            QCOL = nqh * HD
            QHG = nqh // 2
            QPAD = (nqh != NKV)
            NQS = 16 if QPAD else QHG
            PH_SOUT0 = (C.val("so0"), C.val("so1"), C.val("sout"))

            xsb = st.enter_context(nc.sbuf_tensor(name + "_xsb", [128, 2, D], F32))
            csb = st.enter_context(nc.sbuf_tensor(name + "_csb", [128, 2, 2, 16, 32], F32))
            xcb = st.enter_context(nc.sbuf_tensor(name + "_xcb", [128, 2, D], F32))
            sqd = st.enter_context(nc.sbuf_tensor(name + "_sqd", [128, D], F32))
            hsc = st.enter_context(nc.sbuf_tensor(name + "_hsc", [128, D], F32))
            nsq = st.enter_context(nc.sbuf_tensor(name + "_nsq", [128, 1536], F32))
            nst = st.enter_context(nc.sbuf_tensor(name + "_nst", [128, 32], F32))
            stt = st.enter_context(nc.sbuf_tensor(name + "_stt", [128, 16], F32))
            hb = st.enter_context(nc.sbuf_tensor(name + "_hb", [128, D], BF16))
            hTb = st.enter_context(nc.sbuf_tensor(name + "_hTb", [128, DC, 128], BF16))
            qrb = st.enter_context(nc.sbuf_tensor(name + "_qrb", [128, QCOL], BF16))
            krb = st.enter_context(nc.sbuf_tensor(name + "_krb", [128, 512], BF16))
            rtmp = st.enter_context(nc.sbuf_tensor(name + "_rtmp", [128, 4, 16, 32], F32))
            qTb = st.enter_context(nc.sbuf_tensor(name + "_qTb", [128, NQS, 512], BF16))
            KTb = st.enter_context(nc.sbuf_tensor(name + "_KTb", [128, 4, S], BF16))
            Vb = st.enter_context(nc.sbuf_tensor(name + "_Vb", [128, NCH, NKV, 65], BF16))
            expb = st.enter_context(nc.sbuf_tensor(name + "_expb", [128, 4, 512], BF16))
            aoutb = st.enter_context(nc.sbuf_tensor(name + "_aoutb", [128, QHG, 512], BF16))
            rbf = st.enter_context(nc.sbuf_tensor(name + "_rbf", [128, 512], F32))
            rib = st.enter_context(nc.sbuf_tensor(name + "_rib", [128, 512], BF16))
            qsqm = st.enter_context(nc.sbuf_tensor(name + "_qsqm", [128, nqh], F32))
            ksqm = st.enter_context(nc.sbuf_tensor(name + "_ksqm", [128, NKV], F32))
            ksqe = st.enter_context(nc.sbuf_tensor(name + "_ksqe", [128, 16], F32))
            qmx = st.enter_context(nc.sbuf_tensor(name + "_qmx", [16, 1], F32))
            kmx = st.enter_context(nc.sbuf_tensor(name + "_kmx", [16, 1], F32))
            qrow = st.enter_context(nc.sbuf_tensor(name + "_qrow", [1, 16], F32))
            krow = st.enter_context(nc.sbuf_tensor(name + "_krow", [1, 16], F32))
            crow = st.enter_context(nc.sbuf_tensor(name + "_crow", [1, 16], BF16))
            negCb = st.enter_context(nc.sbuf_tensor(name + "_negC", [128, 16], F32))
            if weights_mode == "transposed":
                wqT = st.enter_context(nc.sbuf_tensor(name + "_wqT", [128, DC, 512], BF16))
                wkT = st.enter_context(nc.sbuf_tensor(name + "_wkT", [128, DC, 512], BF16))
                wvT = st.enter_context(nc.sbuf_tensor(name + "_wvT", [128, DC, 512], BF16))
                wtmp = st.enter_context(nc.sbuf_tensor(name + "_wtmp", [128, 4, D], BF16))
            else:
                fwqb = st.enter_context(nc.sbuf_tensor(name + "_fwq", [128, DC, D], BF16))
                fwkb = st.enter_context(nc.sbuf_tensor(name + "_fwk", [128, DC, 512], BF16))
                fwvb = st.enter_context(nc.sbuf_tensor(name + "_fwv", [128, DC, 512], BF16))
            pQV = st.enter_context(nc.psum_tensor(name + "_pQV", [128, 512], F32))
            pK = st.enter_context(nc.psum_tensor(name + "_pK", [128, 512], F32))
            pT0 = st.enter_context(nc.psum_tensor(name + "_pT0", [128, 128], BF16))
            pT1 = st.enter_context(nc.psum_tensor(name + "_pT1", [128, 128], BF16))
            psT = st.enter_context(nc.psum_tensor(name + "_psT", [128, 2, 512], F32))
            poT = st.enter_context(nc.psum_tensor(name + "_poT", [128, 512], F32))
            pbc = st.enter_context(nc.psum_tensor(name + "_pbc", [128, 512], F32))
            pT = (pT0, pT1)

            def score_slot(kt):
                j = kt % 4
                if j == 0:
                    return psT[:, 0, :]
                if j == 1:
                    return psT[:, 1, :]
                if j == 2:
                    return pQV[:]
                return pK[:]

            if weights_mode == "transposed":
                for dst, src in ((wqT, wq_d), (wkT, wk_d), (wvT, wv_d)):
                    def _ld(e, src=src, o0=head0 * HD, wt=C.val("tmm")):
                        if wt > 0:
                            e.wait_ge(sem["tmm"], wt)
                        e.dma_start(wtmp[:], src[o0:o0 + 512, :]
                                    .rearrange("(c p) i -> p c i", p=128)
                                    ).then_inc(sem["swg"], 16)
                    ph.em("gpsimd", _ld)
                    C.add("swg", 16)
                    wsw = C.val("swg")
                    for oc in range(4):
                        def xw(e, wsw=wsw):
                            e.wait_ge(sem["swg"], wsw)
                        transposes(
                            ph, pT,
                            lambda j, oc=oc: wtmp[:, oc, j * 128:(j + 1) * 128],
                            lambda j, oc=oc, dst=dst: dst[:, j, oc * 128:(oc + 1) * 128],
                            DC, extra_tensor_wait=xw)
            else:
                for dst, src in ((fwqb, fwq_d), (fwkb, fwk_d), (fwvb, fwv_d)):
                    def _ld(e, dst=dst, src=src):
                        e.dma_start(dst[:], src.rearrange("(c p) o -> p c o", p=128)
                                    ).then_inc(sem["swg"], 16)
                    ph.em("gpsimd", _ld)
                    C.add("swg", 16)

            def _vones(e):
                e.memset(Vb[:], 1.0).then_inc(sem["vev"], 1)
                if QPAD:
                    e.memset(qTb[:], 0.0).then_inc(sem["vev"], 1)
            ph.em("vector", _vones)
            C.add("vev", 2 if QPAD else 1)
            W_SWG = C.val("swg")

            xsb_free = [0, 0]
            csb_free = [0, 0]
            stt_war = [0, 0]
            hb_war = [0]
            hsc_war = [0]

            def getq(j, oh):
                return (wqT if weights_mode == "transposed" else fwqb)[
                    :, j, oh * 512:(oh + 1) * 512]

            def getk(j):
                return (wkT if weights_mode == "transposed" else fwkb)[:, j, :]

            def getv(j):
                return (wvT if weights_mode == "transposed" else fwvb)[:, j, :]

            def newton(e, stt):
                e.tensor_tensor(stt[:, 6:7], stt[:, 5:6], stt[:, 5:6], ALU.mult)
                e.tensor_tensor(stt[:, 6:7], stt[:, 6:7], stt[:, 3:4], ALU.mult)
                e.tensor_scalar(stt[:, 6:7], stt[:, 6:7], -0.5, 1.5, ALU.mult, ALU.add)
                e.tensor_tensor(stt[:, 6:7], stt[:, 6:7], stt[:, 5:6], ALU.mult)

            def emit_A1_chunk(s, c4):
                c = s * 4 + c4
                slot = c % 2
                po = 8 * slot
                sxs = f"sx{slot}"

                def _ldx(e, c=c, slot=slot, wx=xsb_free[slot], wc=csb_free[slot],
                         sxs=sxs):
                    if wx > 0:
                        e.wait_ge(sem["ssc"], wx)
                    e.dma_start(xsb[:, slot, :], src_x_d[c * 128:(c + 1) * 128, :]
                                ).then_inc(sem[sxs], 16)
                    if wc > 0:
                        e.wait_ge(sem["vev"], wc)
                    e.dma_start(csb[:, slot, :, 0:nqh, :],
                                cs_d[:, c * 128:(c + 1) * 128, head0:head0 + nqh, :]
                                .rearrange("a p h f -> p a h f")
                                ).then_inc(sem[sxs], 16)
                ph.em("sync", _ldx)
                C.add(sxs, 32)

                def _red(e, slot=slot, po=po, wx=C.val(sxs), ws=stt_war[slot],
                         sxs=sxs):
                    e.wait_ge(sem[sxs], wx)
                    if ws > 0:
                        e.wait_ge(sem["ssc"], ws)
                    e.reduce_sum(stt[:, po:po + 1], xsb[:, slot, :],
                                 axis=AX.X).then_inc(sem["vev"], 1)
                ph.em("vector", _red)
                C.add("vev")

                def _chain(e, slot=slot, po=po, wv=C.val("vev"), wt=hb_war[0],
                           wh=hsc_war[0], b1=C.val("ssc") + 1, rms=ln_rms):
                    e.wait_ge(sem["vev"], wv)

                    def s_(i):
                        return stt[:, po + i:po + i + 1]
                    xc = xcb[:, slot, :]
                    e.activation(s_(1), s_(0), AF.Copy,
                                 scale=-1.0 / D).then_inc(sem["ssc"], 1)
                    e.wait_ge(sem["ssc"], b1)
                    e.add(xc, xsb[:, slot, :], s_(1))
                    e.activation(sqd[:], xc, AF.Square, accum_out=s_(2))
                    e.activation(s_(3), s_(2), AF.Copy, scale=1.0 / D, bias=EPS_LN)
                    e.activation(s_(4), s_(3), AF.Ln)
                    e.activation(s_(5), s_(4), AF.Exp,
                                 scale=-0.5).then_inc(sem["ssc"], 1)
                    e.wait_ge(sem["ssc"], b1 + 1)
                    e.mul(s_(6), s_(5), s_(5))
                    e.mul(s_(6), s_(6), s_(3))
                    e.activation(s_(6), s_(6), AF.Copy, scale=-0.5, bias=1.5)
                    e.mul(s_(6), s_(6), s_(5)).then_inc(sem["ssc"], 1)
                    e.wait_ge(sem["ssc"], b1 + 2)
                    if not rms:
                        if wt > 0:
                            e.wait_ge(sem["tmm"], wt)
                        e.mul(hb[:], xc, s_(6)).then_inc(sem["ssc"], 1)
                    else:
                        e.mul(xc, xc, s_(6))
                        e.activation(sqd[:], xc, AF.Square, accum_out=s_(2))
                        e.activation(s_(3), s_(2), AF.Copy, scale=1.0 / D,
                                     bias=EPS_RMS)
                        e.activation(s_(4), s_(3), AF.Ln)
                        e.activation(s_(5), s_(4), AF.Exp,
                                     scale=-0.5).then_inc(sem["ssc"], 1)
                        e.wait_ge(sem["ssc"], b1 + 3)
                        e.mul(s_(6), s_(5), s_(5))
                        e.mul(s_(6), s_(6), s_(3))
                        e.activation(s_(6), s_(6), AF.Copy, scale=-0.5, bias=1.5)
                        e.mul(s_(6), s_(6), s_(5)).then_inc(sem["ssc"], 1)
                        e.wait_ge(sem["ssc"], b1 + 4)
                        if wh > 0:
                            e.wait_ge(sem["vev"], wh)
                        e.mul(hsc[:], xc, s_(6)).then_inc(sem["ssc"], 1)
                ph.em("scalar", _chain)
                C.add("ssc", 4 if not ln_rms else 6)
                stt_war[slot] = C.val("ssc")
                xsb_free[slot] = C.val("ssc")

                if ln_rms:
                    def _hTT(e, ws=C.val("ssc"), wt=hb_war[0]):
                        e.wait_ge(sem["ssc"], ws)
                        if wt > 0:
                            e.wait_ge(sem["tmm"], wt)
                        e.tensor_tensor(hb[:], hsc[:], ln1r[:],
                                        ALU.mult).then_inc(sem["vev"], 1)
                    ph.em("vector", _hTT)
                    C.add("vev")
                    hsc_war[0] = C.val("vev")
                    hb_ready = ("vev", C.val("vev"))
                else:
                    hb_ready = ("ssc", C.val("ssc"))

                transposes(ph, pT,
                           lambda j: hb[:, j * 128:(j + 1) * 128],
                           lambda j: hTb[:, j, :], DC, data_wait=hb_ready)
                hb_war[0] = C.val("tmm")

                for oh in range(QCOL // 512):
                    def _qmm(e, oh=oh, wv=C.val("vev"), ws=W_SWG, wsc=C.val("ssc")):
                        e.wait_ge(sem["swg"], ws)
                        e.wait_ge(sem["vev"], wv)
                        e.wait_ge(sem["ssc"], wsc)
                        for j in range(DC):
                            mm = e.matmul(pQV[:], hTb[:, j, :], getq(j, oh),
                                          start=(j == 0), stop=(j == DC - 1))
                        mm.then_inc(sem["tmm"], 1)
                    ph.em("tensor", _qmm)
                    C.add("tmm")

                    def _rope(e, oh=oh, wt=C.val("tmm"), slot=slot,
                              bar=C.val("vev") + 1):
                        e.wait_ge(sem["tmm"], wt)
                        h0 = oh * 8
                        q1 = bass.AP(pQV, 0, [[512, 128], [64, 8], [1, 32]])
                        q2 = bass.AP(pQV, 32, [[512, 128], [64, 8], [1, 32]])
                        cosa = csb[:, slot, 0, h0:h0 + 8, :]
                        sina = csb[:, slot, 1, h0:h0 + 8, :]
                        e.tensor_tensor(rtmp[:, 0, 0:8, :], q1, cosa, ALU.mult)
                        e.tensor_tensor(rtmp[:, 1, 0:8, :], q2, sina, ALU.mult)
                        e.tensor_tensor(rtmp[:, 2, 0:8, :], q2, cosa, ALU.mult)
                        e.tensor_tensor(rtmp[:, 3, 0:8, :], q1, sina,
                                        ALU.mult).then_inc(sem["vev"], 1)
                        e.wait_ge(sem["vev"], bar)
                        e.tensor_tensor(
                            bass.AP(qrb, oh * 512, [[QCOL, 128], [64, 8], [1, 32]]),
                            rtmp[:, 0, 0:8, :], rtmp[:, 1, 0:8, :], ALU.subtract)
                        e.tensor_tensor(
                            bass.AP(qrb, oh * 512 + 32, [[QCOL, 128], [64, 8], [1, 32]]),
                            rtmp[:, 2, 0:8, :], rtmp[:, 3, 0:8, :],
                            ALU.add).then_inc(sem["vev"], 1)
                    ph.em("vector", _rope)
                    C.add("vev", 2)

                def _kmm(e, ws=W_SWG):
                    for j in range(DC):
                        mm = e.matmul(pK[:], hTb[:, j, :], getk(j),
                                      start=(j == 0), stop=(j == DC - 1))
                    mm.then_inc(sem["tmm"], 1)
                ph.em("tensor", _kmm)
                C.add("tmm")

                def _ropek(e, wt=C.val("tmm"), slot=slot, bar=C.val("vev") + 1):
                    e.wait_ge(sem["tmm"], wt)
                    k1 = bass.AP(pK, 0, [[512, 128], [64, NKV], [1, 32]])
                    k2 = bass.AP(pK, 32, [[512, 128], [64, NKV], [1, 32]])
                    cosa = csb[:, slot, 0, 0:NKV, :]
                    sina = csb[:, slot, 1, 0:NKV, :]
                    e.tensor_tensor(rtmp[:, 0, 0:NKV, :], k1, cosa, ALU.mult)
                    e.tensor_tensor(rtmp[:, 1, 0:NKV, :], k2, sina, ALU.mult)
                    e.tensor_tensor(rtmp[:, 2, 0:NKV, :], k2, cosa, ALU.mult)
                    e.tensor_tensor(rtmp[:, 3, 0:NKV, :], k1, sina,
                                    ALU.mult).then_inc(sem["vev"], 1)
                    e.wait_ge(sem["vev"], bar)
                    e.tensor_tensor(bass.AP(krb, 0, [[512, 128], [64, NKV], [1, 32]]),
                                    rtmp[:, 0, 0:NKV, :], rtmp[:, 1, 0:NKV, :],
                                    ALU.subtract)
                    e.tensor_tensor(bass.AP(krb, 32, [[512, 128], [64, NKV], [1, 32]]),
                                    rtmp[:, 2, 0:NKV, :], rtmp[:, 3, 0:NKV, :],
                                    ALU.add).then_inc(sem["vev"], 1)
                ph.em("vector", _ropek)
                C.add("vev", 2)
                csb_free[slot] = C.val("vev")

                def _norms(e, c=c, c4=c4, w0=C.val("vev"), bar=C.val("vev") + 1):
                    e.wait_ge(sem["vev"], w0)
                    e.tensor_tensor(nsq[:, 0:QCOL], qrb[:], qrb[:], ALU.mult)
                    e.tensor_tensor(nsq[:, 1024:1536], krb[:], krb[:],
                                    ALU.mult).then_inc(sem["vev"], 1)
                    e.wait_ge(sem["vev"], bar)
                    e.tensor_reduce(nst[:, 0:nqh],
                                    bass.AP(nsq, 0, [[1536, 128], [64, nqh], [1, 64]]),
                                    axis=AX.X, op=ALU.add)
                    e.tensor_reduce(nst[:, 16:16 + NKV],
                                    bass.AP(nsq, 1024, [[1536, 128], [64, NKV], [1, 64]]),
                                    axis=AX.X, op=ALU.add).then_inc(sem["vev"], 1)
                    e.wait_ge(sem["vev"], bar + 1)
                    if c4 == 0:
                        e.tensor_copy(qsqm[:], nst[:, 0:nqh])
                    else:
                        e.tensor_tensor(qsqm[:], qsqm[:], nst[:, 0:nqh], ALU.max)
                    if c == 0:
                        e.tensor_copy(ksqm[:], nst[:, 16:16 + NKV]
                                      ).then_inc(sem["vev"], 1)
                    else:
                        e.tensor_tensor(ksqm[:], ksqm[:], nst[:, 16:16 + NKV],
                                        ALU.max).then_inc(sem["vev"], 1)
                ph.em("vector", _norms)
                C.add("vev", 3)

                if not QPAD:
                    transposes(ph, pT,
                               lambda j: qrb[:, j * 128:(j + 1) * 128],
                               lambda j, c4=c4: qTb[:, j, c4 * 128:(c4 + 1) * 128],
                               QCOL // 128)
                else:
                    first_wait = C.val("vev")
                    for j in range(QCOL // 128):
                        pt = pT[j % 2]

                        def _trq(e, pt=pt, j=j, w0=first_wait,
                                 wwar=(C.val("vev") - 1 if j >= 2 else 0),
                                 src=qrb[:, j * 128:(j + 1) * 128]):
                            if j == 0 and w0 > 0:
                                e.wait_ge(sem["vev"], w0)
                            if wwar > 0:
                                e.wait_ge(sem["vev"], wwar)
                            e.transpose(pt[:], src, identb[:]).then_inc(sem["tmm"], 1)
                        ph.em("tensor", _trq)
                        C.add("tmm")

                        def _evq(e, pt=pt, j=j, wt=C.val("tmm"), c4=c4):
                            e.wait_ge(sem["tmm"], wt)
                            hA = PERM_Q[2 * j]
                            hB = PERM_Q[2 * j + 1]
                            e.tensor_copy(qTb[0:64, hA, c4 * 128:(c4 + 1) * 128],
                                          pt[0:64, :])
                            e.tensor_copy(qTb[64:128, hB, c4 * 128:(c4 + 1) * 128],
                                          pt[64:128, :]).then_inc(sem["vev"], 1)
                        ph.em("vector", _evq)
                        C.add("vev")
                transposes(ph, pT,
                           lambda j: krb[:, j * 128:(j + 1) * 128],
                           lambda j, c=c: KTb[:, j, c * 128:(c + 1) * 128], 4)

                def _vmm(e, wv=C.val("vev"), ws=W_SWG):
                    e.wait_ge(sem["vev"], wv)
                    for j in range(DC):
                        mm = e.matmul(pQV[:], hTb[:, j, :], getv(j),
                                      start=(j == 0), stop=(j == DC - 1))
                    mm.then_inc(sem["tmm"], 1)
                ph.em("tensor", _vmm)
                C.add("tmm")

                def _vev(e, wt=C.val("tmm"), c=c):
                    e.wait_ge(sem["tmm"], wt)
                    e.tensor_copy(
                        bass.AP(Vb, c * NKV * 65,
                                [[NCH * NKV * 65, 128], [65, NKV], [1, 64]]),
                        pQV[:]).then_inc(sem["vev"], 1)
                ph.em("vector", _vev)
                C.add("vev")

            def emit_negC(s):
                def _t1(e, wv=C.val("vev")):
                    e.wait_ge(sem["vev"], wv)
                    e.transpose(bass.AP(pbc, 0, [[512, nqh], [1, 128]]),
                                qsqm[:], identf[:]).then_inc(sem["tmm"], 1)
                ph.em("tensor", _t1)
                C.add("tmm")

                def _r1(e, wt=C.val("tmm")):
                    e.wait_ge(sem["tmm"], wt)
                    e.reduce_max(qmx[0:nqh, :],
                                 bass.AP(pbc, 0, [[512, nqh], [1, 128]]),
                                 axis=AX.X).then_inc(sem["vev"], 1)
                ph.em("vector", _r1)
                C.add("vev")

                if QPAD:
                    def _kx(e, w0=C.val("vev")):
                        e.wait_ge(sem["vev"], w0)
                        for r in range(4):
                            mm = e.tensor_copy(
                                bass.AP(ksqe, r, [[16, 128], [4, 4]]),
                                bass.AP(ksqm, r % 2, [[NKV, 128], [2, 4]]))
                        mm.then_inc(sem["vev"], 1)
                    ph.em("vector", _kx)
                    C.add("vev")
                    ktr, nkc = ksqe, 16
                else:
                    ktr, nkc = ksqm, NKV

                def _t2(e, wv=C.val("vev"), ktr=ktr, nkc=nkc):
                    e.wait_ge(sem["vev"], wv)
                    e.transpose(bass.AP(pbc, 0, [[512, nkc], [1, 128]]),
                                ktr[:], identf[:]).then_inc(sem["tmm"], 1)
                ph.em("tensor", _t2)
                C.add("tmm")

                def _r2(e, wt=C.val("tmm"), nkc=nkc):
                    e.wait_ge(sem["tmm"], wt)
                    e.reduce_max(kmx[0:nkc, :],
                                 bass.AP(pbc, 0, [[512, nkc], [1, 128]]),
                                 axis=AX.X).then_inc(sem["vev"], 1)
                ph.em("vector", _r2)
                C.add("vev")

                def _b1(e, wv=C.val("vev"), nkc=nkc):
                    e.wait_ge(sem["vev"], wv)
                    e.dma_start(ncb_d[0, 0:nqh],
                                bass.AP(qmx, 0, [[1, nqh], [1, 1]])
                                ).then_inc(sem["snc"], 16)
                    e.dma_start(ncb_d[1, 0:nkc],
                                bass.AP(kmx, 0, [[1, nkc], [1, 1]])
                                ).then_inc(sem["snc"], 16)
                ph.em("gpsimd", _b1)
                C.add("snc", 32)

                def _b2(e, ws=C.val("snc"), nq=nqh):
                    e.wait_ge(sem["snc"], ws)
                    e.dma_start(bass.AP(qrow, 0, [[16, 1], [1, nq]]),
                                ncb_d[0, 0:nq]).then_inc(sem["snc"], 16)
                    e.dma_start(bass.AP(krow, 0, [[16, 1], [1, nq]]),
                                ncb_d[1, 0:nq]).then_inc(sem["snc"], 16)
                ph.em("gpsimd", _b2)
                C.add("snc", 32)

                def _cr(e, ws=C.val("snc"), bar=C.val("vev") + 1):
                    e.wait_ge(sem["snc"], ws)
                    e.tensor_tensor(qrow[:, 0:nqh], qrow[:, 0:nqh], krow[:, 0:nqh],
                                    ALU.add).then_inc(sem["vev"], 1)
                    e.wait_ge(sem["vev"], bar)
                    e.tensor_scalar(crow[:, 0:nqh], qrow[:, 0:nqh], -1.0 / 16.0, None,
                                    ALU.mult).then_inc(sem["vev"], 1)
                ph.em("vector", _cr)
                C.add("vev", 2)

                def _bc(e, wv=C.val("vev")):
                    e.wait_ge(sem["vev"], wv)
                    e.matmul(bass.AP(pbc, 0, [[512, 128], [1, nqh]]),
                             onesb[0:1, :], crow[:, 0:nqh],
                             start=True, stop=True).then_inc(sem["tmm"], 1)
                ph.em("tensor", _bc)
                C.add("tmm")

                def _ev(e, wt=C.val("tmm")):
                    e.wait_ge(sem["tmm"], wt)
                    e.tensor_copy(negCb[:, 0:nqh],
                                  bass.AP(pbc, 0, [[512, 128], [1, nqh]])
                                  ).then_inc(sem["vev"], 1)
                ph.em("vector", _ev)
                C.add("vev")

            def emit_A2_head(s, h8, first_of_phase):
                nkt = 4 * (s + 1)
                NG = s + 1
                kvh = kv_of(h8)
                poff = (h8 % 2) * 64
                kpoff = (kvh % 2) * 64
                hg = h8 // 2
                khg = kvh // 2
                A1_VEV = C.val("vev")
                ssc0 = C.val("ssc")
                tmm0 = C.val("tmm")
                tmv0 = C.val("tmv")
                vmk0 = C.val("vmk")
                sout0 = PH_SOUT0 if first_of_phase else None
                NEGC_VEV = C.val("vev")

                if QPAD:
                    def sc_mm(e, kt):
                        e.matmul(score_slot(kt),
                                 KTb[:, khg, kt * 128:(kt + 1) * 128],
                                 qTb[:, h8, :],
                                 start=True, stop=True).then_inc(sem["tmm"], 1)
                else:
                    def sc_mm(e, kt):
                        e.matmul(score_slot(kt),
                                 KTb[kpoff:kpoff + 64, khg, kt * 128:(kt + 1) * 128],
                                 qTb[poff:poff + 64, hg, :],
                                 start=True, stop=True).then_inc(sem["tmm"], 1)

                def av_mm(e, kt):
                    e.matmul(bass.AP(poT, 0, [[512, 65], [1, 512]]),
                             Vb[:, kt, kvh, :], expb[:, kt % 4, :],
                             start=(kt == 0), stop=(kt == nkt - 1),
                             skip_group_check=True).then_inc(sem["tmv"], 1)

                def _ten(e, s=s, nkt=nkt, NG=NG, ssc0=ssc0, vmk0=vmk0,
                         A1_VEV=A1_VEV):
                    e.wait_ge(sem["vev"], A1_VEV)
                    for g in range(NG):
                        if ssc0 + 4 * g > 0:
                            e.wait_ge(sem["ssc"], ssc0 + 4 * g)
                        for kt in range(4 * g, 4 * g + 4):
                            sc_mm(e, kt)
                        if g >= 1:
                            for kt in range(4 * (g - 1), 4 * g):
                                av_mm(e, kt)
                    e.wait_ge(sem["vmk"], vmk0 + 4)
                    for kt in range(nkt - 4, nkt):
                        av_mm(e, kt)
                ph.em("tensor", _ten)
                C.add("tmm", nkt)
                C.add("tmv", nkt)

                def _sca(e, nkt=nkt, NG=NG, tmm0=tmm0, tmv0=tmv0, h8=h8,
                         wnc=NEGC_VEV):
                    e.wait_ge(sem["vev"], wnc)
                    for g in range(NG):
                        if tmv0 + 4 * g > 0:
                            e.wait_ge(sem["tmv"], tmv0 + 4 * g)
                        e.wait_ge(sem["tmm"], tmm0 + 4 * g + 4)
                        for kt in range(4 * g, 4 * g + 4):
                            bc = POS_OF[h8] if QPAD else h8
                            e.activation(expb[:, kt % 4, :], score_slot(kt), AF.Exp,
                                         bias=negCb[:, bc:bc + 1],
                                         scale=0.125).then_inc(sem["ssc"], 1)
                ph.em("scalar", _sca)
                C.add("ssc", nkt)

                def _msk(e, nkt=nkt, ssc0=ssc0):
                    e.wait_ge(sem["ssc"], ssc0 + nkt)
                    for dd in range(4):
                        kt = nkt - 4 + dd
                        e.tensor_tensor(expb[:, kt % 4, :], expb[:, kt % 4, :],
                                        maskb[:, dd, :], ALU.mult
                                        ).then_inc(sem["vmk"], 1)
                ph.em("vector", _msk)
                C.add("vmk", 4)

                def _tail(e, wt=C.val("tmv"), so=sout0, bar=C.val("vev") + 1):
                    if so is not None:
                        for sname, v in zip(("so0", "so1", "sout"), so):
                            if v > 0:
                                e.wait_ge(sem[sname], v)
                    e.wait_ge(sem["tmv"], wt)
                    e.reciprocal(rbf[64:65, :],
                                 bass.AP(poT, 64 * 512, [[512, 1], [1, 512]])
                                 ).then_inc(sem["vev"], 1)
                    e.wait_ge(sem["vev"], bar)
                    e.tensor_copy(rib[64:65, :], rbf[64:65, :]).then_inc(sem["vev"], 1)
                ph.em("vector", _tail)
                C.add("vev", 2)

                def _tbc(e, wv=C.val("vev")):
                    e.wait_ge(sem["vev"], wv)
                    e.matmul(bass.AP(pbc, 0, [[512, 64], [1, 512]]),
                             onesb[64:65, 0:64], rib[64:65, :],
                             start=True, stop=True).then_inc(sem["tmm"], 1)
                ph.em("tensor", _tbc)
                C.add("tmm")

                def _tev(e, wt=C.val("tmm"), hg=hg, poff=poff, bar=C.val("vev") + 1):
                    e.wait_ge(sem["tmm"], wt)
                    e.tensor_copy(rbf[0:64, :],
                                  bass.AP(pbc, 0, [[512, 64], [1, 512]])
                                  ).then_inc(sem["vev"], 1)
                    e.wait_ge(sem["vev"], bar)
                    e.tensor_tensor(
                        bass.AP(aoutb, poff * (QHG * 512) + hg * 512,
                                [[QHG * 512, 64], [1, 512]]),
                        bass.AP(poT, 0, [[512, 64], [1, 512]]),
                        rbf[0:64, :], ALU.mult).then_inc(sem["vev"], 1)
                ph.em("vector", _tev)
                C.add("vev", 2)

            for s in range(NSUP):
                for c4 in range(4):
                    emit_A1_chunk(s, c4)
                emit_negC(s)
                for h8 in range(nqh):
                    emit_A2_head(s, h8, first_of_phase=(s == 0 and h8 == 0))

                def _sp(e, s=s, wv=C.val("vev"), h0g=head0 // 2):
                    e.wait_ge(sem["vev"], wv)
                    e.dma_start(aout_d[s, :, h0g:h0g + QHG, :],
                                aoutb[:]).then_inc(sem["sout"], 16)
                ph.em("sync", _sp)
                C.add("sout", 16)
                if s + 1 < NSUP:
                    def _w(e, ws=C.val("sout")):
                        e.wait_ge(sem["sout"], ws)
                    ph.em("vector", _w)

            ph.play(nc)
            st.close()

        def emit_proj(name, w_src, w_is_transposed, resid_mode, resid_src_d, dst_d,
                      aout_ready_sout):
            ph = Phase(name)
            st = contextlib.ExitStack()
            wob = st.enter_context(nc.sbuf_tensor(name + "_wob", [128, DC, D], BF16))
            wtmp = st.enter_context(nc.sbuf_tensor(name + "_wtmp", [128, DC, D], BF16))
            aob = st.enter_context(nc.sbuf_tensor(name + "_aob", [128, 2, DC, 512], BF16))
            xab = st.enter_context(nc.sbuf_tensor(name + "_xab", [128, 2, D], F32))
            orb = st.enter_context(nc.sbuf_tensor(name + "_orb", [128, 2, D], F32))
            stt = st.enter_context(nc.sbuf_tensor(name + "_stt", [128, 16], F32))
            sqs = st.enter_context(nc.sbuf_tensor(name + "_sqs", [128, D], F32))
            sq2 = st.enter_context(nc.sbuf_tensor(name + "_sq2", [128, D], F32))
            xns = st.enter_context(nc.sbuf_tensor(name + "_xns", [128, D], F32))
            pP = st.enter_context(nc.psum_tensor(name + "_pP", [128, 2, 512], F32))
            pT0 = st.enter_context(nc.psum_tensor(name + "_pT0", [128, 128], BF16))
            pT1 = st.enter_context(nc.psum_tensor(name + "_pT1", [128, 128], BF16))
            pT = (pT0, pT1)

            def _ld(e):
                e.dma_start(wtmp[:], w_src.rearrange("(c p) i -> p c i", p=128)
                            ).then_inc(sem["swg"], 16)
            ph.em("gpsimd", _ld)
            C.add("swg", 16)
            wsw = C.val("swg")
            if w_is_transposed:
                for oc in range(DC):
                    def xw(e, wsw=wsw):
                        e.wait_ge(sem["swg"], wsw)
                    transposes(ph, pT,
                               lambda j, oc=oc: wtmp[:, oc, j * 128:(j + 1) * 128],
                               lambda j, oc=oc: wob[:, j, oc * 128:(oc + 1) * 128],
                               DC, extra_tensor_wait=xw)
                wsrcb = wob
            else:
                wsrcb = wtmp
            W_VEV = C.val("vev")
            W_SWG = C.val("swg")

            aob_war = [0, 0]
            orb_war = [0, 0]
            xab_war = [0, 0]
            stt_war = [0, 0]
            xns_war = [0]
            for s in range(NSUP):
                slot = s % 2

                sas = f"sa{slot}"

                def _lda(e, s=s, slot=slot, so=aout_ready_sout, wt=aob_war[slot],
                         sas=sas):
                    e.wait_ge(sem["sout"], so)
                    if wt > 0:
                        e.wait_ge(sem["tmm"], wt)
                    e.dma_start(aob[:, slot, :, :], aout_d[s]).then_inc(sem[sas], 16)
                ph.em("sync", _lda)
                C.add(sas, 16)

                for t4 in range(4):
                    c = s * 4 + t4
                    cslot = c % 2
                    sxs = f"sx{cslot}"

                    wsem = "vev" if resid_mode == "x" else "ssc"

                    def _ldx(e, c=c, cslot=cslot, wv=xab_war[cslot], sxs=sxs,
                             wsem=wsem):
                        if wv > 0:
                            e.wait_ge(sem[wsem], wv)
                        e.dma_start(xab[:, cslot, :],
                                    resid_src_d[c * 128:(c + 1) * 128, :]
                                    ).then_inc(sem[sxs], 16)
                    ph.em("sync", _ldx)
                    C.add(sxs, 16)

                    for dh in range(2):
                        def _mm(e, slot=slot, t4=t4, dh=dh, wsa=C.val(sas),
                                wv=W_VEV, ws=W_SWG, war=C.val("vev"), sas=sas):
                            e.wait_ge(sem[sas], wsa)
                            e.wait_ge(sem["swg"], ws)
                            if wv > 0:
                                e.wait_ge(sem["vev"], wv)
                            e.wait_ge(sem["vev"], war)
                            for hg in range(DC):
                                mm = e.matmul(pP[:, dh, :],
                                              aob[:, slot, hg, t4 * 128:(t4 + 1) * 128],
                                              wsrcb[:, hg, dh * 512:(dh + 1) * 512],
                                              start=(hg == 0), stop=(hg == DC - 1))
                            mm.then_inc(sem["tmm"], 1)
                        ph.em("tensor", _mm)
                        C.add("tmm")
                    aob_war[slot] = C.val("tmm")

                    if resid_mode == "x":
                        def _add(e, cslot=cslot, wt=C.val("tmm"), wx=C.val(sxs),
                                 wo=orb_war[cslot], sxs=sxs):
                            e.wait_ge(sem[sxs], wx)
                            e.wait_ge(sem["tmm"], wt)
                            if wo > 0:
                                e.wait_ge(sem[f"so{cslot}"], wo)
                            e.tensor_tensor(orb[:, cslot, 0:512], xab[:, cslot, 0:512],
                                            pP[:, 0, :], ALU.add)
                            e.tensor_tensor(orb[:, cslot, 512:1024],
                                            xab[:, cslot, 512:1024],
                                            pP[:, 1, :], ALU.add).then_inc(sem["vev"], 1)
                        ph.em("vector", _add)
                        C.add("vev")
                        xab_war[cslot] = C.val("vev")
                    else:
                        po = 8 * cslot

                        def _red(e, cslot=cslot, po=po, wx=C.val(sxs), sxs=sxs,
                                 ws=stt_war[cslot]):
                            e.wait_ge(sem[sxs], wx)
                            if ws > 0:
                                e.wait_ge(sem["ssc"], ws)
                            e.reduce_sum(stt[:, po:po + 1], xab[:, cslot, :],
                                         axis=AX.X).then_inc(sem["vev"], 1)
                        ph.em("vector", _red)
                        C.add("vev")

                        def _chain(e, cslot=cslot, po=po, wv=C.val("vev"),
                                   wh=xns_war[0], b1=C.val("ssc") + 1):
                            e.wait_ge(sem["vev"], wv)

                            def s_(i):
                                return stt[:, po + i:po + i + 1]
                            xa = xab[:, cslot, :]
                            e.activation(s_(1), s_(0), AF.Copy,
                                         scale=-1.0 / D).then_inc(sem["ssc"], 1)
                            e.wait_ge(sem["ssc"], b1)
                            e.add(sqs[:], xa, s_(1))
                            e.activation(sq2[:], sqs[:], AF.Square, accum_out=s_(2))
                            e.activation(s_(3), s_(2), AF.Copy, scale=1.0 / D,
                                         bias=EPS_LN)
                            e.activation(s_(4), s_(3), AF.Ln)
                            e.activation(s_(5), s_(4), AF.Exp,
                                         scale=-0.5).then_inc(sem["ssc"], 1)
                            e.wait_ge(sem["ssc"], b1 + 1)
                            e.mul(s_(6), s_(5), s_(5))
                            e.mul(s_(6), s_(6), s_(3))
                            e.activation(s_(6), s_(6), AF.Copy, scale=-0.5, bias=1.5)
                            e.mul(s_(6), s_(6), s_(5)).then_inc(sem["ssc"], 1)
                            e.wait_ge(sem["ssc"], b1 + 2)
                            if wh > 0:
                                e.wait_ge(sem["vev"], wh)
                            e.mul(xns[:], sqs[:], s_(6)).then_inc(sem["ssc"], 1)
                        ph.em("scalar", _chain)
                        C.add("ssc", 4)
                        stt_war[cslot] = C.val("ssc")
                        xab_war[cslot] = C.val("ssc")

                        def _fin(e, cslot=cslot, ws=C.val("ssc"), wt=C.val("tmm"),
                                 wo=orb_war[cslot]):
                            e.wait_ge(sem["ssc"], ws)
                            e.wait_ge(sem["tmm"], wt)
                            if wo > 0:
                                e.wait_ge(sem[f"so{cslot}"], wo)
                            e.tensor_tensor(orb[:, cslot, 0:512], xns[:, 0:512],
                                            pP[:, 0, :], ALU.add)
                            e.tensor_tensor(orb[:, cslot, 512:1024], xns[:, 512:1024],
                                            pP[:, 1, :], ALU.add).then_inc(sem["vev"], 1)
                        ph.em("vector", _fin)
                        C.add("vev")
                        xns_war[0] = C.val("vev")

                    def _st(e, c=c, cslot=cslot, wv=C.val("vev")):
                        e.wait_ge(sem["vev"], wv)
                        e.dma_start(dst_d[c * 128:(c + 1) * 128, :],
                                    orb[:, cslot, :]).then_inc(sem[f"so{cslot}"], 16)
                    ph.em("sync", _st)
                    C.add(f"so{cslot}", 16)
                    orb_war[cslot] = C.val(f"so{cslot}")
            ph.play(nc)
            st.close()

        def newton(e, stt):
            e.tensor_tensor(stt[:, 6:7], stt[:, 5:6], stt[:, 5:6], ALU.mult)
            e.tensor_tensor(stt[:, 6:7], stt[:, 6:7], stt[:, 3:4], ALU.mult)
            e.tensor_scalar(stt[:, 6:7], stt[:, 6:7], -0.5, 1.5, ALU.mult, ALU.add)
            e.tensor_tensor(stt[:, 6:7], stt[:, 6:7], stt[:, 5:6], ALU.mult)

        emit_attention("Ap0", x_d, 8, lambda h: h, 0, "transposed")
        emit_attention("Ap1", x_d, 8, lambda h: h, 8, "transposed")
        ASP1 = C.val("sout")
        emit_proj("A3", wo_d, True, "x", x_d, x1_d, ASP1)
        X1_DONE = (C.val("so0"), C.val("so1"))

        ln1r_g = es.enter_context(nc.sbuf_tensor("ln1r", [128, D], F32))
        ph = Phase("LD1")

        def _l1(e, wo=X1_DONE):
            e.wait_ge(sem["so0"], wo[0])
            e.wait_ge(sem["so1"], wo[1])
            e.dma_start(ln1r_g[:], ln1_d).then_inc(sem["saux"], 16)
        ph.em("sync", _l1)
        C.add("saux", 16)

        def _l1w(e, wa=C.val("saux")):
            e.wait_ge(sem["saux"], wa)
        ph.em("vector", _l1w)
        ph.play(nc)

        emit_attention("F1", x1_d, 16, lambda h: h // 2, 0, "natural",
                       ln_rms=True, ln1r=ln1r_g)
        FSP = C.val("sout")
        emit_proj("F1A3", fwo_d, False, "lnx1", x1_d, y1_d, FSP)
        Y1_DONE = (C.val("so0"), C.val("so1"))

        ph = Phase("F2pre")
        st = contextlib.ExitStack()
        ln2r = st.enter_context(nc.sbuf_tensor("ln2r", [128, D], F32))
        rwb = st.enter_context(nc.sbuf_tensor("rwb", [128, DC, NE], BF16))
        ysb = st.enter_context(nc.sbuf_tensor("f2p_ysb", [128, 2, D], F32))
        stt2 = st.enter_context(nc.sbuf_tensor("f2p_stt", [128, 24], F32))
        sqs2 = st.enter_context(nc.sbuf_tensor("f2p_sqs", [128, D], F32))
        h2b = st.enter_context(nc.sbuf_tensor("f2p_h2b", [128, D], BF16))
        h2T = st.enter_context(nc.sbuf_tensor("f2p_h2T", [128, 2, DC, 128], BF16))
        prb = st.enter_context(nc.sbuf_tensor("f2p_prb", [128, NE], F32))
        wtb = st.enter_context(nc.sbuf_tensor("f2p_wtb", [128, NE], F32))
        pT0 = st.enter_context(nc.psum_tensor("f2p_pT0", [128, 128], BF16))
        pT1 = st.enter_context(nc.psum_tensor("f2p_pT1", [128, 128], BF16))
        pR = st.enter_context(nc.psum_tensor("f2p_pR", [128, NE], F32))
        pT2 = (pT0, pT1)

        def _ld2(e, wo=Y1_DONE):
            e.wait_ge(sem["so0"], wo[0])
            e.wait_ge(sem["so1"], wo[1])
            e.dma_start(ln2r[:], ln2_d).then_inc(sem["saux"], 16)
        ph.em("sync", _ld2)
        C.add("saux", 16)
        LN2_AUX = C.val("saux")

        def _ldr(e):
            e.dma_start(rwb[:], rw_d.rearrange("(c p) e -> p c e", p=128)
                        ).then_inc(sem["swg"], 16)
        ph.em("gpsimd", _ldr)
        C.add("swg", 16)
        RW_SWG = C.val("swg")

        ysb_free = [0, 0]
        h2T_war = [0, 0]
        stt2_war = [0, 0]
        hsc_war2 = [0]
        for c in range(NCH):
            slot = c % 2
            tslot = c % 2
            po = 8 * slot
            sxs = f"sx{slot}"

            def _ldy(e, c=c, slot=slot, wv=ysb_free[slot], sxs=sxs):
                if wv > 0:
                    e.wait_ge(sem["ssc"], wv)
                e.dma_start(ysb[:, slot, :], y1_d[c * 128:(c + 1) * 128, :]
                            ).then_inc(sem[sxs], 16)
            ph.em("sync", _ldy)
            C.add(sxs, 16)

            def _chain(e, slot=slot, po=po, wx=C.val(sxs), wa=LN2_AUX, sxs=sxs,
                       wh=hsc_war2[0], ws2=stt2_war[slot], b1=C.val("ssc") + 1):
                e.wait_ge(sem[sxs], wx)
                e.wait_ge(sem["saux"], wa)
                if ws2 > 0:
                    e.wait_ge(sem["ssc"], ws2)
                if wh > 0:
                    e.wait_ge(sem["vev"], wh)

                def s_(i):
                    return stt2[:, po + i:po + i + 1]
                yv = ysb[:, slot, :]
                e.activation(sqs2[:], yv, AF.Square, accum_out=s_(2))
                e.activation(s_(3), s_(2), AF.Copy, scale=1.0 / D, bias=EPS_RMS)
                e.activation(s_(4), s_(3), AF.Ln)
                e.activation(s_(5), s_(4), AF.Exp, scale=-0.5).then_inc(sem["ssc"], 1)
                e.wait_ge(sem["ssc"], b1)
                e.mul(s_(6), s_(5), s_(5))
                e.mul(s_(6), s_(6), s_(3))
                e.activation(s_(6), s_(6), AF.Copy, scale=-0.5, bias=1.5)
                e.mul(s_(6), s_(6), s_(5)).then_inc(sem["ssc"], 1)
                e.wait_ge(sem["ssc"], b1 + 1)
                e.mul(sqs2[:], yv, s_(6)).then_inc(sem["ssc"], 1)
            ph.em("scalar", _chain)
            C.add("ssc", 3)
            stt2_war[slot] = C.val("ssc")
            ysb_free[slot] = C.val("ssc")

            def _h2(e, ws=C.val("ssc")):
                e.wait_ge(sem["ssc"], ws)
                e.tensor_tensor(h2b[:], sqs2[:], ln2r[:],
                                ALU.mult).then_inc(sem["vev"], 1)
            ph.em("vector", _h2)
            C.add("vev")
            hsc_war2[0] = C.val("vev")

            def xvw(e, wo=h2T_war[tslot], tslot=tslot):
                if wo > 0:
                    e.wait_ge(sem[f"so{tslot}"], wo)
            transposes(ph, pT2,
                       lambda j: h2b[:, j * 128:(j + 1) * 128],
                       lambda j, tslot=tslot: h2T[:, tslot, j, :],
                       DC, extra_vector_wait=xvw)

            def _sth(e, c=c, tslot=tslot, wv=C.val("vev")):
                e.wait_ge(sem["vev"], wv)
                e.dma_start(h2t_d[:, :, c * 128:(c + 1) * 128]
                            .rearrange("a p t -> p a t"),
                            h2T[:, tslot, :, :]).then_inc(sem[f"so{tslot}"], 16)
            ph.em("sync", _sth)
            C.add(f"so{tslot}", 16)
            h2T_war[tslot] = C.val(f"so{tslot}")

            def _rmm(e, tslot=tslot, wv=C.val("vev"), ws=RW_SWG, wsc=C.val("ssc")):
                e.wait_ge(sem["swg"], ws)
                e.wait_ge(sem["vev"], wv)
                e.wait_ge(sem["ssc"], wsc)
                for j in range(DC):
                    mm = e.matmul(pR[:], h2T[:, tslot, j, :], rwb[:, j, :],
                                  start=(j == 0), stop=(j == DC - 1))
                mm.then_inc(sem["tmm"], 1)
            ph.em("tensor", _rmm)
            C.add("tmm")

            def _sm(e, wt=C.val("tmm")):
                e.wait_ge(sem["tmm"], wt)
                e.reduce_max(stt2[:, 16:17], pR[:], axis=AX.X, negate=True
                             ).then_inc(sem["vev"], 1)
            ph.em("vector", _sm)
            C.add("vev")

            def _ex(e, wv=C.val("vev")):
                e.wait_ge(sem["vev"], wv)
                e.activation(prb[:], pR[:], AF.Exp, bias=stt2[:, 16:17],
                             accum_out=stt2[:, 17:18]).then_inc(sem["ssc"], 1)
            ph.em("scalar", _ex)
            C.add("ssc")

            def _top2(e, c=c, ws=C.val("ssc"), b=C.val("vev")):
                e.wait_ge(sem["ssc"], ws)

                def bar(ins, i):
                    ins.then_inc(sem["vev"], 1)
                    e.wait_ge(sem["vev"], b + i)
                bar(e.reciprocal(stt2[:, 18:19], stt2[:, 17:18]), 1)
                bar(e.tensor_scalar(prb[:], prb[:], stt2[:, 18:19], None, ALU.mult), 2)
                bar(e.reduce_max(stt2[:, 19:20], prb[:], axis=AX.X), 3)
                bar(e.tensor_scalar(wtb[:], prb[:], stt2[:, 19:20], None, ALU.is_lt), 4)
                bar(e.tensor_tensor(wtb[:], wtb[:], prb[:], ALU.mult), 5)
                bar(e.reduce_max(stt2[:, 20:21], wtb[:], axis=AX.X), 6)
                bar(e.tensor_scalar(wtb[:], prb[:], stt2[:, 20:21], None, ALU.is_ge), 7)
                bar(e.tensor_tensor(wtb[:], wtb[:], prb[:], ALU.mult), 8)
                bar(e.reduce_sum(stt2[:, 21:22], wtb[:], axis=AX.X), 9)
                bar(e.reciprocal(stt2[:, 22:23], stt2[:, 21:22]), 10)
                tq, tc = c // TQC, c % TQC
                e.tensor_scalar(
                    bass.AP(wfull, tq * (NE * TQC) + tc,
                            [[NTQ * NE * TQC, 128], [TQC, NE], [1, 1]]),
                    wtb[:], stt2[:, 22:23], None, ALU.mult).then_inc(sem["vev"], 1)
            ph.em("vector", _top2)
            C.add("vev", 11)
        H2T_DONE = (C.val("so0"), C.val("so1"))
        ph.play(nc)
        st.close()

        GUSL = [(0, 4), (4, 4), (8, 4), (12, 4), (16, 4), (20, 4), (24, 2), (26, 2)]
        NGU = len(GUSL)
        GR_SL = [n * 2 * 2 for _, n in GUSL]
        GR_PER_E = sum(GR_SL)
        GGR_PER_E = GR_PER_E // 2

        ph = Phase("F2")
        st = contextlib.ExitStack()
        h2Tq = st.enter_context(nc.sbuf_tensor("f2_h2Tq", [128, DC, TQT], BF16))
        aTb = st.enter_context(nc.sbuf_tensor("f2_aTb", [128, MC, TQT], BF16))
        yac = st.enter_context(nc.sbuf_tensor("f2_yac", [128, TQC, D], F32))
        wgub = st.enter_context(nc.sbuf_tensor("f2_wgub", [128, 2, 2, DC, 512], BF16))
        wdb = st.enter_context(nc.sbuf_tensor("f2_wdb", [128, 2, 4, D], BF16))
        sib = st.enter_context(nc.sbuf_tensor("f2_sib", [128, 4, 512], BF16))
        wsel = st.enter_context(nc.sbuf_tensor("f2_wsel", [128, 8], F32))
        xfb = st.enter_context(nc.sbuf_tensor("f2_xfb", [128, 2, D], F32))
        yfb = st.enter_context(nc.sbuf_tensor("f2_yfb", [128, 2, D], F32))
        ofb = st.enter_context(nc.sbuf_tensor("f2_ofb", [128, 2, D], F32))
        pF = st.enter_context(nc.psum_tensor("f2_pF", [128, 8, 512], F32))

        def _z0(e, tgt=C.val("vz") + 1):
            e.memset(yac[:], 0.0).then_inc(sem["vz"], 1)
            e.wait_ge(sem["vz"], tgt)
            e.sem_inc(sem["vmoe"], 4)
        ph.em("vector", _z0)
        C.add("vz")
        C.add("vmoe", 4)

        def _sb(e):
            e.sem_inc(sem["ssil"], 1)
        ph.em("scalar", _sb)
        C.add("ssil", 1)

        TMO_0 = C.val("tmoe")
        TD_0 = C.val("tdown")
        SIL_0 = C.val("ssil")
        VMO_0 = C.val("vmoe")
        VDE_0 = C.val("vdev")
        TSL_0 = C.val("tslc")
        SWG_0 = C.val("swg")

        H2Q_AUX0 = C.val("shq")

        def emit_ldh(tq):
            def _ldh(e, tq=tq, wo=H2T_DONE, wt=TMO_0 + tq * NE * GR_PER_E):
                e.wait_ge(sem["so0"], wo[0])
                e.wait_ge(sem["so1"], wo[1])
                if tq > 0:
                    e.wait_ge(sem["tmoe"], wt)
                e.dma_start(h2Tq[:],
                            h2t_d[:, :, tq * TQT:(tq + 1) * TQT]
                            .rearrange("a p t -> p a t")).then_inc(sem["shq"], 16)
            ph.em("sync", _ldh)
            C.add("shq", 16)

        for tq in range(NTQ):
            for e_i in range(NE):
                for si, (mc0, nmc) in enumerate(GUSL):
                    gsl = (tq * NE + e_i) * NGU + si

                    def _lw(e, e_i=e_i, mc0=mc0, nmc=nmc, gsl=gsl, si=si, tq=tq):
                        if si == 0 and e_i == 0:
                            e.wait_ge(sem["shq"], H2Q_AUX0 + 16 * (tq + 1))
                        if gsl >= 2:
                            done = sum(GR_SL[k % NGU] for k in range(gsl - 1))
                            e.wait_ge(sem["tmoe"], TMO_0 + done)
                        sg = f"sgu{si % 2}"
                        sl = gsl % 2
                        e.dma_start(wgub[:, sl, 0, :, 0:nmc * 128],
                                    wg_d[e_i, :, mc0 * 128:(mc0 + nmc) * 128]
                                    .rearrange("(c p) m -> p c m", p=128)
                                    ).then_inc(sem[sg], 16)
                        e.dma_start(wgub[:, sl, 1, :, 0:nmc * 128],
                                    wu_d[e_i, :, mc0 * 128:(mc0 + nmc) * 128]
                                    .rearrange("(c p) m -> p c m", p=128)
                                    ).then_inc(sem[sg], 16)
                    ph.em("gpsimd", _lw)
                    C.add(f"sgu{si % 2}", 32)
                for dg in range(2):
                    for si, (mc0, nmc) in enumerate(GUSL):
                        wsl = (tq * NE + e_i) * 16 + dg * NGU + si

                        def _lwd(e, e_i=e_i, mc0=mc0, nmc=nmc, wsl=wsl, si=si):
                            if wsl >= 2:
                                prev = wsl - 2
                                ns = sum(1 for k in range(prev + 1)
                                         if (k % NGU) != NGU - 1)
                                nf = (prev + 1) - ns
                                if ns > 0:
                                    e.wait_ge(sem["tslc"], TSL_0 + ns)
                                if nf > 0:
                                    e.wait_ge(sem["tdown"], TD_0 + nf * 8)
                            e.dma_start(wdb[:, wsl % 2, 0:nmc, :],
                                        wd_d[e_i, mc0 * 128:(mc0 + nmc) * 128, :]
                                        .rearrange("(c p) d -> p c d", p=128)
                                        ).then_inc(sem[f"swd{si % 2}"], 16)
                        ph.em("gpsimd", _lwd)
                        C.add(f"swd{si % 2}", 16)

        NFLAT = NTQ * NE

        def _ten(e):
            with e.register("f2rg0") as rg0, e.register("f2rg1") as rg1, \
                 e.register("f2rs") as rs, \
                 e.register("f2rv") as rv, e.register("f2rvd") as rvd, \
                 e.register("f2rw0") as rwd0, e.register("f2rw1") as rwd1, \
                 e.register("f2rvm") as rvm:
                e.reg_mov(rg0, 32)
                e.reg_mov(rg1, 32)
                e.reg_mov(rs, SIL_0 - 1)
                e.reg_mov(rv, VMO_0 - 1)
                e.reg_mov(rvd, VDE_0)
                e.reg_mov(rwd0, 16)
                e.reg_mov(rwd1, 16)
                e.reg_mov(rvm, VMO_0 + GGR_PER_E)
                with e.Fori(0, NFLAT):
                    e.wait_ge(sem["vdev"], rvd)
                    gidx = 0
                    for si, (mc0, nmc) in enumerate(GUSL):
                        rg = rg0 if si % 2 == 0 else rg1
                        e.wait_ge(sem[f"sgu{si % 2}"], rg)
                        e.reg_add(rg, rg, 32)
                        for ml in range(nmc):
                            for tg in range(2):
                                for gu in range(2):
                                    bank = gu * 2 + (gidx % 2)
                                    if gu == 0:
                                        e.wait_ge(sem["ssil"], rs)
                                        e.reg_add(rs, rs, 1)
                                    else:
                                        e.wait_ge(sem["vmoe"], rv)
                                        e.reg_add(rv, rv, 1)
                                    for j in range(DC):
                                        mm = e.matmul(
                                            pF[:, bank, :],
                                            wgub[:, si % 2, gu, j,
                                                 ml * 128:(ml + 1) * 128],
                                            h2Tq[:, j, tg * 512:(tg + 1) * 512],
                                            start=(j == 0), stop=(j == DC - 1),
                                            skip_group_check=True)
                                    mm.then_inc(sem["tmoe"], 1)
                                gidx += 1
                    e.wait_ge(sem["vmoe"], rvm)
                    e.reg_add(rvm, rvm, GGR_PER_E)
                    for dg in range(2):
                        if dg == 1:
                            e.wait_ge(sem["vdev"], rvd)
                        for si, (mc0, nmc) in enumerate(GUSL):
                            rwd = rwd0 if si % 2 == 0 else rwd1
                            e.wait_ge(sem[f"swd{si % 2}"], rwd)
                            e.reg_add(rwd, rwd, 16)
                            for ml in range(nmc):
                                mc = mc0 + ml
                                for tc in range(TQC):
                                    mm = e.matmul(
                                        pF[:, tc, :],
                                        aTb[:, mc, tc * 128:(tc + 1) * 128],
                                        wdb[:, (dg * NGU + si) % 2, ml,
                                            dg * 512:(dg + 1) * 512],
                                        start=(mc == 0), stop=(mc == MC - 1),
                                        skip_group_check=True)
                                    if mc == MC - 1:
                                        mm.then_inc(sem["tdown"], 1)
                                    elif ml == nmc - 1 and tc == TQC - 1:
                                        mm.then_inc(sem["tslc"], 1)
                        if dg == 0:
                            e.reg_add(rvd, rvd, 8)
                    e.reg_add(rvd, rvd, 8)
        ph.em("tensor", _ten)
        C.add("tmoe", NFLAT * GR_PER_E)
        C.add("tdown", NFLAT * 16)
        C.add("tslc", NFLAT * 14)

        def _sca(e):
            with e.register("f2srt") as rt, e.register("f2srv") as rv2:
                e.reg_mov(rt, TMO_0 + 1)
                e.reg_mov(rv2, VMO_0 - 3)
                with e.Fori(0, NFLAT):
                    g = 0
                    for si, (mc0, nmc) in enumerate(GUSL):
                        for ml in range(nmc):
                            for tg in range(2):
                                bank = g % 2
                                e.wait_ge(sem["tmoe"], rt)
                                e.reg_add(rt, rt, 2)
                                e.wait_ge(sem["vmoe"], rv2)
                                e.reg_add(rv2, rv2, 1)
                                e.activation(sib[:, g % 4, :], pF[:, bank, :],
                                             AF.Silu).then_inc(sem["ssil"], 1)
                                g += 1
        ph.em("scalar", _sca)
        C.add("ssil", NFLAT * GGR_PER_E)

        ofb_war = [0, 0]
        for tq in range(NTQ):
            emit_ldh(tq)
            SIL_T = SIL_0 + tq * NE * GGR_PER_E
            TMO_T = TMO_0 + tq * NE * GR_PER_E
            TD_T = TD_0 + tq * NE * 16

            def _vec(e, tq=tq, SIL_T=SIL_T, TMO_T=TMO_T, TD_T=TD_T):
                with e.register(f"f2vs{tq}") as rsl, e.register(f"f2vt{tq}") as rt, \
                     e.register(f"f2vd{tq}") as rtd, e.register(f"f2vw{tq}") as rwo:
                    e.reg_mov(rsl, SIL_T + 1)
                    e.reg_mov(rt, TMO_T + 2)
                    e.reg_mov(rtd, TD_T + 1)
                    e.reg_mov(rwo, tq * (NE * TQC))
                    with e.Fori(0, NE):
                        e.tensor_copy(wsel[:, 0:TQC],
                                      bass.AP(wfull, rwo,
                                              [[NTQ * NE * TQC, 128], [1, TQC]]))
                        g = 0
                        for si, (mc0, nmc) in enumerate(GUSL):
                            for ml in range(nmc):
                                mc = mc0 + ml
                                for tg in range(2):
                                    bank = g % 2
                                    e.wait_ge(sem["ssil"], rsl)
                                    e.reg_add(rsl, rsl, 1)
                                    e.wait_ge(sem["tmoe"], rt)
                                    e.reg_add(rt, rt, 2)
                                    e.tensor_tensor(
                                        aTb[:, mc, tg * 512:(tg + 1) * 512],
                                        sib[:, g % 4, :], pF[:, 2 + bank, :],
                                        ALU.mult).then_inc(sem["vmoe"], 1)
                                    g += 1
                        for dg in range(2):
                            for tc in range(TQC):
                                e.wait_ge(sem["tdown"], rtd)
                                e.reg_add(rtd, rtd, 1)
                                e.scalar_tensor_tensor(
                                    yac[:, tc, dg * 512:(dg + 1) * 512],
                                    pF[:, tc, :],
                                    wsel[:, tc:tc + 1],
                                    yac[:, tc, dg * 512:(dg + 1) * 512],
                                    ALU.mult, ALU.add).then_inc(sem["vdev"], 1)
                        e.reg_add(rwo, rwo, TQC)
            ph.em("vector", _vec)
            C.add("vmoe", NE * GGR_PER_E)
            C.add("vdev", NE * 16)

            for tc in range(TQC):
                c = tq * TQC + tc
                cs = tc % 2

                sxs = f"sx{cs}"

                def _ldf(e, c=c, cs=cs, wo=ofb_war[cs], sxs=sxs):
                    if wo > 0:
                        e.wait_ge(sem[f"so{cs}"], wo)
                    e.dma_start(xfb[:, cs, :], x1_d[c * 128:(c + 1) * 128, :]
                                ).then_inc(sem[sxs], 16)
                    e.dma_start(yfb[:, cs, :], y1_d[c * 128:(c + 1) * 128, :]
                                ).then_inc(sem[sxs], 16)
                ph.em("sync", _ldf)
                C.add(sxs, 32)

                def _fin(e, tc=tc, cs=cs, wx=C.val(sxs), zt=C.val("vz") + 1,
                         sxs=sxs, wd=C.val("vdev"), bar=C.val("vev") + 1):
                    e.wait_ge(sem[sxs], wx)
                    e.wait_ge(sem["vdev"], wd)
                    e.tensor_tensor(ofb[:, cs, :], yfb[:, cs, :], yac[:, tc, :],
                                    ALU.add).then_inc(sem["vev"], 1)
                    e.wait_ge(sem["vev"], bar)
                    e.tensor_tensor(ofb[:, cs, :], ofb[:, cs, :], xfb[:, cs, :],
                                    ALU.add).then_inc(sem["vev"], 1)
                    e.memset(yac[:, tc, :], 0.0).then_inc(sem["vz"], 1)
                    e.wait_ge(sem["vz"], zt)
                ph.em("vector", _fin)
                C.add("vz")
                C.add("vev", 2)

                def _sto(e, c=c, cs=cs, wv=C.val("vev")):
                    e.wait_ge(sem["vev"], wv)
                    e.dma_start(out_d[c * 128:(c + 1) * 128, :],
                                ofb[:, cs, :]).then_inc(sem[f"so{cs}"], 16)
                ph.em("sync", _sto)
                C.add(f"so{cs}", 16)
                ofb_war[cs] = C.val(f"so{cs}")

        def _done(e, w0=C.val("so0"), w1=C.val("so1"), w2=C.val("sout")):
            e.wait_ge(sem["so0"], w0)
            e.wait_ge(sem["so1"], w1)
            e.wait_ge(sem["sout"], w2)
        ph.em("vector", _done)
        ph.play(nc)
        st.close()

    n_instr = sum(1 for _ in nc.all_instructions())
    print(f"[kernel] built S={S}: {n_instr} instructions")
    return nc


def _consts(S):
    identb = np.eye(128, dtype=ml_dtypes.bfloat16)
    identf = np.eye(128, dtype=np.float32)
    onesb = np.ones((128, 128), dtype=ml_dtypes.bfloat16)
    inv_freq = (1.0 / (10000.0 ** (np.arange(0, HD, 2, dtype=np.float64) / HD)))
    t = np.arange(S, dtype=np.float64)
    ang = np.outer(t, inv_freq)
    cos = np.cos(ang).astype(np.float32)
    sin = np.sin(ang).astype(np.float32)
    cs = np.stack([cos, sin])[:, :, None, :]
    cs = np.ascontiguousarray(np.broadcast_to(cs, (2, S, NH, 32)).astype(np.float32))
    kk = np.arange(512)[:, None]
    qq = np.arange(512)[None, :]
    mask = np.ascontiguousarray(
        (kk <= qq).reshape(4, 128, 512).astype(ml_dtypes.bfloat16))
    return identb, identf, onesb, cs, mask


def make_in_maps(inputs, S=4096, n=8):
    identb, identf, onesb, cs, mask = _consts(S)
    f32 = lambda a: np.ascontiguousarray(np.asarray(a, dtype=np.float32))
    rep = lambda a: np.ascontiguousarray(
        np.broadcast_to(np.asarray(a, np.float32).reshape(1, D), (128, D)))
    xs = f32(inputs["L0_attn_x"])
    common = {
        "wq": f32(inputs["L0_attn_Wq"]), "wk": f32(inputs["L0_attn_Wk"]),
        "wv": f32(inputs["L0_attn_Wv"]), "wo": f32(inputs["L0_attn_Wo"]),
        "ln1": rep(inputs["L0_ffn_ln1_w"]), "ln2": rep(inputs["L0_ffn_ln2_w"]),
        "fwq": np.ascontiguousarray(
            f32(inputs["L0_ffn_wq"]).reshape(D, 16, 64)[:, PERM_Q, :].reshape(D, D)),
        "fwk": f32(inputs["L0_ffn_wk"]),
        "fwv": f32(inputs["L0_ffn_wv"]), "fwo": f32(inputs["L0_ffn_wo"]),
        "rw": f32(inputs["L0_ffn_router_w"]),
        "wgate": f32(inputs["L0_ffn_w_gate"]), "wup": f32(inputs["L0_ffn_w_up"]),
        "wdown": f32(inputs["L0_ffn_w_down"]),
        "identb": identb, "identf": identf, "onesb": onesb,
        "cossin": cs, "mask01": mask,
    }
    return [dict(common, x=xs[c][:S]) for c in range(n)]


def kernel(**inputs):
    S = 4096
    n = 8
    nc = build(S=S)
    in_maps = make_in_maps(inputs, S=S, n=n)
    res = run_bass_kernel_spmd(nc, in_maps, core_ids=list(range(n)))
    return np.stack([res.results[c]["out"] for c in range(n)], axis=0)


# version 04
# speedup vs baseline: 1.0313x; 1.0313x over previous
import contextlib
import numpy as np
import ml_dtypes
import concourse.bass as bass
import concourse.mybir as mybir
from concourse.bass_utils import run_bass_kernel_spmd

F32 = mybir.dt.float32
BF16 = mybir.dt.bfloat16
AF = mybir.ActivationFunctionType
ALU = mybir.AluOpType
AX = mybir.AxisListType

D = 1024
DC = D // 128
NH = 16
NKV = 8
HD = 64
M = 3584
MC = M // 128
NE = 8
EPS_LN = 1e-5
EPS_RMS = 1e-6

PERM_Q = [0, 2, 1, 3, 4, 6, 5, 7, 8, 10, 9, 11, 12, 14, 13, 15]
POS_OF = {h: p for p, h in enumerate(PERM_Q)}

ENGINES = ("sync", "gpsimd", "tensor", "scalar", "vector")


class Phase:
    def __init__(self, name):
        self.name = name
        self.ops = {k: [] for k in ENGINES}

    def em(self, eng, fn):
        self.ops[eng].append(fn)

    def play(self, nc):
        with nc.Block() as b:
            for eng in ENGINES:
                fns = self.ops[eng]
                if fns:
                    getattr(b, eng)(lambda e, fns=fns: [f(e) for f in fns])


class Ctr:
    def __init__(self):
        self.v = {}

    def add(self, k, n=1):
        self.v[k] = self.v.get(k, 0) + n
        return self.v[k]

    def val(self, k):
        return self.v.get(k, 0)


def build(S=4096, debug=False):
    NCH = S // 128
    NSUP = S // 512
    NTQ = max(1, S // 1024)
    TQT = S // NTQ
    TQC = TQT // 128

    nc = bass.Bass("TRN2", target_bir_lowering=False)

    x_d = nc.dram_tensor("x", [S, D], F32, kind="ExternalInput").ap()
    wq_d = nc.dram_tensor("wq", [D, D], F32, kind="ExternalInput").ap()
    wk_d = nc.dram_tensor("wk", [D, D], F32, kind="ExternalInput").ap()
    wv_d = nc.dram_tensor("wv", [D, D], F32, kind="ExternalInput").ap()
    wo_d = nc.dram_tensor("wo", [D, D], F32, kind="ExternalInput").ap()
    ln1_d = nc.dram_tensor("ln1", [128, D], F32, kind="ExternalInput").ap()
    ln2_d = nc.dram_tensor("ln2", [128, D], F32, kind="ExternalInput").ap()
    fwq_d = nc.dram_tensor("fwq", [D, D], F32, kind="ExternalInput").ap()
    fwk_d = nc.dram_tensor("fwk", [D, 512], F32, kind="ExternalInput").ap()
    fwv_d = nc.dram_tensor("fwv", [D, 512], F32, kind="ExternalInput").ap()
    fwo_d = nc.dram_tensor("fwo", [D, D], F32, kind="ExternalInput").ap()
    rw_d = nc.dram_tensor("rw", [D, NE], F32, kind="ExternalInput").ap()
    wg_d = nc.dram_tensor("wgate", [NE, D, M], F32, kind="ExternalInput").ap()
    wu_d = nc.dram_tensor("wup", [NE, D, M], F32, kind="ExternalInput").ap()
    wd_d = nc.dram_tensor("wdown", [NE, M, D], F32, kind="ExternalInput").ap()
    identb_d = nc.dram_tensor("identb", [128, 128], BF16, kind="ExternalInput").ap()
    identf_d = nc.dram_tensor("identf", [128, 128], F32, kind="ExternalInput").ap()
    onesb_d = nc.dram_tensor("onesb", [128, 128], BF16, kind="ExternalInput").ap()
    cs_d = nc.dram_tensor("cossin", [2, S, NH, 32], F32, kind="ExternalInput").ap()
    mask_d = nc.dram_tensor("mask01", [4, 128, 512], BF16, kind="ExternalInput").ap()
    out_d = nc.dram_tensor("out", [S, D], F32, kind="ExternalOutput").ap()

    skind = "ExternalOutput" if debug else "Internal"
    x1_d = nc.dram_tensor("x1s", [S, D], F32, kind=skind).ap()
    y1_d = nc.dram_tensor("y1s", [S, D], F32, kind=skind).ap()
    aout_d = nc.dram_tensor("aouts", [NSUP, 128, NH // 2, 512], BF16,
                            kind="Internal").ap()
    h2t_d = nc.dram_tensor("h2ts", [DC, 128, S], BF16, kind=skind).ap()

    C = Ctr()
    es = contextlib.ExitStack()
    with es:
        sem = {}
        for name in ("sx0", "sx1", "sa0", "sa1", "shq", "sgu0", "sgu1",
                     "swd0", "swd1", "so0", "so1", "sout", "swg", "saux", "snc",
                     "tmm", "tmv", "tbb", "tslc", "vev", "vmk", "ssc",
                     "tmoe", "tdown", "ssil", "vmoe", "vdev", "vz"):
            sem[name] = es.enter_context(nc.semaphore(name))

        identb = es.enter_context(nc.sbuf_tensor("identb_s", [128, 128], BF16))
        identf = es.enter_context(nc.sbuf_tensor("identf_s", [128, 128], F32))
        onesb = es.enter_context(nc.sbuf_tensor("onesb_s", [128, 128], BF16))
        maskb = es.enter_context(nc.sbuf_tensor("maskb", [128, 4, 512], BF16))
        wfull = es.enter_context(nc.sbuf_tensor("wfull", [128, NTQ * NE * TQC], F32))

        ph = Phase("P0")

        def _p0(e):
            e.dma_start(identb[:], identb_d).then_inc(sem["saux"], 16)
            e.dma_start(identf[:], identf_d).then_inc(sem["saux"], 16)
            e.dma_start(onesb[:], onesb_d).then_inc(sem["saux"], 16)
            e.dma_start(maskb[:], mask_d.rearrange("a p q -> p a q")
                        ).then_inc(sem["saux"], 16)
        ph.em("sync", _p0)
        C.add("saux", 64)

        def _p0w(e, w=C.val("saux")):
            e.wait_ge(sem["saux"], w)
        ph.em("vector", _p0w)
        ph.play(nc)

        def transposes(ph, pT, src_ap_of, dst_ap_of, n,
                       extra_tensor_wait=None, extra_vector_wait=None,
                       data_wait=None):
            if data_wait is None:
                data_wait = ("vev", C.val("vev"))
            for j in range(n):
                pt = pT[j % 2]

                def _tr(e, pt=pt, j=j, dw=data_wait,
                        wwar=(C.val("vev") - 1 if j >= 2 else 0),
                        src=src_ap_of(j),
                        xw=extra_tensor_wait if j == 0 else None):
                    if xw is not None:
                        xw(e)
                    if j == 0 and dw[1] > 0:
                        e.wait_ge(sem[dw[0]], dw[1])
                    if wwar > 0:
                        e.wait_ge(sem["vev"], wwar)
                    e.transpose(pt[:], src, identb[:]).then_inc(sem["tmm"], 1)
                ph.em("tensor", _tr)
                C.add("tmm")

                def _ev(e, pt=pt, wt=C.val("tmm"), dst=dst_ap_of(j),
                        xw=extra_vector_wait if j == 0 else None):
                    if xw is not None:
                        xw(e)
                    e.wait_ge(sem["tmm"], wt)
                    e.tensor_copy(dst, pt[:]).then_inc(sem["vev"], 1)
                ph.em("vector", _ev)
                C.add("vev")

        def emit_attention(name, src_x_d, nqh, kv_of, head0, weights_mode,
                           ln_rms=False, ln1r=None):
            ph = Phase(name)
            st = contextlib.ExitStack()
            QCOL = nqh * HD
            QHG = nqh // 2
            QPAD = (nqh != NKV)
            NQS = 16 if QPAD else QHG
            PH_SOUT0 = (C.val("so0"), C.val("so1"), C.val("sout"))

            xsb = st.enter_context(nc.sbuf_tensor(name + "_xsb", [128, 2, D], F32))
            csb = st.enter_context(nc.sbuf_tensor(name + "_csb", [128, 2, 2, 16, 32], F32))
            xcb = st.enter_context(nc.sbuf_tensor(name + "_xcb", [128, 2, D], F32))
            sqd = st.enter_context(nc.sbuf_tensor(name + "_sqd", [128, D], F32))
            hsc = st.enter_context(nc.sbuf_tensor(name + "_hsc", [128, D], F32))
            nsq = st.enter_context(nc.sbuf_tensor(name + "_nsq", [128, 1536], F32))
            nst = st.enter_context(nc.sbuf_tensor(name + "_nst", [128, 32], F32))
            stt = st.enter_context(nc.sbuf_tensor(name + "_stt", [128, 16], F32))
            hb = st.enter_context(nc.sbuf_tensor(name + "_hb", [128, D], BF16))
            hTb = st.enter_context(nc.sbuf_tensor(name + "_hTb", [128, DC, 128], BF16))
            qrb = st.enter_context(nc.sbuf_tensor(name + "_qrb", [128, QCOL], BF16))
            krb = st.enter_context(nc.sbuf_tensor(name + "_krb", [128, 512], BF16))
            rtmp = st.enter_context(nc.sbuf_tensor(name + "_rtmp", [128, 4, 16, 32], F32))
            qTb = st.enter_context(nc.sbuf_tensor(name + "_qTb", [128, NQS, 512], BF16))
            KTb = st.enter_context(nc.sbuf_tensor(name + "_KTb", [128, 4, S], BF16))
            Vb = st.enter_context(nc.sbuf_tensor(name + "_Vb", [128, NCH, NKV, 65], BF16))
            expb = st.enter_context(nc.sbuf_tensor(name + "_expb", [128, 4, 512], BF16))
            aoutb = st.enter_context(nc.sbuf_tensor(name + "_aoutb", [128, QHG, 512], BF16))
            rbf = st.enter_context(nc.sbuf_tensor(name + "_rbf", [128, 512], F32))
            rib = st.enter_context(nc.sbuf_tensor(name + "_rib", [128, 512], BF16))
            qsqm = st.enter_context(nc.sbuf_tensor(name + "_qsqm", [128, nqh], F32))
            ksqm = st.enter_context(nc.sbuf_tensor(name + "_ksqm", [128, NKV], F32))
            ksqe = st.enter_context(nc.sbuf_tensor(name + "_ksqe", [128, 16], F32))
            qmx = st.enter_context(nc.sbuf_tensor(name + "_qmx", [16, 1], F32))
            kmx = st.enter_context(nc.sbuf_tensor(name + "_kmx", [16, 1], F32))
            ccol = st.enter_context(nc.sbuf_tensor(name + "_ccol", [16, 1], F32))
            ccs = st.enter_context(nc.sbuf_tensor(name + "_ccs", [16, 1], F32))
            crow = st.enter_context(nc.sbuf_tensor(name + "_crow", [1, 16], BF16))
            negCb = st.enter_context(nc.sbuf_tensor(name + "_negC", [128, 16], F32))
            if weights_mode == "transposed":
                wqT = st.enter_context(nc.sbuf_tensor(name + "_wqT", [128, DC, 512], BF16))
                wkT = st.enter_context(nc.sbuf_tensor(name + "_wkT", [128, DC, 512], BF16))
                wvT = st.enter_context(nc.sbuf_tensor(name + "_wvT", [128, DC, 512], BF16))
                wtmp = st.enter_context(nc.sbuf_tensor(name + "_wtmp", [128, 4, D], BF16))
            else:
                fwqb = st.enter_context(nc.sbuf_tensor(name + "_fwq", [128, DC, D], BF16))
                fwkb = st.enter_context(nc.sbuf_tensor(name + "_fwk", [128, DC, 512], BF16))
                fwvb = st.enter_context(nc.sbuf_tensor(name + "_fwv", [128, DC, 512], BF16))
            pQV = st.enter_context(nc.psum_tensor(name + "_pQV", [128, 512], F32))
            pK = st.enter_context(nc.psum_tensor(name + "_pK", [128, 512], F32))
            pT0 = st.enter_context(nc.psum_tensor(name + "_pT0", [128, 128], BF16))
            pT1 = st.enter_context(nc.psum_tensor(name + "_pT1", [128, 128], BF16))
            psT = st.enter_context(nc.psum_tensor(name + "_psT", [128, 2, 512], F32))
            poT = st.enter_context(nc.psum_tensor(name + "_poT", [128, 512], F32))
            pbc = st.enter_context(nc.psum_tensor(name + "_pbc", [128, 512], F32))
            pT = (pT0, pT1)

            def score_slot(kt):
                j = kt % 4
                if j == 0:
                    return psT[:, 0, :]
                if j == 1:
                    return psT[:, 1, :]
                if j == 2:
                    return pQV[:]
                return pK[:]

            if weights_mode == "transposed":
                for dst, src in ((wqT, wq_d), (wkT, wk_d), (wvT, wv_d)):
                    def _ld(e, src=src, o0=head0 * HD, wt=C.val("tmm")):
                        if wt > 0:
                            e.wait_ge(sem["tmm"], wt)
                        e.dma_start(wtmp[:], src[o0:o0 + 512, :]
                                    .rearrange("(c p) i -> p c i", p=128)
                                    ).then_inc(sem["swg"], 16)
                    ph.em("gpsimd", _ld)
                    C.add("swg", 16)
                    wsw = C.val("swg")
                    for oc in range(4):
                        def xw(e, wsw=wsw):
                            e.wait_ge(sem["swg"], wsw)
                        transposes(
                            ph, pT,
                            lambda j, oc=oc: wtmp[:, oc, j * 128:(j + 1) * 128],
                            lambda j, oc=oc, dst=dst: dst[:, j, oc * 128:(oc + 1) * 128],
                            DC, extra_tensor_wait=xw)
            else:
                for dst, src in ((fwqb, fwq_d), (fwkb, fwk_d), (fwvb, fwv_d)):
                    def _ld(e, dst=dst, src=src):
                        e.dma_start(dst[:], src.rearrange("(c p) o -> p c o", p=128)
                                    ).then_inc(sem["swg"], 16)
                    ph.em("gpsimd", _ld)
                    C.add("swg", 16)

            def _vones(e):
                e.memset(Vb[:], 1.0).then_inc(sem["vev"], 1)
                if QPAD:
                    e.memset(qTb[:], 0.0).then_inc(sem["vev"], 1)
            ph.em("vector", _vones)
            C.add("vev", 2 if QPAD else 1)
            W_SWG = C.val("swg")

            xsb_free = [0, 0]
            csb_free = [0, 0]
            stt_war = [0, 0]
            hb_war = [0]
            hsc_war = [0]

            def getq(j, oh):
                return (wqT if weights_mode == "transposed" else fwqb)[
                    :, j, oh * 512:(oh + 1) * 512]

            def getk(j):
                return (wkT if weights_mode == "transposed" else fwkb)[:, j, :]

            def getv(j):
                return (wvT if weights_mode == "transposed" else fwvb)[:, j, :]

            def newton(e, stt):
                e.tensor_tensor(stt[:, 6:7], stt[:, 5:6], stt[:, 5:6], ALU.mult)
                e.tensor_tensor(stt[:, 6:7], stt[:, 6:7], stt[:, 3:4], ALU.mult)
                e.tensor_scalar(stt[:, 6:7], stt[:, 6:7], -0.5, 1.5, ALU.mult, ALU.add)
                e.tensor_tensor(stt[:, 6:7], stt[:, 6:7], stt[:, 5:6], ALU.mult)

            def emit_A1_chunk(s, c4):
                c = s * 4 + c4
                slot = c % 2
                po = 8 * slot
                sxs = f"sx{slot}"

                def _ldx(e, c=c, slot=slot, wx=xsb_free[slot], wc=csb_free[slot],
                         sxs=sxs):
                    if wx > 0:
                        e.wait_ge(sem["ssc"], wx)
                    e.dma_start(xsb[:, slot, :], src_x_d[c * 128:(c + 1) * 128, :]
                                ).then_inc(sem[sxs], 16)
                    if wc > 0:
                        e.wait_ge(sem["vev"], wc)
                    e.dma_start(csb[:, slot, :, 0:nqh, :],
                                cs_d[:, c * 128:(c + 1) * 128, head0:head0 + nqh, :]
                                .rearrange("a p h f -> p a h f")
                                ).then_inc(sem[sxs], 16)
                ph.em("sync", _ldx)
                C.add(sxs, 32)

                def _red(e, slot=slot, po=po, wx=C.val(sxs), ws=stt_war[slot],
                         sxs=sxs):
                    e.wait_ge(sem[sxs], wx)
                    if ws > 0:
                        e.wait_ge(sem["ssc"], ws)
                    e.reduce_sum(stt[:, po:po + 1], xsb[:, slot, :],
                                 axis=AX.X).then_inc(sem["vev"], 1)
                ph.em("vector", _red)
                C.add("vev")

                def _chain(e, slot=slot, po=po, wv=C.val("vev"), wt=hb_war[0],
                           wh=hsc_war[0], b1=C.val("ssc") + 1, rms=ln_rms):
                    e.wait_ge(sem["vev"], wv)

                    def s_(i):
                        return stt[:, po + i:po + i + 1]
                    xc = xcb[:, slot, :]
                    e.activation(s_(1), s_(0), AF.Copy,
                                 scale=-1.0 / D).then_inc(sem["ssc"], 1)
                    e.wait_ge(sem["ssc"], b1)
                    e.add(xc, xsb[:, slot, :], s_(1))
                    e.activation(sqd[:], xc, AF.Square, accum_out=s_(2))
                    e.activation(s_(3), s_(2), AF.Copy, scale=1.0 / D, bias=EPS_LN)
                    e.activation(s_(4), s_(3), AF.Ln)
                    e.activation(s_(5), s_(4), AF.Exp,
                                 scale=-0.5).then_inc(sem["ssc"], 1)
                    e.wait_ge(sem["ssc"], b1 + 1)
                    e.mul(s_(6), s_(5), s_(5))
                    e.mul(s_(6), s_(6), s_(3))
                    e.activation(s_(6), s_(6), AF.Copy, scale=-0.5, bias=1.5)
                    e.mul(s_(6), s_(6), s_(5)).then_inc(sem["ssc"], 1)
                    e.wait_ge(sem["ssc"], b1 + 2)
                    if not rms:
                        if wt > 0:
                            e.wait_ge(sem["tmm"], wt)
                        e.mul(hb[:], xc, s_(6)).then_inc(sem["ssc"], 1)
                    else:
                        e.mul(xc, xc, s_(6))
                        e.activation(sqd[:], xc, AF.Square, accum_out=s_(2))
                        e.activation(s_(3), s_(2), AF.Copy, scale=1.0 / D,
                                     bias=EPS_RMS)
                        e.activation(s_(4), s_(3), AF.Ln)
                        e.activation(s_(5), s_(4), AF.Exp,
                                     scale=-0.5).then_inc(sem["ssc"], 1)
                        e.wait_ge(sem["ssc"], b1 + 3)
                        e.mul(s_(6), s_(5), s_(5))
                        e.mul(s_(6), s_(6), s_(3))
                        e.activation(s_(6), s_(6), AF.Copy, scale=-0.5, bias=1.5)
                        e.mul(s_(6), s_(6), s_(5)).then_inc(sem["ssc"], 1)
                        e.wait_ge(sem["ssc"], b1 + 4)
                        if wh > 0:
                            e.wait_ge(sem["vev"], wh)
                        e.mul(hsc[:], xc, s_(6)).then_inc(sem["ssc"], 1)
                ph.em("scalar", _chain)
                C.add("ssc", 4 if not ln_rms else 6)
                stt_war[slot] = C.val("ssc")
                xsb_free[slot] = C.val("ssc")

                if ln_rms:
                    def _hTT(e, ws=C.val("ssc"), wt=hb_war[0]):
                        e.wait_ge(sem["ssc"], ws)
                        if wt > 0:
                            e.wait_ge(sem["tmm"], wt)
                        e.tensor_tensor(hb[:], hsc[:], ln1r[:],
                                        ALU.mult).then_inc(sem["vev"], 1)
                    ph.em("vector", _hTT)
                    C.add("vev")
                    hsc_war[0] = C.val("vev")
                    hb_ready = ("vev", C.val("vev"))
                else:
                    hb_ready = ("ssc", C.val("ssc"))

                transposes(ph, pT,
                           lambda j: hb[:, j * 128:(j + 1) * 128],
                           lambda j: hTb[:, j, :], DC, data_wait=hb_ready)
                hb_war[0] = C.val("tmm")

                for oh in range(QCOL // 512):
                    def _qmm(e, oh=oh, wv=C.val("vev"), ws=W_SWG, wsc=C.val("ssc")):
                        e.wait_ge(sem["swg"], ws)
                        e.wait_ge(sem["vev"], wv)
                        e.wait_ge(sem["ssc"], wsc)
                        for j in range(DC):
                            mm = e.matmul(pQV[:], hTb[:, j, :], getq(j, oh),
                                          start=(j == 0), stop=(j == DC - 1))
                        mm.then_inc(sem["tmm"], 1)
                    ph.em("tensor", _qmm)
                    C.add("tmm")

                    def _rope(e, oh=oh, wt=C.val("tmm"), slot=slot,
                              bar=C.val("vev") + 1):
                        e.wait_ge(sem["tmm"], wt)
                        h0 = oh * 8
                        q1 = bass.AP(pQV, 0, [[512, 128], [64, 8], [1, 32]])
                        q2 = bass.AP(pQV, 32, [[512, 128], [64, 8], [1, 32]])
                        cosa = csb[:, slot, 0, h0:h0 + 8, :]
                        sina = csb[:, slot, 1, h0:h0 + 8, :]
                        e.tensor_tensor(rtmp[:, 0, 0:8, :], q1, cosa, ALU.mult)
                        e.tensor_tensor(rtmp[:, 1, 0:8, :], q2, sina, ALU.mult)
                        e.tensor_tensor(rtmp[:, 2, 0:8, :], q2, cosa, ALU.mult)
                        e.tensor_tensor(rtmp[:, 3, 0:8, :], q1, sina,
                                        ALU.mult).then_inc(sem["vev"], 1)
                        e.wait_ge(sem["vev"], bar)
                        e.tensor_tensor(
                            bass.AP(qrb, oh * 512, [[QCOL, 128], [64, 8], [1, 32]]),
                            rtmp[:, 0, 0:8, :], rtmp[:, 1, 0:8, :], ALU.subtract)
                        e.tensor_tensor(
                            bass.AP(qrb, oh * 512 + 32, [[QCOL, 128], [64, 8], [1, 32]]),
                            rtmp[:, 2, 0:8, :], rtmp[:, 3, 0:8, :],
                            ALU.add).then_inc(sem["vev"], 1)
                    ph.em("vector", _rope)
                    C.add("vev", 2)

                def _kmm(e, ws=W_SWG):
                    for j in range(DC):
                        mm = e.matmul(pK[:], hTb[:, j, :], getk(j),
                                      start=(j == 0), stop=(j == DC - 1))
                    mm.then_inc(sem["tmm"], 1)
                ph.em("tensor", _kmm)
                C.add("tmm")

                def _ropek(e, wt=C.val("tmm"), slot=slot, bar=C.val("vev") + 1):
                    e.wait_ge(sem["tmm"], wt)
                    k1 = bass.AP(pK, 0, [[512, 128], [64, NKV], [1, 32]])
                    k2 = bass.AP(pK, 32, [[512, 128], [64, NKV], [1, 32]])
                    cosa = csb[:, slot, 0, 0:NKV, :]
                    sina = csb[:, slot, 1, 0:NKV, :]
                    e.tensor_tensor(rtmp[:, 0, 0:NKV, :], k1, cosa, ALU.mult)
                    e.tensor_tensor(rtmp[:, 1, 0:NKV, :], k2, sina, ALU.mult)
                    e.tensor_tensor(rtmp[:, 2, 0:NKV, :], k2, cosa, ALU.mult)
                    e.tensor_tensor(rtmp[:, 3, 0:NKV, :], k1, sina,
                                    ALU.mult).then_inc(sem["vev"], 1)
                    e.wait_ge(sem["vev"], bar)
                    e.tensor_tensor(bass.AP(krb, 0, [[512, 128], [64, NKV], [1, 32]]),
                                    rtmp[:, 0, 0:NKV, :], rtmp[:, 1, 0:NKV, :],
                                    ALU.subtract)
                    e.tensor_tensor(bass.AP(krb, 32, [[512, 128], [64, NKV], [1, 32]]),
                                    rtmp[:, 2, 0:NKV, :], rtmp[:, 3, 0:NKV, :],
                                    ALU.add).then_inc(sem["vev"], 1)
                ph.em("vector", _ropek)
                C.add("vev", 2)
                csb_free[slot] = C.val("vev")

                def _norms(e, c=c, c4=c4, w0=C.val("vev"), bar=C.val("vev") + 1):
                    e.wait_ge(sem["vev"], w0)
                    e.tensor_tensor(nsq[:, 0:QCOL], qrb[:], qrb[:], ALU.mult)
                    e.tensor_tensor(nsq[:, 1024:1536], krb[:], krb[:],
                                    ALU.mult).then_inc(sem["vev"], 1)
                    e.wait_ge(sem["vev"], bar)
                    e.tensor_reduce(nst[:, 0:nqh],
                                    bass.AP(nsq, 0, [[1536, 128], [64, nqh], [1, 64]]),
                                    axis=AX.X, op=ALU.add)
                    e.tensor_reduce(nst[:, 16:16 + NKV],
                                    bass.AP(nsq, 1024, [[1536, 128], [64, NKV], [1, 64]]),
                                    axis=AX.X, op=ALU.add).then_inc(sem["vev"], 1)
                    e.wait_ge(sem["vev"], bar + 1)
                    if c4 == 0:
                        e.tensor_copy(qsqm[:], nst[:, 0:nqh])
                    else:
                        e.tensor_tensor(qsqm[:], qsqm[:], nst[:, 0:nqh], ALU.max)
                    if c == 0:
                        e.tensor_copy(ksqm[:], nst[:, 16:16 + NKV]
                                      ).then_inc(sem["vev"], 1)
                    else:
                        e.tensor_tensor(ksqm[:], ksqm[:], nst[:, 16:16 + NKV],
                                        ALU.max).then_inc(sem["vev"], 1)
                ph.em("vector", _norms)
                C.add("vev", 3)

                if not QPAD:
                    transposes(ph, pT,
                               lambda j: qrb[:, j * 128:(j + 1) * 128],
                               lambda j, c4=c4: qTb[:, j, c4 * 128:(c4 + 1) * 128],
                               QCOL // 128)
                else:
                    first_wait = C.val("vev")
                    for j in range(QCOL // 128):
                        pt = pT[j % 2]

                        def _trq(e, pt=pt, j=j, w0=first_wait,
                                 wwar=(C.val("vev") - 1 if j >= 2 else 0),
                                 src=qrb[:, j * 128:(j + 1) * 128]):
                            if j == 0 and w0 > 0:
                                e.wait_ge(sem["vev"], w0)
                            if wwar > 0:
                                e.wait_ge(sem["vev"], wwar)
                            e.transpose(pt[:], src, identb[:]).then_inc(sem["tmm"], 1)
                        ph.em("tensor", _trq)
                        C.add("tmm")

                        def _evq(e, pt=pt, j=j, wt=C.val("tmm"), c4=c4):
                            e.wait_ge(sem["tmm"], wt)
                            hA = PERM_Q[2 * j]
                            hB = PERM_Q[2 * j + 1]
                            e.tensor_copy(qTb[0:64, hA, c4 * 128:(c4 + 1) * 128],
                                          pt[0:64, :])
                            e.tensor_copy(qTb[64:128, hB, c4 * 128:(c4 + 1) * 128],
                                          pt[64:128, :]).then_inc(sem["vev"], 1)
                        ph.em("vector", _evq)
                        C.add("vev")
                transposes(ph, pT,
                           lambda j: krb[:, j * 128:(j + 1) * 128],
                           lambda j, c=c: KTb[:, j, c * 128:(c + 1) * 128], 4)

                def _vmm(e, wv=C.val("vev"), ws=W_SWG):
                    e.wait_ge(sem["vev"], wv)
                    for j in range(DC):
                        mm = e.matmul(pQV[:], hTb[:, j, :], getv(j),
                                      start=(j == 0), stop=(j == DC - 1))
                    mm.then_inc(sem["tmm"], 1)
                ph.em("tensor", _vmm)
                C.add("tmm")

                def _vev(e, wt=C.val("tmm"), c=c):
                    e.wait_ge(sem["tmm"], wt)
                    e.tensor_copy(
                        bass.AP(Vb, c * NKV * 65,
                                [[NCH * NKV * 65, 128], [65, NKV], [1, 64]]),
                        pQV[:]).then_inc(sem["vev"], 1)
                ph.em("vector", _vev)
                C.add("vev")

            def emit_negC(s):
                def _t1(e, wv=C.val("vev")):
                    e.wait_ge(sem["vev"], wv)
                    e.transpose(bass.AP(pbc, 0, [[512, nqh], [1, 128]]),
                                qsqm[:], identf[:]).then_inc(sem["tmm"], 1)
                ph.em("tensor", _t1)
                C.add("tmm")

                def _r1(e, wt=C.val("tmm")):
                    e.wait_ge(sem["tmm"], wt)
                    e.reduce_max(qmx[0:nqh, :],
                                 bass.AP(pbc, 0, [[512, nqh], [1, 128]]),
                                 axis=AX.X).then_inc(sem["vev"], 1)
                ph.em("vector", _r1)
                C.add("vev")

                if QPAD:
                    def _kx(e, w0=C.val("vev")):
                        e.wait_ge(sem["vev"], w0)
                        for r in range(4):
                            mm = e.tensor_copy(
                                bass.AP(ksqe, r, [[16, 128], [4, 4]]),
                                bass.AP(ksqm, r % 2, [[NKV, 128], [2, 4]]))
                        mm.then_inc(sem["vev"], 1)
                    ph.em("vector", _kx)
                    C.add("vev")
                    ktr, nkc = ksqe, 16
                else:
                    ktr, nkc = ksqm, NKV

                def _t2(e, wv=C.val("vev"), ktr=ktr, nkc=nkc):
                    e.wait_ge(sem["vev"], wv)
                    e.transpose(bass.AP(pbc, 0, [[512, nkc], [1, 128]]),
                                ktr[:], identf[:]).then_inc(sem["tmm"], 1)
                ph.em("tensor", _t2)
                C.add("tmm")

                def _r2(e, wt=C.val("tmm"), nkc=nkc):
                    e.wait_ge(sem["tmm"], wt)
                    e.reduce_max(kmx[0:nkc, :],
                                 bass.AP(pbc, 0, [[512, nkc], [1, 128]]),
                                 axis=AX.X).then_inc(sem["vev"], 1)
                ph.em("vector", _r2)
                C.add("vev")

                def _cc(e, wv=C.val("vev"), bar=C.val("vev") + 1):
                    e.wait_ge(sem["vev"], wv)
                    e.tensor_tensor(ccol[0:nqh, :], qmx[0:nqh, :], kmx[0:nqh, :],
                                    ALU.add).then_inc(sem["vev"], 1)
                    e.wait_ge(sem["vev"], bar)
                    e.tensor_scalar(ccs[0:nqh, :], ccol[0:nqh, :], -1.0 / 16.0,
                                    None, ALU.mult).then_inc(sem["vev"], 1)
                ph.em("vector", _cc)
                C.add("vev", 2)

                def _tc(e, wv=C.val("vev")):
                    e.wait_ge(sem["vev"], wv)
                    e.transpose(bass.AP(pbc, 0, [[512, 1], [1, nqh]]),
                                ccs[0:nqh, :],
                                identf[0:nqh, 0:nqh]).then_inc(sem["tmm"], 1)
                ph.em("tensor", _tc)
                C.add("tmm")

                def _cv(e, wt=C.val("tmm")):
                    e.wait_ge(sem["tmm"], wt)
                    e.tensor_copy(crow[:, 0:nqh],
                                  bass.AP(pbc, 0, [[512, 1], [1, nqh]])
                                  ).then_inc(sem["vev"], 1)
                ph.em("vector", _cv)
                C.add("vev")

                def _bc(e, wv=C.val("vev")):
                    e.wait_ge(sem["vev"], wv)
                    e.matmul(bass.AP(pbc, 0, [[512, 128], [1, nqh]]),
                             onesb[0:1, :], crow[:, 0:nqh],
                             start=True, stop=True).then_inc(sem["tmm"], 1)
                ph.em("tensor", _bc)
                C.add("tmm")

                def _ev(e, wt=C.val("tmm")):
                    e.wait_ge(sem["tmm"], wt)
                    e.tensor_copy(negCb[:, 0:nqh],
                                  bass.AP(pbc, 0, [[512, 128], [1, nqh]])
                                  ).then_inc(sem["vev"], 1)
                ph.em("vector", _ev)
                C.add("vev")

            def emit_A2_head(s, h8, first_of_phase, negc_done, prev_tbc):
                nkt = 4 * (s + 1)
                NG = s + 1
                kvh = kv_of(h8)
                poff = (h8 % 2) * 64
                kpoff = (kvh % 2) * 64
                hg = h8 // 2
                khg = kvh // 2
                A1_VEV = C.val("vev")
                ssc0 = C.val("ssc")
                tmm0 = C.val("tmm")
                tmv0 = C.val("tmv")
                vmk0 = C.val("vmk")
                sout0 = PH_SOUT0 if first_of_phase else None

                if QPAD:
                    def sc_mm(e, kt):
                        e.matmul(score_slot(kt),
                                 KTb[:, khg, kt * 128:(kt + 1) * 128],
                                 qTb[:, h8, :],
                                 start=True, stop=True).then_inc(sem["tmm"], 1)
                else:
                    def sc_mm(e, kt):
                        e.matmul(score_slot(kt),
                                 KTb[kpoff:kpoff + 64, khg, kt * 128:(kt + 1) * 128],
                                 qTb[poff:poff + 64, hg, :],
                                 start=True, stop=True).then_inc(sem["tmm"], 1)

                def av_mm(e, kt):
                    e.matmul(bass.AP(poT, 0, [[512, 65], [1, 512]]),
                             Vb[:, kt, kvh, :], expb[:, kt % 4, :],
                             start=(kt == 0), stop=(kt == nkt - 1),
                             skip_group_check=True).then_inc(sem["tmv"], 1)

                GPRE = min(2, NG)

                def _ten_pre(e, GPRE=GPRE, ssc0=ssc0, h8=h8, negc_done=negc_done):
                    if h8 == 0:
                        e.wait_ge(sem["vev"], negc_done)
                    for g in range(GPRE):
                        if ssc0 + 4 * g > 0:
                            e.wait_ge(sem["ssc"], ssc0 + 4 * g)
                        for kt in range(4 * g, 4 * g + 4):
                            sc_mm(e, kt)
                ph.em("tensor", _ten_pre)
                C.add("tmm", 4 * GPRE)

                if prev_tbc is not None:
                    ph.em("tensor", prev_tbc)

                def _ten_rest(e, nkt=nkt, NG=NG, GPRE=GPRE, ssc0=ssc0,
                              vmk0=vmk0, A1_VEV=A1_VEV):
                    e.wait_ge(sem["vev"], A1_VEV)
                    if NG >= 2:
                        e.wait_ge(sem["ssc"], ssc0 + 4)
                        for kt in range(0, 4):
                            av_mm(e, kt)
                        for g in range(GPRE, NG):
                            e.wait_ge(sem["ssc"], ssc0 + 4 * g)
                            for kt in range(4 * g, 4 * g + 4):
                                sc_mm(e, kt)
                            for kt in range(4 * (g - 1), 4 * g):
                                av_mm(e, kt)
                    e.wait_ge(sem["vmk"], vmk0 + 4)
                    for kt in range(nkt - 4, nkt):
                        av_mm(e, kt)
                ph.em("tensor", _ten_rest)
                C.add("tmm", 4 * (NG - GPRE))
                C.add("tmv", nkt)

                def _sca(e, nkt=nkt, NG=NG, tmm0=tmm0, tmv0=tmv0, h8=h8,
                         wnc=negc_done):
                    e.wait_ge(sem["vev"], wnc)
                    for g in range(NG):
                        if tmv0 + 4 * g > 0:
                            e.wait_ge(sem["tmv"], tmv0 + 4 * g)
                        e.wait_ge(sem["tmm"], tmm0 + 4 * g + 4)
                        for kt in range(4 * g, 4 * g + 4):
                            bc = POS_OF[h8] if QPAD else h8
                            e.activation(expb[:, kt % 4, :], score_slot(kt), AF.Exp,
                                         bias=negCb[:, bc:bc + 1],
                                         scale=0.125).then_inc(sem["ssc"], 1)
                ph.em("scalar", _sca)
                C.add("ssc", nkt)

                def _msk(e, nkt=nkt, ssc0=ssc0):
                    e.wait_ge(sem["ssc"], ssc0 + nkt)
                    for dd in range(4):
                        kt = nkt - 4 + dd
                        e.tensor_tensor(expb[:, kt % 4, :], expb[:, kt % 4, :],
                                        maskb[:, dd, :], ALU.mult
                                        ).then_inc(sem["vmk"], 1)
                ph.em("vector", _msk)
                C.add("vmk", 4)

                def _tail(e, wt=C.val("tmv"), so=sout0, bar=C.val("vev") + 1):
                    if so is not None:
                        for sname, v in zip(("so0", "so1", "sout"), so):
                            if v > 0:
                                e.wait_ge(sem[sname], v)
                    e.wait_ge(sem["tmv"], wt)
                    e.reciprocal(rbf[64:65, :],
                                 bass.AP(poT, 64 * 512, [[512, 1], [1, 512]])
                                 ).then_inc(sem["vev"], 1)
                    e.wait_ge(sem["vev"], bar)
                    e.tensor_copy(rib[64:65, :], rbf[64:65, :]).then_inc(sem["vev"], 1)
                ph.em("vector", _tail)
                C.add("vev", 2)

                C.add("tbb")
                tb_t = C.val("tbb")

                def _tbc(e, wv=C.val("vev"), tb=tb_t):
                    e.wait_ge(sem["vev"], wv)
                    e.matmul(bass.AP(pbc, 0, [[512, 64], [1, 512]]),
                             onesb[64:65, 0:64], rib[64:65, :],
                             start=True, stop=True).then_inc(sem["tbb"], 1)

                def _tev(e, wt=tb_t, hg=hg, poff=poff, bar=C.val("vev") + 1):
                    e.wait_ge(sem["tbb"], wt)
                    e.tensor_copy(rbf[0:64, :],
                                  bass.AP(pbc, 0, [[512, 64], [1, 512]])
                                  ).then_inc(sem["vev"], 1)
                    e.wait_ge(sem["vev"], bar)
                    e.tensor_tensor(
                        bass.AP(aoutb, poff * (QHG * 512) + hg * 512,
                                [[QHG * 512, 64], [1, 512]]),
                        bass.AP(poT, 0, [[512, 64], [1, 512]]),
                        rbf[0:64, :], ALU.mult).then_inc(sem["vev"], 1)
                ph.em("vector", _tev)
                C.add("vev", 2)
                return _tbc

            for s in range(NSUP):
                for c4 in range(4):
                    emit_A1_chunk(s, c4)
                emit_negC(s)
                NEGC_DONE = C.val("vev")
                pend = None
                for h8 in range(nqh):
                    pend = emit_A2_head(s, h8,
                                        first_of_phase=(s == 0 and h8 == 0),
                                        negc_done=NEGC_DONE, prev_tbc=pend)
                ph.em("tensor", pend)

                def _sp(e, s=s, wv=C.val("vev"), h0g=head0 // 2):
                    e.wait_ge(sem["vev"], wv)
                    e.dma_start(aout_d[s, :, h0g:h0g + QHG, :],
                                aoutb[:]).then_inc(sem["sout"], 16)
                ph.em("sync", _sp)
                C.add("sout", 16)
                if s + 1 < NSUP:
                    def _w(e, ws=C.val("sout")):
                        e.wait_ge(sem["sout"], ws)
                    ph.em("vector", _w)

            ph.play(nc)
            st.close()

        def emit_proj(name, w_src, w_is_transposed, resid_mode, resid_src_d, dst_d,
                      aout_ready_sout):
            ph = Phase(name)
            st = contextlib.ExitStack()
            wob = st.enter_context(nc.sbuf_tensor(name + "_wob", [128, DC, D], BF16))
            wtmp = st.enter_context(nc.sbuf_tensor(name + "_wtmp", [128, DC, D], BF16))
            aob = st.enter_context(nc.sbuf_tensor(name + "_aob", [128, 2, DC, 512], BF16))
            xab = st.enter_context(nc.sbuf_tensor(name + "_xab", [128, 2, D], F32))
            orb = st.enter_context(nc.sbuf_tensor(name + "_orb", [128, 2, D], F32))
            stt = st.enter_context(nc.sbuf_tensor(name + "_stt", [128, 16], F32))
            sqs = st.enter_context(nc.sbuf_tensor(name + "_sqs", [128, D], F32))
            sq2 = st.enter_context(nc.sbuf_tensor(name + "_sq2", [128, D], F32))
            xns = st.enter_context(nc.sbuf_tensor(name + "_xns", [128, D], F32))
            pP = st.enter_context(nc.psum_tensor(name + "_pP", [128, 2, 512], F32))
            pT0 = st.enter_context(nc.psum_tensor(name + "_pT0", [128, 128], BF16))
            pT1 = st.enter_context(nc.psum_tensor(name + "_pT1", [128, 128], BF16))
            pT = (pT0, pT1)

            def _ld(e):
                e.dma_start(wtmp[:], w_src.rearrange("(c p) i -> p c i", p=128)
                            ).then_inc(sem["swg"], 16)
            ph.em("gpsimd", _ld)
            C.add("swg", 16)
            wsw = C.val("swg")
            if w_is_transposed:
                for oc in range(DC):
                    def xw(e, wsw=wsw):
                        e.wait_ge(sem["swg"], wsw)
                    transposes(ph, pT,
                               lambda j, oc=oc: wtmp[:, oc, j * 128:(j + 1) * 128],
                               lambda j, oc=oc: wob[:, j, oc * 128:(oc + 1) * 128],
                               DC, extra_tensor_wait=xw)
                wsrcb = wob
            else:
                wsrcb = wtmp
            W_VEV = C.val("vev")
            W_SWG = C.val("swg")

            aob_war = [0, 0]
            orb_war = [0, 0]
            xab_war = [0, 0]
            stt_war = [0, 0]
            xns_war = [0]
            for s in range(NSUP):
                slot = s % 2

                sas = f"sa{slot}"

                def _lda(e, s=s, slot=slot, so=aout_ready_sout, wt=aob_war[slot],
                         sas=sas):
                    e.wait_ge(sem["sout"], so)
                    if wt > 0:
                        e.wait_ge(sem["tmm"], wt)
                    e.dma_start(aob[:, slot, :, :], aout_d[s]).then_inc(sem[sas], 16)
                ph.em("sync", _lda)
                C.add(sas, 16)

                for t4 in range(4):
                    c = s * 4 + t4
                    cslot = c % 2
                    sxs = f"sx{cslot}"

                    wsem = "vev" if resid_mode == "x" else "ssc"

                    def _ldx(e, c=c, cslot=cslot, wv=xab_war[cslot], sxs=sxs,
                             wsem=wsem):
                        if wv > 0:
                            e.wait_ge(sem[wsem], wv)
                        e.dma_start(xab[:, cslot, :],
                                    resid_src_d[c * 128:(c + 1) * 128, :]
                                    ).then_inc(sem[sxs], 16)
                    ph.em("sync", _ldx)
                    C.add(sxs, 16)

                    for dh in range(2):
                        def _mm(e, slot=slot, t4=t4, dh=dh, wsa=C.val(sas),
                                wv=W_VEV, ws=W_SWG, war=C.val("vev"), sas=sas):
                            e.wait_ge(sem[sas], wsa)
                            e.wait_ge(sem["swg"], ws)
                            if wv > 0:
                                e.wait_ge(sem["vev"], wv)
                            e.wait_ge(sem["vev"], war)
                            for hg in range(DC):
                                mm = e.matmul(pP[:, dh, :],
                                              aob[:, slot, hg, t4 * 128:(t4 + 1) * 128],
                                              wsrcb[:, hg, dh * 512:(dh + 1) * 512],
                                              start=(hg == 0), stop=(hg == DC - 1))
                            mm.then_inc(sem["tmm"], 1)
                        ph.em("tensor", _mm)
                        C.add("tmm")
                    aob_war[slot] = C.val("tmm")

                    if resid_mode == "x":
                        def _add(e, cslot=cslot, wt=C.val("tmm"), wx=C.val(sxs),
                                 wo=orb_war[cslot], sxs=sxs):
                            e.wait_ge(sem[sxs], wx)
                            e.wait_ge(sem["tmm"], wt)
                            if wo > 0:
                                e.wait_ge(sem[f"so{cslot}"], wo)
                            e.tensor_tensor(orb[:, cslot, 0:512], xab[:, cslot, 0:512],
                                            pP[:, 0, :], ALU.add)
                            e.tensor_tensor(orb[:, cslot, 512:1024],
                                            xab[:, cslot, 512:1024],
                                            pP[:, 1, :], ALU.add).then_inc(sem["vev"], 1)
                        ph.em("vector", _add)
                        C.add("vev")
                        xab_war[cslot] = C.val("vev")
                    else:
                        po = 8 * cslot

                        def _red(e, cslot=cslot, po=po, wx=C.val(sxs), sxs=sxs,
                                 ws=stt_war[cslot]):
                            e.wait_ge(sem[sxs], wx)
                            if ws > 0:
                                e.wait_ge(sem["ssc"], ws)
                            e.reduce_sum(stt[:, po:po + 1], xab[:, cslot, :],
                                         axis=AX.X).then_inc(sem["vev"], 1)
                        ph.em("vector", _red)
                        C.add("vev")

                        def _chain(e, cslot=cslot, po=po, wv=C.val("vev"),
                                   wh=xns_war[0], b1=C.val("ssc") + 1):
                            e.wait_ge(sem["vev"], wv)

                            def s_(i):
                                return stt[:, po + i:po + i + 1]
                            xa = xab[:, cslot, :]
                            e.activation(s_(1), s_(0), AF.Copy,
                                         scale=-1.0 / D).then_inc(sem["ssc"], 1)
                            e.wait_ge(sem["ssc"], b1)
                            e.add(sqs[:], xa, s_(1))
                            e.activation(sq2[:], sqs[:], AF.Square, accum_out=s_(2))
                            e.activation(s_(3), s_(2), AF.Copy, scale=1.0 / D,
                                         bias=EPS_LN)
                            e.activation(s_(4), s_(3), AF.Ln)
                            e.activation(s_(5), s_(4), AF.Exp,
                                         scale=-0.5).then_inc(sem["ssc"], 1)
                            e.wait_ge(sem["ssc"], b1 + 1)
                            e.mul(s_(6), s_(5), s_(5))
                            e.mul(s_(6), s_(6), s_(3))
                            e.activation(s_(6), s_(6), AF.Copy, scale=-0.5, bias=1.5)
                            e.mul(s_(6), s_(6), s_(5)).then_inc(sem["ssc"], 1)
                            e.wait_ge(sem["ssc"], b1 + 2)
                            if wh > 0:
                                e.wait_ge(sem["vev"], wh)
                            e.mul(xns[:], sqs[:], s_(6)).then_inc(sem["ssc"], 1)
                        ph.em("scalar", _chain)
                        C.add("ssc", 4)
                        stt_war[cslot] = C.val("ssc")
                        xab_war[cslot] = C.val("ssc")

                        def _fin(e, cslot=cslot, ws=C.val("ssc"), wt=C.val("tmm"),
                                 wo=orb_war[cslot]):
                            e.wait_ge(sem["ssc"], ws)
                            e.wait_ge(sem["tmm"], wt)
                            if wo > 0:
                                e.wait_ge(sem[f"so{cslot}"], wo)
                            e.tensor_tensor(orb[:, cslot, 0:512], xns[:, 0:512],
                                            pP[:, 0, :], ALU.add)
                            e.tensor_tensor(orb[:, cslot, 512:1024], xns[:, 512:1024],
                                            pP[:, 1, :], ALU.add).then_inc(sem["vev"], 1)
                        ph.em("vector", _fin)
                        C.add("vev")
                        xns_war[0] = C.val("vev")

                    def _st(e, c=c, cslot=cslot, wv=C.val("vev")):
                        e.wait_ge(sem["vev"], wv)
                        e.dma_start(dst_d[c * 128:(c + 1) * 128, :],
                                    orb[:, cslot, :]).then_inc(sem[f"so{cslot}"], 16)
                    ph.em("sync", _st)
                    C.add(f"so{cslot}", 16)
                    orb_war[cslot] = C.val(f"so{cslot}")
            ph.play(nc)
            st.close()

        def newton(e, stt):
            e.tensor_tensor(stt[:, 6:7], stt[:, 5:6], stt[:, 5:6], ALU.mult)
            e.tensor_tensor(stt[:, 6:7], stt[:, 6:7], stt[:, 3:4], ALU.mult)
            e.tensor_scalar(stt[:, 6:7], stt[:, 6:7], -0.5, 1.5, ALU.mult, ALU.add)
            e.tensor_tensor(stt[:, 6:7], stt[:, 6:7], stt[:, 5:6], ALU.mult)

        emit_attention("Ap0", x_d, 8, lambda h: h, 0, "transposed")
        emit_attention("Ap1", x_d, 8, lambda h: h, 8, "transposed")
        ASP1 = C.val("sout")
        emit_proj("A3", wo_d, True, "x", x_d, x1_d, ASP1)
        X1_DONE = (C.val("so0"), C.val("so1"))

        ln1r_g = es.enter_context(nc.sbuf_tensor("ln1r", [128, D], F32))
        ph = Phase("LD1")

        def _l1(e, wo=X1_DONE):
            e.wait_ge(sem["so0"], wo[0])
            e.wait_ge(sem["so1"], wo[1])
            e.dma_start(ln1r_g[:], ln1_d).then_inc(sem["saux"], 16)
        ph.em("sync", _l1)
        C.add("saux", 16)

        def _l1w(e, wa=C.val("saux")):
            e.wait_ge(sem["saux"], wa)
        ph.em("vector", _l1w)
        ph.play(nc)

        emit_attention("F1", x1_d, 16, lambda h: h // 2, 0, "natural",
                       ln_rms=True, ln1r=ln1r_g)
        FSP = C.val("sout")
        emit_proj("F1A3", fwo_d, False, "lnx1", x1_d, y1_d, FSP)
        Y1_DONE = (C.val("so0"), C.val("so1"))

        ph = Phase("F2pre")
        st = contextlib.ExitStack()
        ln2r = st.enter_context(nc.sbuf_tensor("ln2r", [128, D], F32))
        rwb = st.enter_context(nc.sbuf_tensor("rwb", [128, DC, NE], BF16))
        ysb = st.enter_context(nc.sbuf_tensor("f2p_ysb", [128, 2, D], F32))
        stt2 = st.enter_context(nc.sbuf_tensor("f2p_stt", [128, 24], F32))
        sqs2 = st.enter_context(nc.sbuf_tensor("f2p_sqs", [128, D], F32))
        h2b = st.enter_context(nc.sbuf_tensor("f2p_h2b", [128, D], BF16))
        h2T = st.enter_context(nc.sbuf_tensor("f2p_h2T", [128, 2, DC, 128], BF16))
        prb = st.enter_context(nc.sbuf_tensor("f2p_prb", [128, NE], F32))
        wtb = st.enter_context(nc.sbuf_tensor("f2p_wtb", [128, NE], F32))
        pT0 = st.enter_context(nc.psum_tensor("f2p_pT0", [128, 128], BF16))
        pT1 = st.enter_context(nc.psum_tensor("f2p_pT1", [128, 128], BF16))
        pR = st.enter_context(nc.psum_tensor("f2p_pR", [128, NE], F32))
        pT2 = (pT0, pT1)

        def _ld2(e, wo=Y1_DONE):
            e.wait_ge(sem["so0"], wo[0])
            e.wait_ge(sem["so1"], wo[1])
            e.dma_start(ln2r[:], ln2_d).then_inc(sem["saux"], 16)
        ph.em("sync", _ld2)
        C.add("saux", 16)
        LN2_AUX = C.val("saux")

        def _ldr(e):
            e.dma_start(rwb[:], rw_d.rearrange("(c p) e -> p c e", p=128)
                        ).then_inc(sem["swg"], 16)
        ph.em("gpsimd", _ldr)
        C.add("swg", 16)
        RW_SWG = C.val("swg")

        ysb_free = [0, 0]
        h2T_war = [0, 0]
        stt2_war = [0, 0]
        hsc_war2 = [0]
        for c in range(NCH):
            slot = c % 2
            tslot = c % 2
            po = 8 * slot
            sxs = f"sx{slot}"

            def _ldy(e, c=c, slot=slot, wv=ysb_free[slot], sxs=sxs):
                if wv > 0:
                    e.wait_ge(sem["ssc"], wv)
                e.dma_start(ysb[:, slot, :], y1_d[c * 128:(c + 1) * 128, :]
                            ).then_inc(sem[sxs], 16)
            ph.em("sync", _ldy)
            C.add(sxs, 16)

            def _chain(e, slot=slot, po=po, wx=C.val(sxs), wa=LN2_AUX, sxs=sxs,
                       wh=hsc_war2[0], ws2=stt2_war[slot], b1=C.val("ssc") + 1):
                e.wait_ge(sem[sxs], wx)
                e.wait_ge(sem["saux"], wa)
                if ws2 > 0:
                    e.wait_ge(sem["ssc"], ws2)
                if wh > 0:
                    e.wait_ge(sem["vev"], wh)

                def s_(i):
                    return stt2[:, po + i:po + i + 1]
                yv = ysb[:, slot, :]
                e.activation(sqs2[:], yv, AF.Square, accum_out=s_(2))
                e.activation(s_(3), s_(2), AF.Copy, scale=1.0 / D, bias=EPS_RMS)
                e.activation(s_(4), s_(3), AF.Ln)
                e.activation(s_(5), s_(4), AF.Exp, scale=-0.5).then_inc(sem["ssc"], 1)
                e.wait_ge(sem["ssc"], b1)
                e.mul(s_(6), s_(5), s_(5))
                e.mul(s_(6), s_(6), s_(3))
                e.activation(s_(6), s_(6), AF.Copy, scale=-0.5, bias=1.5)
                e.mul(s_(6), s_(6), s_(5)).then_inc(sem["ssc"], 1)
                e.wait_ge(sem["ssc"], b1 + 1)
                e.mul(sqs2[:], yv, s_(6)).then_inc(sem["ssc"], 1)
            ph.em("scalar", _chain)
            C.add("ssc", 3)
            stt2_war[slot] = C.val("ssc")
            ysb_free[slot] = C.val("ssc")

            def _h2(e, ws=C.val("ssc")):
                e.wait_ge(sem["ssc"], ws)
                e.tensor_tensor(h2b[:], sqs2[:], ln2r[:],
                                ALU.mult).then_inc(sem["vev"], 1)
            ph.em("vector", _h2)
            C.add("vev")
            hsc_war2[0] = C.val("vev")

            def xvw(e, wo=h2T_war[tslot], tslot=tslot):
                if wo > 0:
                    e.wait_ge(sem[f"so{tslot}"], wo)
            transposes(ph, pT2,
                       lambda j: h2b[:, j * 128:(j + 1) * 128],
                       lambda j, tslot=tslot: h2T[:, tslot, j, :],
                       DC, extra_vector_wait=xvw)

            def _sth(e, c=c, tslot=tslot, wv=C.val("vev")):
                e.wait_ge(sem["vev"], wv)
                e.dma_start(h2t_d[:, :, c * 128:(c + 1) * 128]
                            .rearrange("a p t -> p a t"),
                            h2T[:, tslot, :, :]).then_inc(sem[f"so{tslot}"], 16)
            ph.em("sync", _sth)
            C.add(f"so{tslot}", 16)
            h2T_war[tslot] = C.val(f"so{tslot}")

            def _rmm(e, tslot=tslot, wv=C.val("vev"), ws=RW_SWG, wsc=C.val("ssc")):
                e.wait_ge(sem["swg"], ws)
                e.wait_ge(sem["vev"], wv)
                e.wait_ge(sem["ssc"], wsc)
                for j in range(DC):
                    mm = e.matmul(pR[:], h2T[:, tslot, j, :], rwb[:, j, :],
                                  start=(j == 0), stop=(j == DC - 1))
                mm.then_inc(sem["tmm"], 1)
            ph.em("tensor", _rmm)
            C.add("tmm")

            def _sm(e, wt=C.val("tmm")):
                e.wait_ge(sem["tmm"], wt)
                e.reduce_max(stt2[:, 16:17], pR[:], axis=AX.X, negate=True
                             ).then_inc(sem["vev"], 1)
            ph.em("vector", _sm)
            C.add("vev")

            def _ex(e, wv=C.val("vev")):
                e.wait_ge(sem["vev"], wv)
                e.activation(prb[:], pR[:], AF.Exp, bias=stt2[:, 16:17],
                             accum_out=stt2[:, 17:18]).then_inc(sem["ssc"], 1)
            ph.em("scalar", _ex)
            C.add("ssc")

            def _top2(e, c=c, ws=C.val("ssc"), b=C.val("vev")):
                e.wait_ge(sem["ssc"], ws)

                def bar(ins, i):
                    ins.then_inc(sem["vev"], 1)
                    e.wait_ge(sem["vev"], b + i)
                bar(e.reciprocal(stt2[:, 18:19], stt2[:, 17:18]), 1)
                bar(e.tensor_scalar(prb[:], prb[:], stt2[:, 18:19], None, ALU.mult), 2)
                bar(e.reduce_max(stt2[:, 19:20], prb[:], axis=AX.X), 3)
                bar(e.tensor_scalar(wtb[:], prb[:], stt2[:, 19:20], None, ALU.is_lt), 4)
                bar(e.tensor_tensor(wtb[:], wtb[:], prb[:], ALU.mult), 5)
                bar(e.reduce_max(stt2[:, 20:21], wtb[:], axis=AX.X), 6)
                bar(e.tensor_scalar(wtb[:], prb[:], stt2[:, 20:21], None, ALU.is_ge), 7)
                bar(e.tensor_tensor(wtb[:], wtb[:], prb[:], ALU.mult), 8)
                bar(e.reduce_sum(stt2[:, 21:22], wtb[:], axis=AX.X), 9)
                bar(e.reciprocal(stt2[:, 22:23], stt2[:, 21:22]), 10)
                tq, tc = c // TQC, c % TQC
                e.tensor_scalar(
                    bass.AP(wfull, tq * (NE * TQC) + tc,
                            [[NTQ * NE * TQC, 128], [TQC, NE], [1, 1]]),
                    wtb[:], stt2[:, 22:23], None, ALU.mult).then_inc(sem["vev"], 1)
            ph.em("vector", _top2)
            C.add("vev", 11)
        H2T_DONE = (C.val("so0"), C.val("so1"))
        ph.play(nc)
        st.close()

        GUSL = [(0, 4), (4, 4), (8, 4), (12, 4), (16, 4), (20, 4), (24, 2), (26, 2)]
        NGU = len(GUSL)
        GR_SL = [n * 2 * 2 for _, n in GUSL]
        GR_PER_E = sum(GR_SL)
        GGR_PER_E = GR_PER_E // 2

        ph = Phase("F2")
        st = contextlib.ExitStack()
        h2Tq = st.enter_context(nc.sbuf_tensor("f2_h2Tq", [128, DC, TQT], BF16))
        aTb = st.enter_context(nc.sbuf_tensor("f2_aTb", [128, MC, TQT], BF16))
        yac = st.enter_context(nc.sbuf_tensor("f2_yac", [128, TQC, D], F32))
        wgub = st.enter_context(nc.sbuf_tensor("f2_wgub", [128, 2, 2, DC, 512], BF16))
        wdb = st.enter_context(nc.sbuf_tensor("f2_wdb", [128, 2, 4, D], BF16))
        sib = st.enter_context(nc.sbuf_tensor("f2_sib", [128, 4, 512], BF16))
        wsel = st.enter_context(nc.sbuf_tensor("f2_wsel", [128, 8], F32))
        xfb = st.enter_context(nc.sbuf_tensor("f2_xfb", [128, 2, D], F32))
        yfb = st.enter_context(nc.sbuf_tensor("f2_yfb", [128, 2, D], F32))
        ofb = st.enter_context(nc.sbuf_tensor("f2_ofb", [128, 2, D], F32))
        pF = st.enter_context(nc.psum_tensor("f2_pF", [128, 8, 512], F32))

        def _z0(e, tgt=C.val("vz") + 1):
            e.memset(yac[:], 0.0).then_inc(sem["vz"], 1)
            e.wait_ge(sem["vz"], tgt)
            e.sem_inc(sem["vmoe"], 4)
        ph.em("vector", _z0)
        C.add("vz")
        C.add("vmoe", 4)

        def _sb(e):
            e.sem_inc(sem["ssil"], 1)
        ph.em("scalar", _sb)
        C.add("ssil", 1)

        TMO_0 = C.val("tmoe")
        TD_0 = C.val("tdown")
        SIL_0 = C.val("ssil")
        VMO_0 = C.val("vmoe")
        VDE_0 = C.val("vdev")
        TSL_0 = C.val("tslc")
        SWG_0 = C.val("swg")

        H2Q_AUX0 = C.val("shq")

        def emit_ldh(tq):
            def _ldh(e, tq=tq, wo=H2T_DONE, wt=TMO_0 + tq * NE * GR_PER_E):
                e.wait_ge(sem["so0"], wo[0])
                e.wait_ge(sem["so1"], wo[1])
                if tq > 0:
                    e.wait_ge(sem["tmoe"], wt)
                e.dma_start(h2Tq[:],
                            h2t_d[:, :, tq * TQT:(tq + 1) * TQT]
                            .rearrange("a p t -> p a t")).then_inc(sem["shq"], 16)
            ph.em("sync", _ldh)
            C.add("shq", 16)

        for tq in range(NTQ):
            for e_i in range(NE):
                for si, (mc0, nmc) in enumerate(GUSL):
                    gsl = (tq * NE + e_i) * NGU + si

                    def _lw(e, e_i=e_i, mc0=mc0, nmc=nmc, gsl=gsl, si=si, tq=tq):
                        if si == 0 and e_i == 0:
                            e.wait_ge(sem["shq"], H2Q_AUX0 + 16 * (tq + 1))
                        if gsl >= 2:
                            done = sum(GR_SL[k % NGU] for k in range(gsl - 1))
                            e.wait_ge(sem["tmoe"], TMO_0 + done)
                        sg = f"sgu{si % 2}"
                        sl = gsl % 2
                        e.dma_start(wgub[:, sl, 0, :, 0:nmc * 128],
                                    wg_d[e_i, :, mc0 * 128:(mc0 + nmc) * 128]
                                    .rearrange("(c p) m -> p c m", p=128)
                                    ).then_inc(sem[sg], 16)
                        e.dma_start(wgub[:, sl, 1, :, 0:nmc * 128],
                                    wu_d[e_i, :, mc0 * 128:(mc0 + nmc) * 128]
                                    .rearrange("(c p) m -> p c m", p=128)
                                    ).then_inc(sem[sg], 16)
                    ph.em("gpsimd", _lw)
                    C.add(f"sgu{si % 2}", 32)
                for dg in range(2):
                    for si, (mc0, nmc) in enumerate(GUSL):
                        wsl = (tq * NE + e_i) * 16 + dg * NGU + si

                        def _lwd(e, e_i=e_i, mc0=mc0, nmc=nmc, wsl=wsl, si=si):
                            if wsl >= 2:
                                prev = wsl - 2
                                ns = sum(1 for k in range(prev + 1)
                                         if (k % NGU) != NGU - 1)
                                nf = (prev + 1) - ns
                                if ns > 0:
                                    e.wait_ge(sem["tslc"], TSL_0 + ns)
                                if nf > 0:
                                    e.wait_ge(sem["tdown"], TD_0 + nf * 8)
                            e.dma_start(wdb[:, wsl % 2, 0:nmc, :],
                                        wd_d[e_i, mc0 * 128:(mc0 + nmc) * 128, :]
                                        .rearrange("(c p) d -> p c d", p=128)
                                        ).then_inc(sem[f"swd{si % 2}"], 16)
                        ph.em("gpsimd", _lwd)
                        C.add(f"swd{si % 2}", 16)

        NFLAT = NTQ * NE

        def _ten(e):
            with e.register("f2rg0") as rg0, e.register("f2rg1") as rg1, \
                 e.register("f2rs") as rs, \
                 e.register("f2rv") as rv, e.register("f2rvd") as rvd, \
                 e.register("f2rw0") as rwd0, e.register("f2rw1") as rwd1, \
                 e.register("f2rvm") as rvm:
                e.reg_mov(rg0, 32)
                e.reg_mov(rg1, 32)
                e.reg_mov(rs, SIL_0 - 1)
                e.reg_mov(rv, VMO_0 - 1)
                e.reg_mov(rvd, VDE_0)
                e.reg_mov(rwd0, 16)
                e.reg_mov(rwd1, 16)
                e.reg_mov(rvm, VMO_0 + GGR_PER_E)
                with e.Fori(0, NFLAT):
                    e.wait_ge(sem["vdev"], rvd)
                    gidx = 0
                    for si, (mc0, nmc) in enumerate(GUSL):
                        rg = rg0 if si % 2 == 0 else rg1
                        e.wait_ge(sem[f"sgu{si % 2}"], rg)
                        e.reg_add(rg, rg, 32)
                        for ml in range(nmc):
                            for tg in range(2):
                                for gu in range(2):
                                    bank = gu * 2 + (gidx % 2)
                                    if gu == 0:
                                        e.wait_ge(sem["ssil"], rs)
                                        e.reg_add(rs, rs, 1)
                                    else:
                                        e.wait_ge(sem["vmoe"], rv)
                                        e.reg_add(rv, rv, 1)
                                    for j in range(DC):
                                        mm = e.matmul(
                                            pF[:, bank, :],
                                            wgub[:, si % 2, gu, j,
                                                 ml * 128:(ml + 1) * 128],
                                            h2Tq[:, j, tg * 512:(tg + 1) * 512],
                                            start=(j == 0), stop=(j == DC - 1),
                                            skip_group_check=True)
                                    mm.then_inc(sem["tmoe"], 1)
                                gidx += 1
                    e.wait_ge(sem["vmoe"], rvm)
                    e.reg_add(rvm, rvm, GGR_PER_E)
                    for dg in range(2):
                        if dg == 1:
                            e.wait_ge(sem["vdev"], rvd)
                        for si, (mc0, nmc) in enumerate(GUSL):
                            rwd = rwd0 if si % 2 == 0 else rwd1
                            e.wait_ge(sem[f"swd{si % 2}"], rwd)
                            e.reg_add(rwd, rwd, 16)
                            for ml in range(nmc):
                                mc = mc0 + ml
                                for tc in range(TQC):
                                    mm = e.matmul(
                                        pF[:, tc, :],
                                        aTb[:, mc, tc * 128:(tc + 1) * 128],
                                        wdb[:, (dg * NGU + si) % 2, ml,
                                            dg * 512:(dg + 1) * 512],
                                        start=(mc == 0), stop=(mc == MC - 1),
                                        skip_group_check=True)
                                    if mc == MC - 1:
                                        mm.then_inc(sem["tdown"], 1)
                                    elif ml == nmc - 1 and tc == TQC - 1:
                                        mm.then_inc(sem["tslc"], 1)
                        if dg == 0:
                            e.reg_add(rvd, rvd, 8)
                    e.reg_add(rvd, rvd, 8)
        ph.em("tensor", _ten)
        C.add("tmoe", NFLAT * GR_PER_E)
        C.add("tdown", NFLAT * 16)
        C.add("tslc", NFLAT * 14)

        def _sca(e):
            with e.register("f2srt") as rt, e.register("f2srv") as rv2:
                e.reg_mov(rt, TMO_0 + 1)
                e.reg_mov(rv2, VMO_0 - 3)
                with e.Fori(0, NFLAT):
                    g = 0
                    for si, (mc0, nmc) in enumerate(GUSL):
                        for ml in range(nmc):
                            for tg in range(2):
                                bank = g % 2
                                e.wait_ge(sem["tmoe"], rt)
                                e.reg_add(rt, rt, 2)
                                e.wait_ge(sem["vmoe"], rv2)
                                e.reg_add(rv2, rv2, 1)
                                e.activation(sib[:, g % 4, :], pF[:, bank, :],
                                             AF.Silu).then_inc(sem["ssil"], 1)
                                g += 1
        ph.em("scalar", _sca)
        C.add("ssil", NFLAT * GGR_PER_E)

        ofb_war = [0, 0]
        for tq in range(NTQ):
            emit_ldh(tq)
            SIL_T = SIL_0 + tq * NE * GGR_PER_E
            TMO_T = TMO_0 + tq * NE * GR_PER_E
            TD_T = TD_0 + tq * NE * 16

            def _vec(e, tq=tq, SIL_T=SIL_T, TMO_T=TMO_T, TD_T=TD_T):
                with e.register(f"f2vs{tq}") as rsl, e.register(f"f2vt{tq}") as rt, \
                     e.register(f"f2vd{tq}") as rtd, e.register(f"f2vw{tq}") as rwo:
                    e.reg_mov(rsl, SIL_T + 1)
                    e.reg_mov(rt, TMO_T + 2)
                    e.reg_mov(rtd, TD_T + 1)
                    e.reg_mov(rwo, tq * (NE * TQC))
                    with e.Fori(0, NE):
                        e.tensor_copy(wsel[:, 0:TQC],
                                      bass.AP(wfull, rwo,
                                              [[NTQ * NE * TQC, 128], [1, TQC]]))
                        g = 0
                        for si, (mc0, nmc) in enumerate(GUSL):
                            for ml in range(nmc):
                                mc = mc0 + ml
                                for tg in range(2):
                                    bank = g % 2
                                    e.wait_ge(sem["ssil"], rsl)
                                    e.reg_add(rsl, rsl, 1)
                                    e.wait_ge(sem["tmoe"], rt)
                                    e.reg_add(rt, rt, 2)
                                    e.tensor_tensor(
                                        aTb[:, mc, tg * 512:(tg + 1) * 512],
                                        sib[:, g % 4, :], pF[:, 2 + bank, :],
                                        ALU.mult).then_inc(sem["vmoe"], 1)
                                    g += 1
                        for dg in range(2):
                            for tc in range(TQC):
                                e.wait_ge(sem["tdown"], rtd)
                                e.reg_add(rtd, rtd, 1)
                                e.scalar_tensor_tensor(
                                    yac[:, tc, dg * 512:(dg + 1) * 512],
                                    pF[:, tc, :],
                                    wsel[:, tc:tc + 1],
                                    yac[:, tc, dg * 512:(dg + 1) * 512],
                                    ALU.mult, ALU.add).then_inc(sem["vdev"], 1)
                        e.reg_add(rwo, rwo, TQC)
            ph.em("vector", _vec)
            C.add("vmoe", NE * GGR_PER_E)
            C.add("vdev", NE * 16)

            for tc in range(TQC):
                c = tq * TQC + tc
                cs = tc % 2

                sxs = f"sx{cs}"

                def _ldf(e, c=c, cs=cs, wo=ofb_war[cs], sxs=sxs):
                    if wo > 0:
                        e.wait_ge(sem[f"so{cs}"], wo)
                    e.dma_start(xfb[:, cs, :], x1_d[c * 128:(c + 1) * 128, :]
                                ).then_inc(sem[sxs], 16)
                    e.dma_start(yfb[:, cs, :], y1_d[c * 128:(c + 1) * 128, :]
                                ).then_inc(sem[sxs], 16)
                ph.em("sync", _ldf)
                C.add(sxs, 32)

                def _fin(e, tc=tc, cs=cs, wx=C.val(sxs), zt=C.val("vz") + 1,
                         sxs=sxs, wd=C.val("vdev"), bar=C.val("vev") + 1):
                    e.wait_ge(sem[sxs], wx)
                    e.wait_ge(sem["vdev"], wd)
                    e.tensor_tensor(ofb[:, cs, :], yfb[:, cs, :], yac[:, tc, :],
                                    ALU.add).then_inc(sem["vev"], 1)
                    e.wait_ge(sem["vev"], bar)
                    e.tensor_tensor(ofb[:, cs, :], ofb[:, cs, :], xfb[:, cs, :],
                                    ALU.add).then_inc(sem["vev"], 1)
                    e.memset(yac[:, tc, :], 0.0).then_inc(sem["vz"], 1)
                    e.wait_ge(sem["vz"], zt)
                ph.em("vector", _fin)
                C.add("vz")
                C.add("vev", 2)

                def _sto(e, c=c, cs=cs, wv=C.val("vev")):
                    e.wait_ge(sem["vev"], wv)
                    e.dma_start(out_d[c * 128:(c + 1) * 128, :],
                                ofb[:, cs, :]).then_inc(sem[f"so{cs}"], 16)
                ph.em("sync", _sto)
                C.add(f"so{cs}", 16)
                ofb_war[cs] = C.val(f"so{cs}")

        def _done(e, w0=C.val("so0"), w1=C.val("so1"), w2=C.val("sout")):
            e.wait_ge(sem["so0"], w0)
            e.wait_ge(sem["so1"], w1)
            e.wait_ge(sem["sout"], w2)
        ph.em("vector", _done)
        ph.play(nc)
        st.close()

    n_instr = sum(1 for _ in nc.all_instructions())
    print(f"[kernel] built S={S}: {n_instr} instructions")
    return nc


def _consts(S):
    identb = np.eye(128, dtype=ml_dtypes.bfloat16)
    identf = np.eye(128, dtype=np.float32)
    onesb = np.ones((128, 128), dtype=ml_dtypes.bfloat16)
    inv_freq = (1.0 / (10000.0 ** (np.arange(0, HD, 2, dtype=np.float64) / HD)))
    t = np.arange(S, dtype=np.float64)
    ang = np.outer(t, inv_freq)
    cos = np.cos(ang).astype(np.float32)
    sin = np.sin(ang).astype(np.float32)
    cs = np.stack([cos, sin])[:, :, None, :]
    cs = np.ascontiguousarray(np.broadcast_to(cs, (2, S, NH, 32)).astype(np.float32))
    kk = np.arange(512)[:, None]
    qq = np.arange(512)[None, :]
    mask = np.ascontiguousarray(
        (kk <= qq).reshape(4, 128, 512).astype(ml_dtypes.bfloat16))
    return identb, identf, onesb, cs, mask


def make_in_maps(inputs, S=4096, n=8):
    identb, identf, onesb, cs, mask = _consts(S)
    f32 = lambda a: np.ascontiguousarray(np.asarray(a, dtype=np.float32))
    rep = lambda a: np.ascontiguousarray(
        np.broadcast_to(np.asarray(a, np.float32).reshape(1, D), (128, D)))
    xs = f32(inputs["L0_attn_x"])
    common = {
        "wq": f32(inputs["L0_attn_Wq"]), "wk": f32(inputs["L0_attn_Wk"]),
        "wv": f32(inputs["L0_attn_Wv"]), "wo": f32(inputs["L0_attn_Wo"]),
        "ln1": rep(inputs["L0_ffn_ln1_w"]), "ln2": rep(inputs["L0_ffn_ln2_w"]),
        "fwq": np.ascontiguousarray(
            f32(inputs["L0_ffn_wq"]).reshape(D, 16, 64)[:, PERM_Q, :].reshape(D, D)),
        "fwk": f32(inputs["L0_ffn_wk"]),
        "fwv": f32(inputs["L0_ffn_wv"]), "fwo": f32(inputs["L0_ffn_wo"]),
        "rw": f32(inputs["L0_ffn_router_w"]),
        "wgate": f32(inputs["L0_ffn_w_gate"]), "wup": f32(inputs["L0_ffn_w_up"]),
        "wdown": f32(inputs["L0_ffn_w_down"]),
        "identb": identb, "identf": identf, "onesb": onesb,
        "cossin": cs, "mask01": mask,
    }
    return [dict(common, x=xs[c][:S]) for c in range(n)]


def kernel(**inputs):
    S = 4096
    n = 8
    nc = build(S=S)
    in_maps = make_in_maps(inputs, S=S, n=n)
    res = run_bass_kernel_spmd(nc, in_maps, core_ids=list(range(n)))
    return np.stack([res.results[c]["out"] for c in range(n)], axis=0)


# version 05
# speedup vs baseline: 1.2383x; 1.2007x over previous
import contextlib
import numpy as np
import ml_dtypes
import concourse.bass as bass
import concourse.mybir as mybir
from concourse.bass_utils import run_bass_kernel_spmd

F32 = mybir.dt.float32
BF16 = mybir.dt.bfloat16
AF = mybir.ActivationFunctionType
ALU = mybir.AluOpType
AX = mybir.AxisListType

D = 1024
DC = D // 128
NH = 16
NKV = 8
HD = 64
M = 3584
MC = M // 128
NE = 8
EPS_LN = 1e-5
EPS_RMS = 1e-6

PERM_Q = [0, 2, 1, 3, 4, 6, 5, 7, 8, 10, 9, 11, 12, 14, 13, 15]
POS_OF = {h: p for p, h in enumerate(PERM_Q)}

ENGINES = ("sync", "gpsimd", "tensor", "scalar", "vector")


class Phase:
    def __init__(self, name):
        self.name = name
        self.ops = {k: [] for k in ENGINES}

    def em(self, eng, fn):
        self.ops[eng].append(fn)

    def play(self, nc):
        with nc.Block() as b:
            for eng in ENGINES:
                fns = self.ops[eng]
                if fns:
                    getattr(b, eng)(lambda e, fns=fns: [f(e) for f in fns])


class Ctr:
    def __init__(self):
        self.v = {}

    def add(self, k, n=1):
        self.v[k] = self.v.get(k, 0) + n
        return self.v[k]

    def val(self, k):
        return self.v.get(k, 0)


def build(S=4096, debug=False):
    NCH = S // 128
    NSUP = S // 512
    NTQ = max(1, S // 1024)
    TQT = S // NTQ
    TQC = TQT // 128

    nc = bass.Bass("TRN2", target_bir_lowering=False)

    x_d = nc.dram_tensor("x", [S, D], F32, kind="ExternalInput").ap()
    wq_d = nc.dram_tensor("wq", [D, D], F32, kind="ExternalInput").ap()
    wk_d = nc.dram_tensor("wk", [D, D], F32, kind="ExternalInput").ap()
    wv_d = nc.dram_tensor("wv", [D, D], F32, kind="ExternalInput").ap()
    wo_d = nc.dram_tensor("wo", [D, D], F32, kind="ExternalInput").ap()
    ln1_d = nc.dram_tensor("ln1", [128, D], F32, kind="ExternalInput").ap()
    ln2_d = nc.dram_tensor("ln2", [128, D], F32, kind="ExternalInput").ap()
    fwq_d = nc.dram_tensor("fwq", [D, D], F32, kind="ExternalInput").ap()
    fwk_d = nc.dram_tensor("fwk", [D, 512], F32, kind="ExternalInput").ap()
    fwv_d = nc.dram_tensor("fwv", [D, 512], F32, kind="ExternalInput").ap()
    fwo_d = nc.dram_tensor("fwo", [D, D], F32, kind="ExternalInput").ap()
    rw_d = nc.dram_tensor("rw", [D, NE], F32, kind="ExternalInput").ap()
    wg_d = nc.dram_tensor("wgate", [NE, D, M], F32, kind="ExternalInput").ap()
    wu_d = nc.dram_tensor("wup", [NE, D, M], F32, kind="ExternalInput").ap()
    wd_d = nc.dram_tensor("wdown", [NE, M, D], F32, kind="ExternalInput").ap()
    identb_d = nc.dram_tensor("identb", [128, 128], BF16, kind="ExternalInput").ap()
    identf_d = nc.dram_tensor("identf", [128, 128], F32, kind="ExternalInput").ap()
    onesb_d = nc.dram_tensor("onesb", [128, 128], BF16, kind="ExternalInput").ap()
    cs_d = nc.dram_tensor("cossin", [2, S, NH, 32], F32, kind="ExternalInput").ap()
    mask_d = nc.dram_tensor("mask01", [4, 128, 512], BF16, kind="ExternalInput").ap()
    out_d = nc.dram_tensor("out", [S, D], F32, kind="ExternalOutput").ap()

    skind = "ExternalOutput" if debug else "Internal"
    x1_d = nc.dram_tensor("x1s", [S, D], F32, kind=skind).ap()
    y1_d = nc.dram_tensor("y1s", [S, D], F32, kind=skind).ap()
    aout_d = nc.dram_tensor("aouts", [NSUP, 128, NH // 2, 512], BF16,
                            kind="Internal").ap()
    h2t_d = nc.dram_tensor("h2ts", [DC, 128, S], BF16, kind=skind).ap()

    C = Ctr()
    es = contextlib.ExitStack()
    with es:
        sem = {}
        for name in ("sx0", "sx1", "sa0", "sa1", "shq", "sgu0", "sgu1",
                     "swd0", "swd1", "so0", "so1", "sout", "swg", "saux", "snc",
                     "tmm", "tmv", "tbb", "tslc", "vev", "vmk", "ssc",
                     "tmoe", "tdown", "ssil", "vmoe", "vdev", "vz"):
            sem[name] = es.enter_context(nc.semaphore(name))

        identb = es.enter_context(nc.sbuf_tensor("identb_s", [128, 128], BF16))
        identf = es.enter_context(nc.sbuf_tensor("identf_s", [128, 128], F32))
        onesb = es.enter_context(nc.sbuf_tensor("onesb_s", [128, 128], BF16))
        maskb = es.enter_context(nc.sbuf_tensor("maskb", [128, 4, 512], BF16))
        wfull = es.enter_context(nc.sbuf_tensor("wfull", [128, NTQ * NE * TQC], F32))

        ph = Phase("P0")

        def _p0(e):
            e.dma_start(identb[:], identb_d).then_inc(sem["saux"], 16)
            e.dma_start(identf[:], identf_d).then_inc(sem["saux"], 16)
            e.dma_start(onesb[:], onesb_d).then_inc(sem["saux"], 16)
            e.dma_start(maskb[:], mask_d.rearrange("a p q -> p a q")
                        ).then_inc(sem["saux"], 16)
        ph.em("sync", _p0)
        C.add("saux", 64)

        def _p0w(e, w=C.val("saux")):
            e.wait_ge(sem["saux"], w)
        ph.em("vector", _p0w)
        ph.play(nc)

        def transposes(ph, pT, src_ap_of, dst_ap_of, n,
                       extra_tensor_wait=None, extra_vector_wait=None,
                       data_wait=None):
            if data_wait is None:
                data_wait = ("vev", C.val("vev"))
            for j in range(n):
                pt = pT[j % 2]

                def _tr(e, pt=pt, j=j, dw=data_wait,
                        wwar=(C.val("vev") - 1 if j >= 2 else 0),
                        src=src_ap_of(j),
                        xw=extra_tensor_wait if j == 0 else None):
                    if xw is not None:
                        xw(e)
                    if j == 0 and dw[1] > 0:
                        e.wait_ge(sem[dw[0]], dw[1])
                    if wwar > 0:
                        e.wait_ge(sem["vev"], wwar)
                    e.transpose(pt[:], src, identb[:]).then_inc(sem["tmm"], 1)
                ph.em("tensor", _tr)
                C.add("tmm")

                def _ev(e, pt=pt, wt=C.val("tmm"), dst=dst_ap_of(j),
                        xw=extra_vector_wait if j == 0 else None):
                    if xw is not None:
                        xw(e)
                    e.wait_ge(sem["tmm"], wt)
                    e.tensor_copy(dst, pt[:]).then_inc(sem["vev"], 1)
                ph.em("vector", _ev)
                C.add("vev")

        def emit_attention(name, src_x_d, nqh, kv_of, head0, weights_mode,
                           ln_rms=False, ln1r=None):
            ph = Phase(name)
            st = contextlib.ExitStack()
            QCOL = nqh * HD
            QHG = nqh // 2
            QPAD = (nqh != NKV)
            NQS = 16 if QPAD else QHG
            PH_SOUT0 = (C.val("so0"), C.val("so1"), C.val("sout"))

            xsb = st.enter_context(nc.sbuf_tensor(name + "_xsb", [128, 2, D], F32))
            csb = st.enter_context(nc.sbuf_tensor(name + "_csb", [128, 2, 2, 16, 32], F32))
            xcb = st.enter_context(nc.sbuf_tensor(name + "_xcb", [128, 2, D], F32))
            sqd = st.enter_context(nc.sbuf_tensor(name + "_sqd", [128, D], F32))
            hsc = st.enter_context(nc.sbuf_tensor(name + "_hsc", [128, D], F32))
            nsq = st.enter_context(nc.sbuf_tensor(name + "_nsq", [128, 1536], F32))
            nst = st.enter_context(nc.sbuf_tensor(name + "_nst", [128, 32], F32))
            stt = st.enter_context(nc.sbuf_tensor(name + "_stt", [128, 16], F32))
            hb = st.enter_context(nc.sbuf_tensor(name + "_hb", [128, D], BF16))
            hTb = st.enter_context(nc.sbuf_tensor(name + "_hTb", [128, DC, 128], BF16))
            qrb = st.enter_context(nc.sbuf_tensor(name + "_qrb", [128, QCOL], BF16))
            krb = st.enter_context(nc.sbuf_tensor(name + "_krb", [128, 512], BF16))
            rtmp = st.enter_context(nc.sbuf_tensor(name + "_rtmp", [128, 4, 16, 32], F32))
            qTb = st.enter_context(nc.sbuf_tensor(name + "_qTb", [128, NQS, 512], BF16))
            KTb = st.enter_context(nc.sbuf_tensor(name + "_KTb", [128, 4, S], BF16))
            Vb = st.enter_context(nc.sbuf_tensor(name + "_Vb", [128, NCH, NKV, 65], BF16))
            expb = st.enter_context(nc.sbuf_tensor(name + "_expb", [128, 4, 512], BF16))
            aoutb = st.enter_context(nc.sbuf_tensor(name + "_aoutb", [128, QHG, 512], BF16))
            rbf = st.enter_context(nc.sbuf_tensor(name + "_rbf", [128, 512], F32))
            rib = st.enter_context(nc.sbuf_tensor(name + "_rib", [128, 512], BF16))
            qsqm = st.enter_context(nc.sbuf_tensor(name + "_qsqm", [128, nqh], F32))
            ksqm = st.enter_context(nc.sbuf_tensor(name + "_ksqm", [128, NKV], F32))
            ksqe = st.enter_context(nc.sbuf_tensor(name + "_ksqe", [128, 16], F32))
            qmx = st.enter_context(nc.sbuf_tensor(name + "_qmx", [16, 1], F32))
            kmx = st.enter_context(nc.sbuf_tensor(name + "_kmx", [16, 1], F32))
            ccol = st.enter_context(nc.sbuf_tensor(name + "_ccol", [16, 1], F32))
            ccs = st.enter_context(nc.sbuf_tensor(name + "_ccs", [16, 1], F32))
            crow = st.enter_context(nc.sbuf_tensor(name + "_crow", [1, 16], BF16))
            negCb = st.enter_context(nc.sbuf_tensor(name + "_negC", [128, 16], F32))
            if weights_mode == "transposed":
                wqT = st.enter_context(nc.sbuf_tensor(name + "_wqT", [128, DC, 512], BF16))
                wkT = st.enter_context(nc.sbuf_tensor(name + "_wkT", [128, DC, 512], BF16))
                wvT = st.enter_context(nc.sbuf_tensor(name + "_wvT", [128, DC, 512], BF16))
                wtmp = st.enter_context(nc.sbuf_tensor(name + "_wtmp", [128, 4, D], BF16))
            else:
                fwqb = st.enter_context(nc.sbuf_tensor(name + "_fwq", [128, DC, D], BF16))
                fwkb = st.enter_context(nc.sbuf_tensor(name + "_fwk", [128, DC, 512], BF16))
                fwvb = st.enter_context(nc.sbuf_tensor(name + "_fwv", [128, DC, 512], BF16))
            pQV = st.enter_context(nc.psum_tensor(name + "_pQV", [128, 512], F32))
            pK = st.enter_context(nc.psum_tensor(name + "_pK", [128, 512], F32))
            pT0 = st.enter_context(nc.psum_tensor(name + "_pT0", [128, 128], BF16))
            pT1 = st.enter_context(nc.psum_tensor(name + "_pT1", [128, 128], BF16))
            psT = st.enter_context(nc.psum_tensor(name + "_psT", [128, 2, 512], F32))
            poT = st.enter_context(nc.psum_tensor(name + "_poT", [128, 512], F32))
            pbc = st.enter_context(nc.psum_tensor(name + "_pbc", [128, 512], F32))
            pT = (pT0, pT1)

            def score_slot(kt):
                j = kt % 4
                if j == 0:
                    return psT[:, 0, :]
                if j == 1:
                    return psT[:, 1, :]
                if j == 2:
                    return pQV[:]
                return pK[:]

            if weights_mode == "transposed":
                for dst, src in ((wqT, wq_d), (wkT, wk_d), (wvT, wv_d)):
                    def _ld(e, src=src, o0=head0 * HD, wt=C.val("tmm")):
                        if wt > 0:
                            e.wait_ge(sem["tmm"], wt)
                        e.dma_start(wtmp[:], src[o0:o0 + 512, :]
                                    .rearrange("(c p) i -> p c i", p=128)
                                    ).then_inc(sem["swg"], 16)
                    ph.em("gpsimd", _ld)
                    C.add("swg", 16)
                    wsw = C.val("swg")
                    for oc in range(4):
                        def xw(e, wsw=wsw):
                            e.wait_ge(sem["swg"], wsw)
                        transposes(
                            ph, pT,
                            lambda j, oc=oc: wtmp[:, oc, j * 128:(j + 1) * 128],
                            lambda j, oc=oc, dst=dst: dst[:, j, oc * 128:(oc + 1) * 128],
                            DC, extra_tensor_wait=xw)
            else:
                for dst, src in ((fwqb, fwq_d), (fwkb, fwk_d), (fwvb, fwv_d)):
                    def _ld(e, dst=dst, src=src):
                        e.dma_start(dst[:], src.rearrange("(c p) o -> p c o", p=128)
                                    ).then_inc(sem["swg"], 16)
                    ph.em("gpsimd", _ld)
                    C.add("swg", 16)

            def _vones(e):
                e.memset(Vb[:], 1.0).then_inc(sem["vev"], 1)
                if QPAD:
                    e.memset(qTb[:], 0.0).then_inc(sem["vev"], 1)
            ph.em("vector", _vones)
            C.add("vev", 2 if QPAD else 1)
            W_SWG = C.val("swg")

            xsb_free = [0, 0]
            csb_free = [0, 0]
            stt_war = [0, 0]
            hb_war = [0]
            hsc_war = [0]

            def getq(j, oh):
                return (wqT if weights_mode == "transposed" else fwqb)[
                    :, j, oh * 512:(oh + 1) * 512]

            def getk(j):
                return (wkT if weights_mode == "transposed" else fwkb)[:, j, :]

            def getv(j):
                return (wvT if weights_mode == "transposed" else fwvb)[:, j, :]

            def newton(e, stt):
                e.tensor_tensor(stt[:, 6:7], stt[:, 5:6], stt[:, 5:6], ALU.mult)
                e.tensor_tensor(stt[:, 6:7], stt[:, 6:7], stt[:, 3:4], ALU.mult)
                e.tensor_scalar(stt[:, 6:7], stt[:, 6:7], -0.5, 1.5, ALU.mult, ALU.add)
                e.tensor_tensor(stt[:, 6:7], stt[:, 6:7], stt[:, 5:6], ALU.mult)

            def emit_A1_chunk(s, c4):
                c = s * 4 + c4
                slot = c % 2
                po = 8 * slot
                sxs = f"sx{slot}"

                def _ldx(e, c=c, slot=slot, wx=xsb_free[slot], wc=csb_free[slot],
                         sxs=sxs):
                    if wx > 0:
                        e.wait_ge(sem["ssc"], wx)
                    e.dma_start(xsb[:, slot, :], src_x_d[c * 128:(c + 1) * 128, :]
                                ).then_inc(sem[sxs], 16)
                    if wc > 0:
                        e.wait_ge(sem["vev"], wc)
                    e.dma_start(csb[:, slot, :, 0:nqh, :],
                                cs_d[:, c * 128:(c + 1) * 128, head0:head0 + nqh, :]
                                .rearrange("a p h f -> p a h f")
                                ).then_inc(sem[sxs], 16)
                ph.em("sync", _ldx)
                C.add(sxs, 32)

                def _red(e, slot=slot, po=po, wx=C.val(sxs), ws=stt_war[slot],
                         sxs=sxs):
                    e.wait_ge(sem[sxs], wx)
                    if ws > 0:
                        e.wait_ge(sem["ssc"], ws)
                    e.reduce_sum(stt[:, po:po + 1], xsb[:, slot, :],
                                 axis=AX.X).then_inc(sem["vev"], 1)
                ph.em("vector", _red)
                C.add("vev")

                def _chain(e, slot=slot, po=po, wv=C.val("vev"), wt=hb_war[0],
                           wh=hsc_war[0], b1=C.val("ssc") + 1, rms=ln_rms):
                    e.wait_ge(sem["vev"], wv)

                    def s_(i):
                        return stt[:, po + i:po + i + 1]
                    xc = xcb[:, slot, :]
                    e.activation(s_(1), s_(0), AF.Copy,
                                 scale=-1.0 / D).then_inc(sem["ssc"], 1)
                    e.wait_ge(sem["ssc"], b1)
                    e.add(xc, xsb[:, slot, :], s_(1))
                    e.activation(sqd[:], xc, AF.Square, accum_out=s_(2))
                    e.activation(s_(3), s_(2), AF.Copy, scale=1.0 / D, bias=EPS_LN)
                    e.activation(s_(4), s_(3), AF.Ln)
                    e.activation(s_(5), s_(4), AF.Exp,
                                 scale=-0.5).then_inc(sem["ssc"], 1)
                    e.wait_ge(sem["ssc"], b1 + 1)
                    e.mul(s_(6), s_(5), s_(5))
                    e.mul(s_(6), s_(6), s_(3))
                    e.activation(s_(6), s_(6), AF.Copy, scale=-0.5, bias=1.5)
                    e.mul(s_(6), s_(6), s_(5)).then_inc(sem["ssc"], 1)
                    e.wait_ge(sem["ssc"], b1 + 2)
                    if not rms:
                        if wt > 0:
                            e.wait_ge(sem["tmm"], wt)
                        e.mul(hb[:], xc, s_(6)).then_inc(sem["ssc"], 1)
                    else:
                        e.mul(xc, xc, s_(6))
                        e.activation(sqd[:], xc, AF.Square, accum_out=s_(2))
                        e.activation(s_(3), s_(2), AF.Copy, scale=1.0 / D,
                                     bias=EPS_RMS)
                        e.activation(s_(4), s_(3), AF.Ln)
                        e.activation(s_(5), s_(4), AF.Exp,
                                     scale=-0.5).then_inc(sem["ssc"], 1)
                        e.wait_ge(sem["ssc"], b1 + 3)
                        e.mul(s_(6), s_(5), s_(5))
                        e.mul(s_(6), s_(6), s_(3))
                        e.activation(s_(6), s_(6), AF.Copy, scale=-0.5, bias=1.5)
                        e.mul(s_(6), s_(6), s_(5)).then_inc(sem["ssc"], 1)
                        e.wait_ge(sem["ssc"], b1 + 4)
                        if wh > 0:
                            e.wait_ge(sem["vev"], wh)
                        e.mul(hsc[:], xc, s_(6)).then_inc(sem["ssc"], 1)
                ph.em("scalar", _chain)
                C.add("ssc", 4 if not ln_rms else 6)
                stt_war[slot] = C.val("ssc")
                xsb_free[slot] = C.val("ssc")

                if ln_rms:
                    def _hTT(e, ws=C.val("ssc"), wt=hb_war[0]):
                        e.wait_ge(sem["ssc"], ws)
                        if wt > 0:
                            e.wait_ge(sem["tmm"], wt)
                        e.tensor_tensor(hb[:], hsc[:], ln1r[:],
                                        ALU.mult).then_inc(sem["vev"], 1)
                    ph.em("vector", _hTT)
                    C.add("vev")
                    hsc_war[0] = C.val("vev")
                    hb_ready = ("vev", C.val("vev"))
                else:
                    hb_ready = ("ssc", C.val("ssc"))

                transposes(ph, pT,
                           lambda j: hb[:, j * 128:(j + 1) * 128],
                           lambda j: hTb[:, j, :], DC, data_wait=hb_ready)
                hb_war[0] = C.val("tmm")

                for oh in range(QCOL // 512):
                    def _qmm(e, oh=oh, wv=C.val("vev"), ws=W_SWG, wsc=C.val("ssc")):
                        e.wait_ge(sem["swg"], ws)
                        e.wait_ge(sem["vev"], wv)
                        e.wait_ge(sem["ssc"], wsc)
                        for j in range(DC):
                            mm = e.matmul(pQV[:], hTb[:, j, :], getq(j, oh),
                                          start=(j == 0), stop=(j == DC - 1))
                        mm.then_inc(sem["tmm"], 1)
                    ph.em("tensor", _qmm)
                    C.add("tmm")

                    def _rope(e, oh=oh, wt=C.val("tmm"), slot=slot,
                              bar=C.val("vev") + 1):
                        e.wait_ge(sem["tmm"], wt)
                        h0 = oh * 8
                        q1 = bass.AP(pQV, 0, [[512, 128], [64, 8], [1, 32]])
                        q2 = bass.AP(pQV, 32, [[512, 128], [64, 8], [1, 32]])
                        cosa = csb[:, slot, 0, h0:h0 + 8, :]
                        sina = csb[:, slot, 1, h0:h0 + 8, :]
                        e.tensor_tensor(rtmp[:, 0, 0:8, :], q1, cosa, ALU.mult)
                        e.tensor_tensor(rtmp[:, 1, 0:8, :], q2, sina, ALU.mult)
                        e.tensor_tensor(rtmp[:, 2, 0:8, :], q2, cosa, ALU.mult)
                        e.tensor_tensor(rtmp[:, 3, 0:8, :], q1, sina,
                                        ALU.mult).then_inc(sem["vev"], 1)
                        e.wait_ge(sem["vev"], bar)
                        e.tensor_tensor(
                            bass.AP(qrb, oh * 512, [[QCOL, 128], [64, 8], [1, 32]]),
                            rtmp[:, 0, 0:8, :], rtmp[:, 1, 0:8, :], ALU.subtract)
                        e.tensor_tensor(
                            bass.AP(qrb, oh * 512 + 32, [[QCOL, 128], [64, 8], [1, 32]]),
                            rtmp[:, 2, 0:8, :], rtmp[:, 3, 0:8, :],
                            ALU.add).then_inc(sem["vev"], 1)
                    ph.em("vector", _rope)
                    C.add("vev", 2)

                def _kmm(e, ws=W_SWG):
                    for j in range(DC):
                        mm = e.matmul(pK[:], hTb[:, j, :], getk(j),
                                      start=(j == 0), stop=(j == DC - 1))
                    mm.then_inc(sem["tmm"], 1)
                ph.em("tensor", _kmm)
                C.add("tmm")

                def _ropek(e, wt=C.val("tmm"), slot=slot, bar=C.val("vev") + 1):
                    e.wait_ge(sem["tmm"], wt)
                    k1 = bass.AP(pK, 0, [[512, 128], [64, NKV], [1, 32]])
                    k2 = bass.AP(pK, 32, [[512, 128], [64, NKV], [1, 32]])
                    cosa = csb[:, slot, 0, 0:NKV, :]
                    sina = csb[:, slot, 1, 0:NKV, :]
                    e.tensor_tensor(rtmp[:, 0, 0:NKV, :], k1, cosa, ALU.mult)
                    e.tensor_tensor(rtmp[:, 1, 0:NKV, :], k2, sina, ALU.mult)
                    e.tensor_tensor(rtmp[:, 2, 0:NKV, :], k2, cosa, ALU.mult)
                    e.tensor_tensor(rtmp[:, 3, 0:NKV, :], k1, sina,
                                    ALU.mult).then_inc(sem["vev"], 1)
                    e.wait_ge(sem["vev"], bar)
                    e.tensor_tensor(bass.AP(krb, 0, [[512, 128], [64, NKV], [1, 32]]),
                                    rtmp[:, 0, 0:NKV, :], rtmp[:, 1, 0:NKV, :],
                                    ALU.subtract)
                    e.tensor_tensor(bass.AP(krb, 32, [[512, 128], [64, NKV], [1, 32]]),
                                    rtmp[:, 2, 0:NKV, :], rtmp[:, 3, 0:NKV, :],
                                    ALU.add).then_inc(sem["vev"], 1)
                ph.em("vector", _ropek)
                C.add("vev", 2)
                csb_free[slot] = C.val("vev")

                def _norms(e, c=c, c4=c4, w0=C.val("vev"), bar=C.val("vev") + 1):
                    e.wait_ge(sem["vev"], w0)
                    e.tensor_tensor(nsq[:, 0:QCOL], qrb[:], qrb[:], ALU.mult)
                    e.tensor_tensor(nsq[:, 1024:1536], krb[:], krb[:],
                                    ALU.mult).then_inc(sem["vev"], 1)
                    e.wait_ge(sem["vev"], bar)
                    e.tensor_reduce(nst[:, 0:nqh],
                                    bass.AP(nsq, 0, [[1536, 128], [64, nqh], [1, 64]]),
                                    axis=AX.X, op=ALU.add)
                    e.tensor_reduce(nst[:, 16:16 + NKV],
                                    bass.AP(nsq, 1024, [[1536, 128], [64, NKV], [1, 64]]),
                                    axis=AX.X, op=ALU.add).then_inc(sem["vev"], 1)
                    e.wait_ge(sem["vev"], bar + 1)
                    if c4 == 0:
                        e.tensor_copy(qsqm[:], nst[:, 0:nqh])
                    else:
                        e.tensor_tensor(qsqm[:], qsqm[:], nst[:, 0:nqh], ALU.max)
                    if c == 0:
                        e.tensor_copy(ksqm[:], nst[:, 16:16 + NKV]
                                      ).then_inc(sem["vev"], 1)
                    else:
                        e.tensor_tensor(ksqm[:], ksqm[:], nst[:, 16:16 + NKV],
                                        ALU.max).then_inc(sem["vev"], 1)
                ph.em("vector", _norms)
                C.add("vev", 3)

                if not QPAD:
                    transposes(ph, pT,
                               lambda j: qrb[:, j * 128:(j + 1) * 128],
                               lambda j, c4=c4: qTb[:, j, c4 * 128:(c4 + 1) * 128],
                               QCOL // 128)
                else:
                    first_wait = C.val("vev")
                    for j in range(QCOL // 128):
                        pt = pT[j % 2]

                        def _trq(e, pt=pt, j=j, w0=first_wait,
                                 wwar=(C.val("vev") - 1 if j >= 2 else 0),
                                 src=qrb[:, j * 128:(j + 1) * 128]):
                            if j == 0 and w0 > 0:
                                e.wait_ge(sem["vev"], w0)
                            if wwar > 0:
                                e.wait_ge(sem["vev"], wwar)
                            e.transpose(pt[:], src, identb[:]).then_inc(sem["tmm"], 1)
                        ph.em("tensor", _trq)
                        C.add("tmm")

                        def _evq(e, pt=pt, j=j, wt=C.val("tmm"), c4=c4):
                            e.wait_ge(sem["tmm"], wt)
                            hA = PERM_Q[2 * j]
                            hB = PERM_Q[2 * j + 1]
                            e.tensor_copy(qTb[0:64, hA, c4 * 128:(c4 + 1) * 128],
                                          pt[0:64, :])
                            e.tensor_copy(qTb[64:128, hB, c4 * 128:(c4 + 1) * 128],
                                          pt[64:128, :]).then_inc(sem["vev"], 1)
                        ph.em("vector", _evq)
                        C.add("vev")
                transposes(ph, pT,
                           lambda j: krb[:, j * 128:(j + 1) * 128],
                           lambda j, c=c: KTb[:, j, c * 128:(c + 1) * 128], 4)

                def _vmm(e, wv=C.val("vev"), ws=W_SWG):
                    e.wait_ge(sem["vev"], wv)
                    for j in range(DC):
                        mm = e.matmul(pQV[:], hTb[:, j, :], getv(j),
                                      start=(j == 0), stop=(j == DC - 1))
                    mm.then_inc(sem["tmm"], 1)
                ph.em("tensor", _vmm)
                C.add("tmm")

                def _vev(e, wt=C.val("tmm"), c=c):
                    e.wait_ge(sem["tmm"], wt)
                    e.tensor_copy(
                        bass.AP(Vb, c * NKV * 65,
                                [[NCH * NKV * 65, 128], [65, NKV], [1, 64]]),
                        pQV[:]).then_inc(sem["vev"], 1)
                ph.em("vector", _vev)
                C.add("vev")

            def emit_negC(s):
                def _t1(e, wv=C.val("vev")):
                    e.wait_ge(sem["vev"], wv)
                    e.transpose(bass.AP(pbc, 0, [[512, nqh], [1, 128]]),
                                qsqm[:], identf[:]).then_inc(sem["tmm"], 1)
                ph.em("tensor", _t1)
                C.add("tmm")

                def _r1(e, wt=C.val("tmm")):
                    e.wait_ge(sem["tmm"], wt)
                    e.reduce_max(qmx[0:nqh, :],
                                 bass.AP(pbc, 0, [[512, nqh], [1, 128]]),
                                 axis=AX.X).then_inc(sem["vev"], 1)
                ph.em("vector", _r1)
                C.add("vev")

                if QPAD:
                    def _kx(e, w0=C.val("vev")):
                        e.wait_ge(sem["vev"], w0)
                        for r in range(4):
                            mm = e.tensor_copy(
                                bass.AP(ksqe, r, [[16, 128], [4, 4]]),
                                bass.AP(ksqm, r % 2, [[NKV, 128], [2, 4]]))
                        mm.then_inc(sem["vev"], 1)
                    ph.em("vector", _kx)
                    C.add("vev")
                    ktr, nkc = ksqe, 16
                else:
                    ktr, nkc = ksqm, NKV

                def _t2(e, wv=C.val("vev"), ktr=ktr, nkc=nkc):
                    e.wait_ge(sem["vev"], wv)
                    e.transpose(bass.AP(pbc, 0, [[512, nkc], [1, 128]]),
                                ktr[:], identf[:]).then_inc(sem["tmm"], 1)
                ph.em("tensor", _t2)
                C.add("tmm")

                def _r2(e, wt=C.val("tmm"), nkc=nkc):
                    e.wait_ge(sem["tmm"], wt)
                    e.reduce_max(kmx[0:nkc, :],
                                 bass.AP(pbc, 0, [[512, nkc], [1, 128]]),
                                 axis=AX.X).then_inc(sem["vev"], 1)
                ph.em("vector", _r2)
                C.add("vev")

                def _cc(e, wv=C.val("vev"), bar=C.val("vev") + 1):
                    e.wait_ge(sem["vev"], wv)
                    e.tensor_tensor(ccol[0:nqh, :], qmx[0:nqh, :], kmx[0:nqh, :],
                                    ALU.add).then_inc(sem["vev"], 1)
                    e.wait_ge(sem["vev"], bar)
                    e.tensor_scalar(ccs[0:nqh, :], ccol[0:nqh, :], -1.0 / 16.0,
                                    None, ALU.mult).then_inc(sem["vev"], 1)
                ph.em("vector", _cc)
                C.add("vev", 2)

                def _tc(e, wv=C.val("vev")):
                    e.wait_ge(sem["vev"], wv)
                    e.transpose(bass.AP(pbc, 0, [[512, 1], [1, nqh]]),
                                ccs[0:nqh, :],
                                identf[0:nqh, 0:nqh]).then_inc(sem["tmm"], 1)
                ph.em("tensor", _tc)
                C.add("tmm")

                def _cv(e, wt=C.val("tmm")):
                    e.wait_ge(sem["tmm"], wt)
                    e.tensor_copy(crow[:, 0:nqh],
                                  bass.AP(pbc, 0, [[512, 1], [1, nqh]])
                                  ).then_inc(sem["vev"], 1)
                ph.em("vector", _cv)
                C.add("vev")

                def _bc(e, wv=C.val("vev")):
                    e.wait_ge(sem["vev"], wv)
                    e.matmul(bass.AP(pbc, 0, [[512, 128], [1, nqh]]),
                             onesb[0:1, :], crow[:, 0:nqh],
                             start=True, stop=True).then_inc(sem["tmm"], 1)
                ph.em("tensor", _bc)
                C.add("tmm")

                def _ev(e, wt=C.val("tmm")):
                    e.wait_ge(sem["tmm"], wt)
                    e.tensor_copy(negCb[:, 0:nqh],
                                  bass.AP(pbc, 0, [[512, 128], [1, nqh]])
                                  ).then_inc(sem["vev"], 1)
                ph.em("vector", _ev)
                C.add("vev")

            def emit_A2_head(s, h8, first_of_phase, negc_done, prev_tbc):
                nkt = 4 * (s + 1)
                NG = s + 1
                kvh = kv_of(h8)
                poff = (h8 % 2) * 64
                kpoff = (kvh % 2) * 64
                hg = h8 // 2
                khg = kvh // 2
                A1_VEV = C.val("vev")
                ssc0 = C.val("ssc")
                tmm0 = C.val("tmm")
                tmv0 = C.val("tmv")
                vmk0 = C.val("vmk")
                sout0 = PH_SOUT0 if first_of_phase else None

                if QPAD:
                    def sc_mm(e, kt):
                        e.matmul(score_slot(kt),
                                 KTb[:, khg, kt * 128:(kt + 1) * 128],
                                 qTb[:, h8, :],
                                 start=True, stop=True).then_inc(sem["tmm"], 1)
                else:
                    def sc_mm(e, kt):
                        e.matmul(score_slot(kt),
                                 KTb[kpoff:kpoff + 64, khg, kt * 128:(kt + 1) * 128],
                                 qTb[poff:poff + 64, hg, :],
                                 start=True, stop=True).then_inc(sem["tmm"], 1)

                def av_mm(e, kt):
                    e.matmul(bass.AP(poT, 0, [[512, 65], [1, 512]]),
                             Vb[:, kt, kvh, :], expb[:, kt % 4, :],
                             start=(kt == 0), stop=(kt == nkt - 1),
                             skip_group_check=True).then_inc(sem["tmv"], 1)

                GPRE = min(2, NG)

                def _ten_pre(e, GPRE=GPRE, ssc0=ssc0, h8=h8, negc_done=negc_done):
                    if h8 == 0:
                        e.wait_ge(sem["vev"], negc_done)
                    for g in range(GPRE):
                        for k4 in range(4):
                            w = ssc0 + 4 * (g - 1) + k4 + 1
                            if w > 0:
                                e.wait_ge(sem["ssc"], w)
                            sc_mm(e, 4 * g + k4)
                ph.em("tensor", _ten_pre)
                C.add("tmm", 4 * GPRE)

                if prev_tbc is not None:
                    ph.em("tensor", prev_tbc)

                def _ten_rest(e, nkt=nkt, NG=NG, GPRE=GPRE, ssc0=ssc0,
                              vmk0=vmk0, A1_VEV=A1_VEV):
                    e.wait_ge(sem["vev"], A1_VEV)
                    if NG >= 2:
                        for k4 in range(4):
                            e.wait_ge(sem["ssc"], ssc0 + k4 + 1)
                            av_mm(e, k4)
                        for g in range(GPRE, NG):
                            for k4 in range(4):
                                e.wait_ge(sem["ssc"], ssc0 + 4 * (g - 1) + k4 + 1)
                                sc_mm(e, 4 * g + k4)
                                av_mm(e, 4 * (g - 1) + k4)
                    for k4 in range(4):
                        e.wait_ge(sem["vmk"], vmk0 + k4 + 1)
                        av_mm(e, nkt - 4 + k4)
                ph.em("tensor", _ten_rest)
                C.add("tmm", 4 * (NG - GPRE))
                C.add("tmv", nkt)

                def _sca(e, nkt=nkt, NG=NG, tmm0=tmm0, tmv0=tmv0, h8=h8,
                         wnc=negc_done):
                    e.wait_ge(sem["vev"], wnc)
                    for g in range(NG):
                        for k4 in range(4):
                            kt = 4 * g + k4
                            wv = tmv0 + 4 * (g - 1) + k4 + 1
                            if wv > 0:
                                e.wait_ge(sem["tmv"], wv)
                            e.wait_ge(sem["tmm"], tmm0 + kt + 1)
                            bc = POS_OF[h8] if QPAD else h8
                            e.activation(expb[:, kt % 4, :], score_slot(kt), AF.Exp,
                                         bias=negCb[:, bc:bc + 1],
                                         scale=0.125).then_inc(sem["ssc"], 1)
                ph.em("scalar", _sca)
                C.add("ssc", nkt)

                def _msk(e, nkt=nkt, ssc0=ssc0):
                    e.wait_ge(sem["ssc"], ssc0 + nkt)
                    for dd in range(4):
                        kt = nkt - 4 + dd
                        e.tensor_tensor(expb[:, kt % 4, :], expb[:, kt % 4, :],
                                        maskb[:, dd, :], ALU.mult
                                        ).then_inc(sem["vmk"], 1)
                ph.em("vector", _msk)
                C.add("vmk", 4)

                def _tail(e, wt=C.val("tmv"), so=sout0, bar=C.val("vev") + 1):
                    if so is not None:
                        for sname, v in zip(("so0", "so1", "sout"), so):
                            if v > 0:
                                e.wait_ge(sem[sname], v)
                    e.wait_ge(sem["tmv"], wt)
                    e.reciprocal(rbf[64:65, :],
                                 bass.AP(poT, 64 * 512, [[512, 1], [1, 512]])
                                 ).then_inc(sem["vev"], 1)
                    e.wait_ge(sem["vev"], bar)
                    e.tensor_copy(rib[64:65, :], rbf[64:65, :]).then_inc(sem["vev"], 1)
                ph.em("vector", _tail)
                C.add("vev", 2)

                C.add("tbb")
                tb_t = C.val("tbb")

                def _tbc(e, wv=C.val("vev"), tb=tb_t):
                    e.wait_ge(sem["vev"], wv)
                    e.matmul(bass.AP(pbc, 0, [[512, 64], [1, 512]]),
                             onesb[64:65, 0:64], rib[64:65, :],
                             start=True, stop=True).then_inc(sem["tbb"], 1)

                def _tev(e, wt=tb_t, hg=hg, poff=poff, bar=C.val("vev") + 1):
                    e.wait_ge(sem["tbb"], wt)
                    e.tensor_copy(rbf[0:64, :],
                                  bass.AP(pbc, 0, [[512, 64], [1, 512]])
                                  ).then_inc(sem["vev"], 1)
                    e.wait_ge(sem["vev"], bar)
                    e.tensor_tensor(
                        bass.AP(aoutb, poff * (QHG * 512) + hg * 512,
                                [[QHG * 512, 64], [1, 512]]),
                        bass.AP(poT, 0, [[512, 64], [1, 512]]),
                        rbf[0:64, :], ALU.mult).then_inc(sem["vev"], 1)
                ph.em("vector", _tev)
                C.add("vev", 2)
                return _tbc

            for s in range(NSUP):
                for c4 in range(4):
                    emit_A1_chunk(s, c4)
                emit_negC(s)
                NEGC_DONE = C.val("vev")
                pend = None
                for h8 in range(nqh):
                    pend = emit_A2_head(s, h8,
                                        first_of_phase=(s == 0 and h8 == 0),
                                        negc_done=NEGC_DONE, prev_tbc=pend)
                ph.em("tensor", pend)

                def _sp(e, s=s, wv=C.val("vev"), h0g=head0 // 2):
                    e.wait_ge(sem["vev"], wv)
                    e.dma_start(aout_d[s, :, h0g:h0g + QHG, :],
                                aoutb[:]).then_inc(sem["sout"], 16)
                ph.em("sync", _sp)
                C.add("sout", 16)
                if s + 1 < NSUP:
                    def _w(e, ws=C.val("sout")):
                        e.wait_ge(sem["sout"], ws)
                    ph.em("vector", _w)

            ph.play(nc)
            st.close()

        def emit_proj(name, w_src, w_is_transposed, resid_mode, resid_src_d, dst_d,
                      aout_ready_sout):
            ph = Phase(name)
            st = contextlib.ExitStack()
            wob = st.enter_context(nc.sbuf_tensor(name + "_wob", [128, DC, D], BF16))
            wtmp = st.enter_context(nc.sbuf_tensor(name + "_wtmp", [128, DC, D], BF16))
            aob = st.enter_context(nc.sbuf_tensor(name + "_aob", [128, 2, DC, 512], BF16))
            xab = st.enter_context(nc.sbuf_tensor(name + "_xab", [128, 2, D], F32))
            orb = st.enter_context(nc.sbuf_tensor(name + "_orb", [128, 2, D], F32))
            stt = st.enter_context(nc.sbuf_tensor(name + "_stt", [128, 16], F32))
            sqs = st.enter_context(nc.sbuf_tensor(name + "_sqs", [128, D], F32))
            sq2 = st.enter_context(nc.sbuf_tensor(name + "_sq2", [128, D], F32))
            xns = st.enter_context(nc.sbuf_tensor(name + "_xns", [128, D], F32))
            pP = st.enter_context(nc.psum_tensor(name + "_pP", [128, 2, 512], F32))
            pT0 = st.enter_context(nc.psum_tensor(name + "_pT0", [128, 128], BF16))
            pT1 = st.enter_context(nc.psum_tensor(name + "_pT1", [128, 128], BF16))
            pT = (pT0, pT1)

            def _ld(e):
                e.dma_start(wtmp[:], w_src.rearrange("(c p) i -> p c i", p=128)
                            ).then_inc(sem["swg"], 16)
            ph.em("gpsimd", _ld)
            C.add("swg", 16)
            wsw = C.val("swg")
            if w_is_transposed:
                for oc in range(DC):
                    def xw(e, wsw=wsw):
                        e.wait_ge(sem["swg"], wsw)
                    transposes(ph, pT,
                               lambda j, oc=oc: wtmp[:, oc, j * 128:(j + 1) * 128],
                               lambda j, oc=oc: wob[:, j, oc * 128:(oc + 1) * 128],
                               DC, extra_tensor_wait=xw)
                wsrcb = wob
            else:
                wsrcb = wtmp
            W_VEV = C.val("vev")
            W_SWG = C.val("swg")

            aob_war = [0, 0]
            orb_war = [0, 0]
            xab_war = [0, 0]
            stt_war = [0, 0]
            xns_war = [0]
            for s in range(NSUP):
                slot = s % 2

                sas = f"sa{slot}"

                def _lda(e, s=s, slot=slot, so=aout_ready_sout, wt=aob_war[slot],
                         sas=sas):
                    e.wait_ge(sem["sout"], so)
                    if wt > 0:
                        e.wait_ge(sem["tmm"], wt)
                    e.dma_start(aob[:, slot, :, :], aout_d[s]).then_inc(sem[sas], 16)
                ph.em("sync", _lda)
                C.add(sas, 16)

                for t4 in range(4):
                    c = s * 4 + t4
                    cslot = c % 2
                    sxs = f"sx{cslot}"

                    wsem = "vev" if resid_mode == "x" else "ssc"

                    def _ldx(e, c=c, cslot=cslot, wv=xab_war[cslot], sxs=sxs,
                             wsem=wsem):
                        if wv > 0:
                            e.wait_ge(sem[wsem], wv)
                        e.dma_start(xab[:, cslot, :],
                                    resid_src_d[c * 128:(c + 1) * 128, :]
                                    ).then_inc(sem[sxs], 16)
                    ph.em("sync", _ldx)
                    C.add(sxs, 16)

                    for dh in range(2):
                        def _mm(e, slot=slot, t4=t4, dh=dh, wsa=C.val(sas),
                                wv=W_VEV, ws=W_SWG, war=C.val("vev"), sas=sas):
                            e.wait_ge(sem[sas], wsa)
                            e.wait_ge(sem["swg"], ws)
                            if wv > 0:
                                e.wait_ge(sem["vev"], wv)
                            e.wait_ge(sem["vev"], war)
                            for hg in range(DC):
                                mm = e.matmul(pP[:, dh, :],
                                              aob[:, slot, hg, t4 * 128:(t4 + 1) * 128],
                                              wsrcb[:, hg, dh * 512:(dh + 1) * 512],
                                              start=(hg == 0), stop=(hg == DC - 1))
                            mm.then_inc(sem["tmm"], 1)
                        ph.em("tensor", _mm)
                        C.add("tmm")
                    aob_war[slot] = C.val("tmm")

                    if resid_mode == "x":
                        def _add(e, cslot=cslot, wt=C.val("tmm"), wx=C.val(sxs),
                                 wo=orb_war[cslot], sxs=sxs):
                            e.wait_ge(sem[sxs], wx)
                            e.wait_ge(sem["tmm"], wt)
                            if wo > 0:
                                e.wait_ge(sem[f"so{cslot}"], wo)
                            e.tensor_tensor(orb[:, cslot, 0:512], xab[:, cslot, 0:512],
                                            pP[:, 0, :], ALU.add)
                            e.tensor_tensor(orb[:, cslot, 512:1024],
                                            xab[:, cslot, 512:1024],
                                            pP[:, 1, :], ALU.add).then_inc(sem["vev"], 1)
                        ph.em("vector", _add)
                        C.add("vev")
                        xab_war[cslot] = C.val("vev")
                    else:
                        po = 8 * cslot

                        def _red(e, cslot=cslot, po=po, wx=C.val(sxs), sxs=sxs,
                                 ws=stt_war[cslot]):
                            e.wait_ge(sem[sxs], wx)
                            if ws > 0:
                                e.wait_ge(sem["ssc"], ws)
                            e.reduce_sum(stt[:, po:po + 1], xab[:, cslot, :],
                                         axis=AX.X).then_inc(sem["vev"], 1)
                        ph.em("vector", _red)
                        C.add("vev")

                        def _chain(e, cslot=cslot, po=po, wv=C.val("vev"),
                                   wh=xns_war[0], b1=C.val("ssc") + 1):
                            e.wait_ge(sem["vev"], wv)

                            def s_(i):
                                return stt[:, po + i:po + i + 1]
                            xa = xab[:, cslot, :]
                            e.activation(s_(1), s_(0), AF.Copy,
                                         scale=-1.0 / D).then_inc(sem["ssc"], 1)
                            e.wait_ge(sem["ssc"], b1)
                            e.add(sqs[:], xa, s_(1))
                            e.activation(sq2[:], sqs[:], AF.Square, accum_out=s_(2))
                            e.activation(s_(3), s_(2), AF.Copy, scale=1.0 / D,
                                         bias=EPS_LN)
                            e.activation(s_(4), s_(3), AF.Ln)
                            e.activation(s_(5), s_(4), AF.Exp,
                                         scale=-0.5).then_inc(sem["ssc"], 1)
                            e.wait_ge(sem["ssc"], b1 + 1)
                            e.mul(s_(6), s_(5), s_(5))
                            e.mul(s_(6), s_(6), s_(3))
                            e.activation(s_(6), s_(6), AF.Copy, scale=-0.5, bias=1.5)
                            e.mul(s_(6), s_(6), s_(5)).then_inc(sem["ssc"], 1)
                            e.wait_ge(sem["ssc"], b1 + 2)
                            if wh > 0:
                                e.wait_ge(sem["vev"], wh)
                            e.mul(xns[:], sqs[:], s_(6)).then_inc(sem["ssc"], 1)
                        ph.em("scalar", _chain)
                        C.add("ssc", 4)
                        stt_war[cslot] = C.val("ssc")
                        xab_war[cslot] = C.val("ssc")

                        def _fin(e, cslot=cslot, ws=C.val("ssc"), wt=C.val("tmm"),
                                 wo=orb_war[cslot]):
                            e.wait_ge(sem["ssc"], ws)
                            e.wait_ge(sem["tmm"], wt)
                            if wo > 0:
                                e.wait_ge(sem[f"so{cslot}"], wo)
                            e.tensor_tensor(orb[:, cslot, 0:512], xns[:, 0:512],
                                            pP[:, 0, :], ALU.add)
                            e.tensor_tensor(orb[:, cslot, 512:1024], xns[:, 512:1024],
                                            pP[:, 1, :], ALU.add).then_inc(sem["vev"], 1)
                        ph.em("vector", _fin)
                        C.add("vev")
                        xns_war[0] = C.val("vev")

                    def _st(e, c=c, cslot=cslot, wv=C.val("vev")):
                        e.wait_ge(sem["vev"], wv)
                        e.dma_start(dst_d[c * 128:(c + 1) * 128, :],
                                    orb[:, cslot, :]).then_inc(sem[f"so{cslot}"], 16)
                    ph.em("sync", _st)
                    C.add(f"so{cslot}", 16)
                    orb_war[cslot] = C.val(f"so{cslot}")
            ph.play(nc)
            st.close()

        def newton(e, stt):
            e.tensor_tensor(stt[:, 6:7], stt[:, 5:6], stt[:, 5:6], ALU.mult)
            e.tensor_tensor(stt[:, 6:7], stt[:, 6:7], stt[:, 3:4], ALU.mult)
            e.tensor_scalar(stt[:, 6:7], stt[:, 6:7], -0.5, 1.5, ALU.mult, ALU.add)
            e.tensor_tensor(stt[:, 6:7], stt[:, 6:7], stt[:, 5:6], ALU.mult)

        emit_attention("Ap0", x_d, 8, lambda h: h, 0, "transposed")
        emit_attention("Ap1", x_d, 8, lambda h: h, 8, "transposed")
        ASP1 = C.val("sout")
        emit_proj("A3", wo_d, True, "x", x_d, x1_d, ASP1)
        X1_DONE = (C.val("so0"), C.val("so1"))

        ln1r_g = es.enter_context(nc.sbuf_tensor("ln1r", [128, D], F32))
        ph = Phase("LD1")

        def _l1(e, wo=X1_DONE):
            e.wait_ge(sem["so0"], wo[0])
            e.wait_ge(sem["so1"], wo[1])
            e.dma_start(ln1r_g[:], ln1_d).then_inc(sem["saux"], 16)
        ph.em("sync", _l1)
        C.add("saux", 16)

        def _l1w(e, wa=C.val("saux")):
            e.wait_ge(sem["saux"], wa)
        ph.em("vector", _l1w)
        ph.play(nc)

        emit_attention("F1", x1_d, 16, lambda h: h // 2, 0, "natural",
                       ln_rms=True, ln1r=ln1r_g)
        FSP = C.val("sout")
        emit_proj("F1A3", fwo_d, False, "lnx1", x1_d, y1_d, FSP)
        Y1_DONE = (C.val("so0"), C.val("so1"))

        ph = Phase("F2pre")
        st = contextlib.ExitStack()
        ln2r = st.enter_context(nc.sbuf_tensor("ln2r", [128, D], F32))
        rwb = st.enter_context(nc.sbuf_tensor("rwb", [128, DC, NE], BF16))
        ysb = st.enter_context(nc.sbuf_tensor("f2p_ysb", [128, 2, D], F32))
        stt2 = st.enter_context(nc.sbuf_tensor("f2p_stt", [128, 24], F32))
        sqs2 = st.enter_context(nc.sbuf_tensor("f2p_sqs", [128, D], F32))
        h2b = st.enter_context(nc.sbuf_tensor("f2p_h2b", [128, D], BF16))
        h2T = st.enter_context(nc.sbuf_tensor("f2p_h2T", [128, 2, DC, 128], BF16))
        prb = st.enter_context(nc.sbuf_tensor("f2p_prb", [128, NE], F32))
        wtb = st.enter_context(nc.sbuf_tensor("f2p_wtb", [128, NE], F32))
        pT0 = st.enter_context(nc.psum_tensor("f2p_pT0", [128, 128], BF16))
        pT1 = st.enter_context(nc.psum_tensor("f2p_pT1", [128, 128], BF16))
        pR = st.enter_context(nc.psum_tensor("f2p_pR", [128, NE], F32))
        pT2 = (pT0, pT1)

        def _ld2(e, wo=Y1_DONE):
            e.wait_ge(sem["so0"], wo[0])
            e.wait_ge(sem["so1"], wo[1])
            e.dma_start(ln2r[:], ln2_d).then_inc(sem["saux"], 16)
        ph.em("sync", _ld2)
        C.add("saux", 16)
        LN2_AUX = C.val("saux")

        def _ldr(e):
            e.dma_start(rwb[:], rw_d.rearrange("(c p) e -> p c e", p=128)
                        ).then_inc(sem["swg"], 16)
        ph.em("gpsimd", _ldr)
        C.add("swg", 16)
        RW_SWG = C.val("swg")

        ysb_free = [0, 0]
        h2T_war = [0, 0]
        stt2_war = [0, 0]
        hsc_war2 = [0]
        for c in range(NCH):
            slot = c % 2
            tslot = c % 2
            po = 8 * slot
            sxs = f"sx{slot}"

            def _ldy(e, c=c, slot=slot, wv=ysb_free[slot], sxs=sxs):
                if wv > 0:
                    e.wait_ge(sem["ssc"], wv)
                e.dma_start(ysb[:, slot, :], y1_d[c * 128:(c + 1) * 128, :]
                            ).then_inc(sem[sxs], 16)
            ph.em("sync", _ldy)
            C.add(sxs, 16)

            def _chain(e, slot=slot, po=po, wx=C.val(sxs), wa=LN2_AUX, sxs=sxs,
                       wh=hsc_war2[0], ws2=stt2_war[slot], b1=C.val("ssc") + 1):
                e.wait_ge(sem[sxs], wx)
                e.wait_ge(sem["saux"], wa)
                if ws2 > 0:
                    e.wait_ge(sem["ssc"], ws2)
                if wh > 0:
                    e.wait_ge(sem["vev"], wh)

                def s_(i):
                    return stt2[:, po + i:po + i + 1]
                yv = ysb[:, slot, :]
                e.activation(sqs2[:], yv, AF.Square, accum_out=s_(2))
                e.activation(s_(3), s_(2), AF.Copy, scale=1.0 / D, bias=EPS_RMS)
                e.activation(s_(4), s_(3), AF.Ln)
                e.activation(s_(5), s_(4), AF.Exp, scale=-0.5).then_inc(sem["ssc"], 1)
                e.wait_ge(sem["ssc"], b1)
                e.mul(s_(6), s_(5), s_(5))
                e.mul(s_(6), s_(6), s_(3))
                e.activation(s_(6), s_(6), AF.Copy, scale=-0.5, bias=1.5)
                e.mul(s_(6), s_(6), s_(5)).then_inc(sem["ssc"], 1)
                e.wait_ge(sem["ssc"], b1 + 1)
                e.mul(sqs2[:], yv, s_(6)).then_inc(sem["ssc"], 1)
            ph.em("scalar", _chain)
            C.add("ssc", 3)
            stt2_war[slot] = C.val("ssc")
            ysb_free[slot] = C.val("ssc")

            def _h2(e, ws=C.val("ssc")):
                e.wait_ge(sem["ssc"], ws)
                e.tensor_tensor(h2b[:], sqs2[:], ln2r[:],
                                ALU.mult).then_inc(sem["vev"], 1)
            ph.em("vector", _h2)
            C.add("vev")
            hsc_war2[0] = C.val("vev")

            def xvw(e, wo=h2T_war[tslot], tslot=tslot):
                if wo > 0:
                    e.wait_ge(sem[f"so{tslot}"], wo)
            transposes(ph, pT2,
                       lambda j: h2b[:, j * 128:(j + 1) * 128],
                       lambda j, tslot=tslot: h2T[:, tslot, j, :],
                       DC, extra_vector_wait=xvw)

            def _sth(e, c=c, tslot=tslot, wv=C.val("vev")):
                e.wait_ge(sem["vev"], wv)
                e.dma_start(h2t_d[:, :, c * 128:(c + 1) * 128]
                            .rearrange("a p t -> p a t"),
                            h2T[:, tslot, :, :]).then_inc(sem[f"so{tslot}"], 16)
            ph.em("sync", _sth)
            C.add(f"so{tslot}", 16)
            h2T_war[tslot] = C.val(f"so{tslot}")

            def _rmm(e, tslot=tslot, wv=C.val("vev"), ws=RW_SWG, wsc=C.val("ssc")):
                e.wait_ge(sem["swg"], ws)
                e.wait_ge(sem["vev"], wv)
                e.wait_ge(sem["ssc"], wsc)
                for j in range(DC):
                    mm = e.matmul(pR[:], h2T[:, tslot, j, :], rwb[:, j, :],
                                  start=(j == 0), stop=(j == DC - 1))
                mm.then_inc(sem["tmm"], 1)
            ph.em("tensor", _rmm)
            C.add("tmm")

            def _sm(e, wt=C.val("tmm")):
                e.wait_ge(sem["tmm"], wt)
                e.reduce_max(stt2[:, 16:17], pR[:], axis=AX.X, negate=True
                             ).then_inc(sem["vev"], 1)
            ph.em("vector", _sm)
            C.add("vev")

            def _ex(e, wv=C.val("vev")):
                e.wait_ge(sem["vev"], wv)
                e.activation(prb[:], pR[:], AF.Exp, bias=stt2[:, 16:17],
                             accum_out=stt2[:, 17:18]).then_inc(sem["ssc"], 1)
            ph.em("scalar", _ex)
            C.add("ssc")

            def _top2(e, c=c, ws=C.val("ssc"), b=C.val("vev")):
                e.wait_ge(sem["ssc"], ws)

                def bar(ins, i):
                    ins.then_inc(sem["vev"], 1)
                    e.wait_ge(sem["vev"], b + i)
                bar(e.reciprocal(stt2[:, 18:19], stt2[:, 17:18]), 1)
                bar(e.tensor_scalar(prb[:], prb[:], stt2[:, 18:19], None, ALU.mult), 2)
                bar(e.reduce_max(stt2[:, 19:20], prb[:], axis=AX.X), 3)
                bar(e.tensor_scalar(wtb[:], prb[:], stt2[:, 19:20], None, ALU.is_lt), 4)
                bar(e.tensor_tensor(wtb[:], wtb[:], prb[:], ALU.mult), 5)
                bar(e.reduce_max(stt2[:, 20:21], wtb[:], axis=AX.X), 6)
                bar(e.tensor_scalar(wtb[:], prb[:], stt2[:, 20:21], None, ALU.is_ge), 7)
                bar(e.tensor_tensor(wtb[:], wtb[:], prb[:], ALU.mult), 8)
                bar(e.reduce_sum(stt2[:, 21:22], wtb[:], axis=AX.X), 9)
                bar(e.reciprocal(stt2[:, 22:23], stt2[:, 21:22]), 10)
                tq, tc = c // TQC, c % TQC
                e.tensor_scalar(
                    bass.AP(wfull, tq * (NE * TQC) + tc,
                            [[NTQ * NE * TQC, 128], [TQC, NE], [1, 1]]),
                    wtb[:], stt2[:, 22:23], None, ALU.mult).then_inc(sem["vev"], 1)
            ph.em("vector", _top2)
            C.add("vev", 11)
        H2T_DONE = (C.val("so0"), C.val("so1"))
        ph.play(nc)
        st.close()

        GUSL = [(0, 4), (4, 4), (8, 4), (12, 4), (16, 4), (20, 4), (24, 2), (26, 2)]
        NGU = len(GUSL)
        GR_SL = [n * 2 * 2 for _, n in GUSL]
        GR_PER_E = sum(GR_SL)
        GGR_PER_E = GR_PER_E // 2

        ph = Phase("F2")
        st = contextlib.ExitStack()
        h2Tq = st.enter_context(nc.sbuf_tensor("f2_h2Tq", [128, DC, TQT], BF16))
        aTb = st.enter_context(nc.sbuf_tensor("f2_aTb", [128, MC, TQT], BF16))
        yac = st.enter_context(nc.sbuf_tensor("f2_yac", [128, TQC, D], F32))
        wgub = st.enter_context(nc.sbuf_tensor("f2_wgub", [128, 2, 2, DC, 512], BF16))
        wdb = st.enter_context(nc.sbuf_tensor("f2_wdb", [128, 2, 4, D], BF16))
        sib = st.enter_context(nc.sbuf_tensor("f2_sib", [128, 4, 512], BF16))
        wsel = st.enter_context(nc.sbuf_tensor("f2_wsel", [128, 8], F32))
        xfb = st.enter_context(nc.sbuf_tensor("f2_xfb", [128, 2, D], F32))
        yfb = st.enter_context(nc.sbuf_tensor("f2_yfb", [128, 2, D], F32))
        ofb = st.enter_context(nc.sbuf_tensor("f2_ofb", [128, 2, D], F32))
        pF = st.enter_context(nc.psum_tensor("f2_pF", [128, 8, 512], F32))

        def _z0(e, tgt=C.val("vz") + 1):
            e.memset(yac[:], 0.0).then_inc(sem["vz"], 1)
            e.wait_ge(sem["vz"], tgt)
            e.sem_inc(sem["vmoe"], 4)
        ph.em("vector", _z0)
        C.add("vz")
        C.add("vmoe", 4)

        def _sb(e):
            e.sem_inc(sem["ssil"], 1)
        ph.em("scalar", _sb)
        C.add("ssil", 1)

        TMO_0 = C.val("tmoe")
        TD_0 = C.val("tdown")
        SIL_0 = C.val("ssil")
        VMO_0 = C.val("vmoe")
        VDE_0 = C.val("vdev")
        TSL_0 = C.val("tslc")
        SWG_0 = C.val("swg")

        H2Q_AUX0 = C.val("shq")

        def emit_ldh(tq):
            def _ldh(e, tq=tq, wo=H2T_DONE, wt=TMO_0 + tq * NE * GR_PER_E):
                e.wait_ge(sem["so0"], wo[0])
                e.wait_ge(sem["so1"], wo[1])
                if tq > 0:
                    e.wait_ge(sem["tmoe"], wt)
                e.dma_start(h2Tq[:],
                            h2t_d[:, :, tq * TQT:(tq + 1) * TQT]
                            .rearrange("a p t -> p a t")).then_inc(sem["shq"], 16)
            ph.em("sync", _ldh)
            C.add("shq", 16)

        for tq in range(NTQ):
            for e_i in range(NE):
                for si, (mc0, nmc) in enumerate(GUSL):
                    gsl = (tq * NE + e_i) * NGU + si

                    def _lw(e, e_i=e_i, mc0=mc0, nmc=nmc, gsl=gsl, si=si, tq=tq):
                        if si == 0 and e_i == 0:
                            e.wait_ge(sem["shq"], H2Q_AUX0 + 16 * (tq + 1))
                        if gsl >= 2:
                            done = sum(GR_SL[k % NGU] for k in range(gsl - 1))
                            e.wait_ge(sem["tmoe"], TMO_0 + done)
                        sg = f"sgu{si % 2}"
                        sl = gsl % 2
                        e.dma_start(wgub[:, sl, 0, :, 0:nmc * 128],
                                    wg_d[e_i, :, mc0 * 128:(mc0 + nmc) * 128]
                                    .rearrange("(c p) m -> p c m", p=128)
                                    ).then_inc(sem[sg], 16)
                        e.dma_start(wgub[:, sl, 1, :, 0:nmc * 128],
                                    wu_d[e_i, :, mc0 * 128:(mc0 + nmc) * 128]
                                    .rearrange("(c p) m -> p c m", p=128)
                                    ).then_inc(sem[sg], 16)
                    ph.em("gpsimd", _lw)
                    C.add(f"sgu{si % 2}", 32)
                for dg in range(2):
                    for si, (mc0, nmc) in enumerate(GUSL):
                        wsl = (tq * NE + e_i) * 16 + dg * NGU + si

                        def _lwd(e, e_i=e_i, mc0=mc0, nmc=nmc, wsl=wsl, si=si):
                            if wsl >= 2:
                                prev = wsl - 2
                                ns = sum(1 for k in range(prev + 1)
                                         if (k % NGU) != NGU - 1)
                                nf = (prev + 1) - ns
                                if ns > 0:
                                    e.wait_ge(sem["tslc"], TSL_0 + ns)
                                if nf > 0:
                                    e.wait_ge(sem["tdown"], TD_0 + nf * 8)
                            e.dma_start(wdb[:, wsl % 2, 0:nmc, :],
                                        wd_d[e_i, mc0 * 128:(mc0 + nmc) * 128, :]
                                        .rearrange("(c p) d -> p c d", p=128)
                                        ).then_inc(sem[f"swd{si % 2}"], 16)
                        ph.em("gpsimd", _lwd)
                        C.add(f"swd{si % 2}", 16)

        NFLAT = NTQ * NE

        def _ten(e):
            with e.register("f2rg0") as rg0, e.register("f2rg1") as rg1, \
                 e.register("f2rs") as rs, \
                 e.register("f2rv") as rv, e.register("f2rvd") as rvd, \
                 e.register("f2rw0") as rwd0, e.register("f2rw1") as rwd1, \
                 e.register("f2rvm") as rvm:
                e.reg_mov(rg0, 32)
                e.reg_mov(rg1, 32)
                e.reg_mov(rs, SIL_0 - 1)
                e.reg_mov(rv, VMO_0 - 1)
                e.reg_mov(rvd, VDE_0)
                e.reg_mov(rwd0, 16)
                e.reg_mov(rwd1, 16)
                e.reg_mov(rvm, VMO_0 + GGR_PER_E)
                with e.Fori(0, NFLAT):
                    e.wait_ge(sem["vdev"], rvd)
                    gidx = 0
                    for si, (mc0, nmc) in enumerate(GUSL):
                        rg = rg0 if si % 2 == 0 else rg1
                        e.wait_ge(sem[f"sgu{si % 2}"], rg)
                        e.reg_add(rg, rg, 32)
                        for ml in range(nmc):
                            for tg in range(2):
                                for gu in range(2):
                                    bank = gu * 2 + (gidx % 2)
                                    if gu == 0:
                                        e.wait_ge(sem["ssil"], rs)
                                        e.reg_add(rs, rs, 1)
                                    else:
                                        e.wait_ge(sem["vmoe"], rv)
                                        e.reg_add(rv, rv, 1)
                                    for j in range(DC):
                                        mm = e.matmul(
                                            pF[:, bank, :],
                                            wgub[:, si % 2, gu, j,
                                                 ml * 128:(ml + 1) * 128],
                                            h2Tq[:, j, tg * 512:(tg + 1) * 512],
                                            start=(j == 0), stop=(j == DC - 1),
                                            skip_group_check=True)
                                    mm.then_inc(sem["tmoe"], 1)
                                gidx += 1
                    e.wait_ge(sem["vmoe"], rvm)
                    e.reg_add(rvm, rvm, GGR_PER_E)
                    for dg in range(2):
                        if dg == 1:
                            e.wait_ge(sem["vdev"], rvd)
                        for si, (mc0, nmc) in enumerate(GUSL):
                            rwd = rwd0 if si % 2 == 0 else rwd1
                            e.wait_ge(sem[f"swd{si % 2}"], rwd)
                            e.reg_add(rwd, rwd, 16)
                            for ml in range(nmc):
                                mc = mc0 + ml
                                for tc in range(TQC):
                                    mm = e.matmul(
                                        pF[:, tc, :],
                                        aTb[:, mc, tc * 128:(tc + 1) * 128],
                                        wdb[:, (dg * NGU + si) % 2, ml,
                                            dg * 512:(dg + 1) * 512],
                                        start=(mc == 0), stop=(mc == MC - 1),
                                        skip_group_check=True)
                                    if mc == MC - 1:
                                        mm.then_inc(sem["tdown"], 1)
                                    elif ml == nmc - 1 and tc == TQC - 1:
                                        mm.then_inc(sem["tslc"], 1)
                        if dg == 0:
                            e.reg_add(rvd, rvd, 8)
                    e.reg_add(rvd, rvd, 8)
        ph.em("tensor", _ten)
        C.add("tmoe", NFLAT * GR_PER_E)
        C.add("tdown", NFLAT * 16)
        C.add("tslc", NFLAT * 14)

        def _sca(e):
            with e.register("f2srt") as rt, e.register("f2srv") as rv2:
                e.reg_mov(rt, TMO_0 + 1)
                e.reg_mov(rv2, VMO_0 - 3)
                with e.Fori(0, NFLAT):
                    g = 0
                    for si, (mc0, nmc) in enumerate(GUSL):
                        for ml in range(nmc):
                            for tg in range(2):
                                bank = g % 2
                                e.wait_ge(sem["tmoe"], rt)
                                e.reg_add(rt, rt, 2)
                                e.wait_ge(sem["vmoe"], rv2)
                                e.reg_add(rv2, rv2, 1)
                                e.activation(sib[:, g % 4, :], pF[:, bank, :],
                                             AF.Silu).then_inc(sem["ssil"], 1)
                                g += 1
        ph.em("scalar", _sca)
        C.add("ssil", NFLAT * GGR_PER_E)

        ofb_war = [0, 0]
        for tq in range(NTQ):
            emit_ldh(tq)
            SIL_T = SIL_0 + tq * NE * GGR_PER_E
            TMO_T = TMO_0 + tq * NE * GR_PER_E
            TD_T = TD_0 + tq * NE * 16

            def _vec(e, tq=tq, SIL_T=SIL_T, TMO_T=TMO_T, TD_T=TD_T):
                with e.register(f"f2vs{tq}") as rsl, e.register(f"f2vt{tq}") as rt, \
                     e.register(f"f2vd{tq}") as rtd, e.register(f"f2vw{tq}") as rwo:
                    e.reg_mov(rsl, SIL_T + 1)
                    e.reg_mov(rt, TMO_T + 2)
                    e.reg_mov(rtd, TD_T + 1)
                    e.reg_mov(rwo, tq * (NE * TQC))
                    with e.Fori(0, NE):
                        e.tensor_copy(wsel[:, 0:TQC],
                                      bass.AP(wfull, rwo,
                                              [[NTQ * NE * TQC, 128], [1, TQC]]))
                        g = 0
                        for si, (mc0, nmc) in enumerate(GUSL):
                            for ml in range(nmc):
                                mc = mc0 + ml
                                for tg in range(2):
                                    bank = g % 2
                                    e.wait_ge(sem["ssil"], rsl)
                                    e.reg_add(rsl, rsl, 1)
                                    e.wait_ge(sem["tmoe"], rt)
                                    e.reg_add(rt, rt, 2)
                                    e.tensor_tensor(
                                        aTb[:, mc, tg * 512:(tg + 1) * 512],
                                        sib[:, g % 4, :], pF[:, 2 + bank, :],
                                        ALU.mult).then_inc(sem["vmoe"], 1)
                                    g += 1
                        for dg in range(2):
                            for tc in range(TQC):
                                e.wait_ge(sem["tdown"], rtd)
                                e.reg_add(rtd, rtd, 1)
                                e.scalar_tensor_tensor(
                                    yac[:, tc, dg * 512:(dg + 1) * 512],
                                    pF[:, tc, :],
                                    wsel[:, tc:tc + 1],
                                    yac[:, tc, dg * 512:(dg + 1) * 512],
                                    ALU.mult, ALU.add).then_inc(sem["vdev"], 1)
                        e.reg_add(rwo, rwo, TQC)
            ph.em("vector", _vec)
            C.add("vmoe", NE * GGR_PER_E)
            C.add("vdev", NE * 16)

            for tc in range(TQC):
                c = tq * TQC + tc
                cs = tc % 2

                sxs = f"sx{cs}"

                def _ldf(e, c=c, cs=cs, wo=ofb_war[cs], sxs=sxs):
                    if wo > 0:
                        e.wait_ge(sem[f"so{cs}"], wo)
                    e.dma_start(xfb[:, cs, :], x1_d[c * 128:(c + 1) * 128, :]
                                ).then_inc(sem[sxs], 16)
                    e.dma_start(yfb[:, cs, :], y1_d[c * 128:(c + 1) * 128, :]
                                ).then_inc(sem[sxs], 16)
                ph.em("sync", _ldf)
                C.add(sxs, 32)

                def _fin(e, tc=tc, cs=cs, wx=C.val(sxs), zt=C.val("vz") + 1,
                         sxs=sxs, wd=C.val("vdev"), bar=C.val("vev") + 1):
                    e.wait_ge(sem[sxs], wx)
                    e.wait_ge(sem["vdev"], wd)
                    e.tensor_tensor(ofb[:, cs, :], yfb[:, cs, :], yac[:, tc, :],
                                    ALU.add).then_inc(sem["vev"], 1)
                    e.wait_ge(sem["vev"], bar)
                    e.tensor_tensor(ofb[:, cs, :], ofb[:, cs, :], xfb[:, cs, :],
                                    ALU.add).then_inc(sem["vev"], 1)
                    e.memset(yac[:, tc, :], 0.0).then_inc(sem["vz"], 1)
                    e.wait_ge(sem["vz"], zt)
                ph.em("vector", _fin)
                C.add("vz")
                C.add("vev", 2)

                def _sto(e, c=c, cs=cs, wv=C.val("vev")):
                    e.wait_ge(sem["vev"], wv)
                    e.dma_start(out_d[c * 128:(c + 1) * 128, :],
                                ofb[:, cs, :]).then_inc(sem[f"so{cs}"], 16)
                ph.em("sync", _sto)
                C.add(f"so{cs}", 16)
                ofb_war[cs] = C.val(f"so{cs}")

        def _done(e, w0=C.val("so0"), w1=C.val("so1"), w2=C.val("sout")):
            e.wait_ge(sem["so0"], w0)
            e.wait_ge(sem["so1"], w1)
            e.wait_ge(sem["sout"], w2)
        ph.em("vector", _done)
        ph.play(nc)
        st.close()

    n_instr = sum(1 for _ in nc.all_instructions())
    print(f"[kernel] built S={S}: {n_instr} instructions")
    return nc


def _consts(S):
    identb = np.eye(128, dtype=ml_dtypes.bfloat16)
    identf = np.eye(128, dtype=np.float32)
    onesb = np.ones((128, 128), dtype=ml_dtypes.bfloat16)
    inv_freq = (1.0 / (10000.0 ** (np.arange(0, HD, 2, dtype=np.float64) / HD)))
    t = np.arange(S, dtype=np.float64)
    ang = np.outer(t, inv_freq)
    cos = np.cos(ang).astype(np.float32)
    sin = np.sin(ang).astype(np.float32)
    cs = np.stack([cos, sin])[:, :, None, :]
    cs = np.ascontiguousarray(np.broadcast_to(cs, (2, S, NH, 32)).astype(np.float32))
    kk = np.arange(512)[:, None]
    qq = np.arange(512)[None, :]
    mask = np.ascontiguousarray(
        (kk <= qq).reshape(4, 128, 512).astype(ml_dtypes.bfloat16))
    return identb, identf, onesb, cs, mask


def make_in_maps(inputs, S=4096, n=8):
    identb, identf, onesb, cs, mask = _consts(S)
    f32 = lambda a: np.ascontiguousarray(np.asarray(a, dtype=np.float32))
    rep = lambda a: np.ascontiguousarray(
        np.broadcast_to(np.asarray(a, np.float32).reshape(1, D), (128, D)))
    xs = f32(inputs["L0_attn_x"])
    common = {
        "wq": f32(inputs["L0_attn_Wq"]), "wk": f32(inputs["L0_attn_Wk"]),
        "wv": f32(inputs["L0_attn_Wv"]), "wo": f32(inputs["L0_attn_Wo"]),
        "ln1": rep(inputs["L0_ffn_ln1_w"]), "ln2": rep(inputs["L0_ffn_ln2_w"]),
        "fwq": np.ascontiguousarray(
            f32(inputs["L0_ffn_wq"]).reshape(D, 16, 64)[:, PERM_Q, :].reshape(D, D)),
        "fwk": f32(inputs["L0_ffn_wk"]),
        "fwv": f32(inputs["L0_ffn_wv"]), "fwo": f32(inputs["L0_ffn_wo"]),
        "rw": f32(inputs["L0_ffn_router_w"]),
        "wgate": f32(inputs["L0_ffn_w_gate"]), "wup": f32(inputs["L0_ffn_w_up"]),
        "wdown": f32(inputs["L0_ffn_w_down"]),
        "identb": identb, "identf": identf, "onesb": onesb,
        "cossin": cs, "mask01": mask,
    }
    return [dict(common, x=xs[c][:S]) for c in range(n)]


def kernel(**inputs):
    S = 4096
    n = 8
    nc = build(S=S)
    in_maps = make_in_maps(inputs, S=S, n=n)
    res = run_bass_kernel_spmd(nc, in_maps, core_ids=list(range(n)))
    return np.stack([res.results[c]["out"] for c in range(n)], axis=0)
